# version 15
# baseline (speedup 1.0000x reference)
import contextlib
import numpy as np
import ml_dtypes
import concourse.bass as bass
import concourse.mybir as mybir
from concourse.bass_core_rust import IndirectOffsetOnAxis
from concourse.bass_utils import run_bass_kernel_spmd

F32 = mybir.dt.float32
BF16 = mybir.dt.bfloat16
I32 = mybir.dt.int32
AF = mybir.ActivationFunctionType
ALU = mybir.AluOpType
AX = mybir.AxisListType

NB = 2
S = 2048
D = 2048
H = 16
HD = 128
ND = 64
RD = 64
R = 512
I8K = 8192
SCALE = float(1.0 / np.sqrt(np.float32(128.0)).astype(np.float32))
LN_EPS = 1e-5
NEG_BIG = -30000.0
MAGIC = 0x5f3759df

ENGS = ("tensor", "vector", "scalar", "gpsimd", "sync")


class Tok:
    __slots__ = ("sem", "val")
    def __init__(self, sem, val):
        self.sem = sem
        self.val = val


class Plan:
    def __init__(self, nc):
        self.nc = nc
        self.seq = {e: [] for e in ENGS}
        self.sems = {}
        self.led = {}
        self.lastwait = {e: {} for e in ENGS}

    def sem(self, name):
        if name not in self.sems:
            self.sems[name] = self.nc.alloc_semaphore(name)
            self.led[name] = 0
        return self.sems[name]

    def _waits(self, eng_name, waits):
        out = []
        lw = self.lastwait[eng_name]
        for t in waits:
            if t is None:
                continue
            if lw.get(t.sem, 0) >= t.val:
                continue
            lw[t.sem] = t.val
            out.append((self.sems[t.sem], t.val))
        return out

    def op(self, eng_name, meth, *args, waits=(), inc=None, inc_k=1, **kwargs):
        w = self._waits(eng_name, waits)
        tok = None
        inch = None
        if inc is not None:
            self.sem(inc)
            self.led[inc] += inc_k
            tok = Tok(inc, self.led[inc])
            inch = self.sems[inc]
        def emit(eng, w=w, meth=meth, args=args, kwargs=kwargs, tok=tok, inch=inch, inc_k=inc_k):
            for (h, v) in w:
                eng.wait_ge(h, v)
            inst = getattr(eng, meth)(*args, **kwargs)
            if tok is not None:
                inst.then_inc(inch, inc_k)
            return inst
        self.seq[eng_name].append(emit)
        return tok

    def waitonly(self, eng_name, waits):
        w = self._waits(eng_name, waits)
        if not w:
            return
        def emit(eng, w=w):
            for (h, v) in w:
                eng.wait_ge(h, v)
        self.seq[eng_name].append(emit)

    POOLQ = {"gpsimd": "g", "sync": "y", "scalar": "c", "vector": "w", "tensor": "t"}

    def dma(self, eng_name, out, in_, waits=(), inc=None, pool=None, npool=8, **kwargs):
        pool = pool or self.POOLQ[eng_name]
        if not hasattr(self, "pool_idx"):
            self.pool_idx = {}
        i = self.pool_idx.get(pool, -1)
        i = (i + 1) % npool
        self.pool_idx[pool] = i
        name = f"{pool}dma{i}"
        self.sem(name)
        w = list(waits)
        if self.led[name] > 0:
            w.append(Tok(name, self.led[name]))
        ind = kwargs.pop("indirect_in", None)
        if ind is not None:
            ax = kwargs.pop("indirect_axis", 0)
            return self.op(eng_name, "indirect_dma_start", out, None, in_,
                           IndirectOffsetOnAxis(ind, ax),
                           waits=w, inc=name, inc_k=16, **kwargs)
        return self.op(eng_name, "dma_start", waits=w, inc=name, inc_k=16,
                       out=out, in_=in_, **kwargs)

    def bank_use(self, key):
        if not hasattr(self, "bankfree"):
            self.bankfree = {}
        return self.bankfree.pop(key, [])

    def bank_done(self, key, tok):
        if not hasattr(self, "bankfree"):
            self.bankfree = {}
        if tok is not None:
            self.bankfree.setdefault(key, []).append(tok)

    def fence(self):
        toks = [Tok(n, self.led[n]) for n in self.led]
        for e in ENGS:
            self.waitonly(e, toks)

    def soft_fence(self, extra=()):
        toks = [Tok(n, self.led[n]) for n in ("t_e", "v_e", "s_e") if n in self.led]
        toks += [t for t in extra if t is not None]
        for e in ("vector", "scalar", "gpsimd", "sync"):
            self.waitonly(e, toks)

    def all_toks(self):
        return [Tok(n, self.led[n]) for n in self.led]


class Ring:
    def __init__(self, n):
        self.n = n
        self.free = [[] for _ in range(n)]
        self.i = 0

    def acquire(self):
        i = self.i % self.n
        self.i += 1
        toks = self.free[i]
        self.free[i] = []
        return i, toks

    def release(self, i, *toks):
        self.free[i].extend(t for t in toks if t is not None)



_NAME_I = [0]
def sbt(nc, name, shape, dt):
    _NAME_I[0] += 1
    return nc.sbuf_tensor(f"{name}{_NAME_I[0]}", shape, dt)

def bfc(a):
    return np.asarray(a, dtype=np.float32).astype(ml_dtypes.bfloat16)



def emit_P0b(p, C, b, qload="gpsimd"):
    x_in, xT, xN, nc = C["x_in"], C["xT"], C["xN"], C["nc"]
    C.setdefault("x_done", {})[b] = []
    with sbt(nc, "p0buf", [128, 4 * D], BF16) as buf:
        ring = Ring(2)
        xn_toks = []
        for tt in range(S // 128):
            i, fr = ring.acquire()
            sl = buf[:, i * D:(i + 1) * D]
            ld = p.dma(qload, sl, x_in[b, tt * 128:(tt + 1) * 128, :], waits=fr)
            st = p.dma("sync", xN[b, tt * 128:(tt + 1) * 128, :], sl, waits=[ld])
            ring.release(i, st)
            xn_toks.append(st)
        ring2 = Ring(2)
        for dc in range(D // 128):
            i, fr = ring2.acquire()
            sl = buf[:, (2 + i) * D:(2 + i) * D + S]
            tp = p.dma("scalar", sl, xN[b, :, dc * 128:(dc + 1) * 128],
                       transpose=True, waits=fr + (xn_toks if dc == 0 else []))
            st = p.dma("sync", xT[b, dc * 128:(dc + 1) * 128, :], sl, waits=[tp])
            ring2.release(i, st)
            C["x_done"][b].append(st)
        for e in ENGS:
            p.waitonly(e, C["x_done"][b])


def rope_rows(p, C, buf_slice, psum_small, tmp, gt, base=0, bank="s", src_tok=None):
    cos_sb, sin_sb, rot_sb = C["cos_sb"], C["sin_sb"], C["rot_sb"]
    lo, hi = base, base + 64
    cs = cos_sb[lo:hi, gt * 512:(gt + 1) * 512]
    sn = sin_sb[lo:hi, gt * 512:(gt + 1) * 512]
    st = [src_tok] if src_tok else [Tok("v_e", p.led.get("v_e", 0)),
                                    Tok("s_e", p.led.get("s_e", 0))]
    mm = p.op("tensor", "matmul", psum_small, rot_sb[lo:hi, :], buf_slice,
              start=True, stop=True,
              waits=st + p.bank_use(bank), inc="t_e")
    t1 = p.op("vector", "tensor_mul", tmp[lo:hi, :], buf_slice, cs,
              waits=st + p.bank_use("ropetmp"), inc="v_e")
    t2 = p.op("vector", "tensor_mul", psum_small, psum_small, sn,
              waits=[mm], inc="v_e")
    t3 = p.op("vector", "tensor_add", buf_slice, tmp[lo:hi, :], psum_small,
              waits=[t1, t2], inc="v_e")
    p.bank_done(bank, t3)
    p.bank_done("ropetmp", t3)
    return t3


def emit_P1(p, C, b, stack):
    nc = C["nc"]
    xT = C["xT"]
    wkvT, wkuT, wvuT, wkrT = C["wkvT"], C["wkuT"], C["wvuT"], C["wkrT"]
    KAll, Vst = C["KAll"], C["Vst"]
    ps_a, ps_b, ps_c, ps_d, ps_s = (C["ps_a"], C["ps_b"], C["ps_c"], C["ps_d"],
                                    C["ps_s"])

    with contextlib.ExitStack() as sco:
        ckvT = sco.enter_context(sbt(nc, "ckvT", [128, 4 * S], BF16))
        krT = sco.enter_context(sbt(nc, "krT", [64, S], BF16))
        C["dbg_ckvT"], C["dbg_krT"] = ckvT, krT

        with contextlib.ExitStack() as stA:
            wkv_sb = stA.enter_context(sbt(nc, "wkv_sb", [128, 2 * 2048], BF16))
            wkr_sb = stA.enter_context(sbt(nc, "wkr_sb", [128, 16 * RD], BF16))
            xg_sb = stA.enter_context(sbt(nc, "xg_sb", [128, 16 * 1024], BF16))
            t64 = stA.enter_context(sbt(nc, "t64", [128, 512], BF16))

            wkr_t = p.dma("gpsimd", wkr_sb[:].rearrange("p (c m) -> p c m", c=16),
                          wkrT.rearrange("(c p) m -> p c m", p=128))
            wkvring = Ring(2)

            xring = Ring(1)
            for gp in range(2):
                xi, xfr = xring.acquire()
                xw = p.dma("gpsimd", xg_sb[:].rearrange("p (c m) -> p c m", c=16),
                           xT[b, :, gp * 1024:(gp + 1) * 1024].rearrange("(c p) m -> p c m", p=128),
                           waits=xfr + C["x_done"][b])
                consumed = []
                for oc in range(4):
                    wvi, wvfr = wkvring.acquire()
                    wvoff = wvi * 2048
                    wkv_t = p.dma("gpsimd",
                                  wkv_sb[:, wvoff:wvoff + 2048].rearrange("p (c m) -> p c m", c=16),
                                  wkvT[:, oc * 128:(oc + 1) * 128].rearrange("(c p) m -> p c m", p=128),
                                  waits=wvfr)
                    keys = ("a", "b") if oc % 2 == 0 else ("c", "d")
                    pss = (ps_a, ps_b) if oc % 2 == 0 else (ps_c, ps_d)
                    mmx = [None, None]
                    for ic in range(16):
                        for gh in range(2):
                            mmx[gh] = p.op("tensor", "matmul", pss[gh][:, 0:512],
                                      wkv_sb[:, wvoff + ic * 128: wvoff + ic * 128 + 128],
                                      xg_sb[:, ic * 1024 + gh * 512: ic * 1024 + gh * 512 + 512],
                                      start=(ic == 0), stop=(ic == 15),
                                      waits=([wkv_t, xw] + p.bank_use(keys[gh])
                                             if ic == 0 else []),
                                      inc=("t_e" if ic == 15 else None))
                    for gh in range(2):
                        gt = gp * 2 + gh
                        ev = p.op("vector", "tensor_copy",
                                  ckvT[:, oc * S + gt * 512: oc * S + gt * 512 + 512],
                                  pss[gh][:, 0:512], waits=[mmx[gh]], inc="v_e")
                        p.bank_done(keys[gh], ev)
                        consumed.append(ev)
                    wkvring.release(wvi, *[mmx[0], mmx[1]])
                for gh in range(2):
                    gt = gp * 2 + gh
                    mm = None
                    for ic in range(16):
                        mm = p.op("tensor", "matmul", ps_s[0:64, 0:512],
                                  wkr_sb[:, ic * RD:(ic + 1) * RD],
                                  xg_sb[:, ic * 1024 + gh * 512: ic * 1024 + gh * 512 + 512],
                                  start=(ic == 0), stop=(ic == 15),
                                  waits=([wkr_t] + p.bank_use("s") if ic == 0 else []),
                                  inc=("t_e" if ic == 15 else None))
                    ev = p.op("vector", "tensor_copy", krT[:, gt * 512:(gt + 1) * 512],
                              ps_s[0:64, 0:512], waits=[mm], inc="v_e")
                    p.bank_done("s", ev)
                    rp = rope_rows(p, C, krT[:, gt * 512:(gt + 1) * 512],
                                   ps_s[0:64, 0:512], t64, gt, base=0, bank="s",
                                   src_tok=ev)
                    consumed.append(rp)
                xring.release(xi, *consumed)
            p.fence()

        with contextlib.ExitStack() as stB:
            wku_sb = stB.enter_context(sbt(nc, "wku_sb", [128, 4 * 1024], BF16))
            wku_t = p.dma("gpsimd", wku_sb[:].rearrange("p (c m) -> p c m", c=4),
                          wkuT.rearrange("(c p) m -> p c m", p=128))
            kbanks = (ps_a, ps_b, ps_c, ps_d)
            for hp in range(8):
                mmx = [None] * 4
                for rc in range(4):
                    for gt in range(4):
                        mmx[gt] = p.op("tensor", "matmul", kbanks[gt][:, 0:512],
                                  wku_sb[:, rc * 1024 + hp * 128: rc * 1024 + hp * 128 + 128],
                                  ckvT[:, rc * S + gt * 512: rc * S + gt * 512 + 512],
                                  start=(rc == 0), stop=(rc == 3),
                                  waits=([wku_t] + p.bank_use("abcd"[gt])
                                         if rc == 0 else []),
                                  inc=("t_e" if rc == 3 else None))
                for gt in range(4):
                    p.op("vector", "tensor_copy",
                         KAll[0:64, (2 * hp) * S + gt * 512:(2 * hp) * S + gt * 512 + 512],
                         kbanks[gt][0:64, 0:512], waits=[mmx[gt]], inc="v_e")
                    ev = p.op("vector", "tensor_copy",
                         KAll[0:64, (2 * hp + 1) * S + gt * 512:(2 * hp + 1) * S + gt * 512 + 512],
                         kbanks[gt][64:128, 0:512], waits=[mmx[gt]], inc="v_e")
                    p.bank_done("abcd"[gt], ev)
            rw = Tok("v_e", p.led["v_e"])
            for h in range(H):
                p.op("vector", "tensor_copy", KAll[64:128, h * S:(h + 1) * S],
                     krT[:, :], waits=[rw], inc="v_e")
            p.fence()

        with contextlib.ExitStack() as stC:
            wvu_sb = stC.enter_context(sbt(nc, "wvu_sb", [128, 4 * D], BF16))
            wvu_t = p.dma("gpsimd", wvu_sb[:].rearrange("p (c m) -> p c m", c=4),
                          wvuT.rearrange("(c p) m -> p c m", p=128))
            ones_v = Vst[:].rearrange("p (a u) -> p a u", u=130)[:, :, 128:130]
            p.op("vector", "memset", ones_v, 1.0, inc="v_e")
            vbanks = (ps_a, ps_b, ps_c, ps_d)
            for tt in range(16):
                mmx = [None] * 4
                for rc in range(4):
                    for dc4 in range(4):
                        mmx[dc4] = p.op("tensor", "matmul", vbanks[dc4][:, 0:512],
                                  ckvT[:, rc * S + tt * 128: rc * S + tt * 128 + 128],
                                  wvu_sb[:, rc * D + dc4 * 512: rc * D + dc4 * 512 + 512],
                                  start=(rc == 0), stop=(rc == 3),
                                  waits=([wvu_t] + p.bank_use("abcd"[dc4])
                                         if rc == 0 else []),
                                  inc=("t_e" if rc == 3 else None))
                for dc4 in range(4):
                    dst = Vst[:, tt * 2080 + dc4 * 4 * 130: tt * 2080 + (dc4 * 4 + 4) * 130]
                    ev = p.op("vector", "tensor_copy",
                         dst.rearrange("p (h m) -> p h m", h=4)[:, :, 0:128],
                         vbanks[dc4][:, 0:512].rearrange("p (h m) -> p h m", h=4),
                         waits=[mmx[dc4]], inc="v_e")
                    p.bank_done("abcd"[dc4], ev)
            p.fence()
        if "p1" in C["debug"]:
            dbg_dump(p, C, "dbg_ckvT_o", ckvT[:, :], 128, 4 * S)
            dbg_dump(p, C, "dbg_krT_o", krT[:, :], 64, S)
    p.fence()




def part_max128(p, C, out11, in_col, nparts=128):
    pm, pmt = C["pm32"], C["pmt32"]
    ps = C["ps_s"]
    t0 = p.op("vector", "memset", pm[:, :], -3e38, inc="v_e")
    t1 = p.op("vector", "tensor_copy", pm[0:nparts, 0:1], in_col, waits=[t0], inc="v_e")
    t2 = p.op("vector", "transpose", pmt[:, :], pm[:, :], waits=[t1], inc="v_e")
    c1 = p.op("vector", "tensor_copy", ps[0:64, 256:288], pmt[64:128, 0:32],
              waits=[t2], inc="v_e")
    t3 = p.op("vector", "tensor_max", pmt[0:64, 0:32], pmt[0:64, 0:32], ps[0:64, 256:288],
              waits=[c1], inc="v_e")
    c2 = p.op("vector", "tensor_copy", ps[0:32, 256:288], pmt[32:64, 0:32],
              waits=[t3], inc="v_e")
    t4 = p.op("vector", "tensor_max", pmt[0:32, 0:32], pmt[0:32, 0:32], ps[0:32, 256:288],
              waits=[c2], inc="v_e")
    t5 = p.op("vector", "tensor_reduce", out11, pmt[0:1, 0:32], AX.X, ALU.max,
              waits=[t4], inc="v_e")
    return t5


def newton_rsqrt(p, C, dst, src, w, nparts=128, waits=()):
    nr1, nr2, magic = C["nr1"], C["nr2"], C["magic_sb"]
    y = dst
    t = nr1[0:nparts, 0:w]
    n2 = nr2[0:nparts, 0:w]
    tv = p.op("vector", "tensor_scalar", n2.bitcast(I32), src.bitcast(I32),
              1, None, ALU.arith_shift_right, waits=list(waits), inc="v_e")
    tv = p.op("vector", "tensor_tensor", y.bitcast(I32), magic[0:nparts, 0:w].bitcast(I32),
              n2.bitcast(I32), ALU.subtract, waits=[tv], inc="v_e")
    for _ in range(3):
        tv = p.op("vector", "tensor_mul", t, y, y, waits=[tv], inc="v_e")
        tv = p.op("vector", "tensor_mul", t, t, src, waits=[tv], inc="v_e")
        tv = p.op("vector", "tensor_scalar", t, t, -0.5, 1.5,
                  ALU.mult, ALU.add, waits=[tv], inc="v_e")
        tv = p.op("vector", "tensor_mul", y, y, t, waits=[tv], inc="v_e")
    return tv


def emit_ln(p, C, resf, xnew_slice, out_f32=None, waits=(), sqscratch=None):
    stat = C["stat"]
    if sqscratch is None:
        sqscratch = xnew_slice
    s1, s2, mu, rst = (stat[:, 0:1], stat[:, 1:2], stat[:, 2:3], stat[:, 3:4])
    t1 = p.op("vector", "tensor_reduce", s1, resf, AX.X, ALU.add, waits=list(waits), inc="v_e")
    t2 = p.op("scalar", "activation", sqscratch, resf, AF.Square,
              accum_out=s2, waits=list(waits), inc="s_e")
    tm = p.op("vector", "tensor_scalar", mu, s1, 1.0 / D, None, ALU.mult,
              waits=[t1], inc="v_e")
    tv = p.op("vector", "tensor_mul", stat[:, 4:5], mu, mu, waits=[tm], inc="v_e")
    tv = p.op("vector", "tensor_scalar", stat[:, 4:5], stat[:, 4:5], LN_EPS, None,
              ALU.subtract, waits=[tv], inc="v_e")
    tv = p.op("vector", "scalar_tensor_tensor", stat[:, 5:6], s2, 1.0 / D,
              stat[:, 4:5], ALU.mult, ALU.subtract, waits=[tv, t2], inc="v_e")
    tv = newton_rsqrt(p, C, rst, stat[:, 5:6], 1, waits=[tv])
    toks = []
    ta = p.op("vector", "tensor_scalar", xnew_slice, resf, mu, rst,
              ALU.subtract, ALU.mult, waits=[tv], inc="v_e")
    toks.append(ta)
    if out_f32 is not None:
        tb = p.op("vector", "tensor_scalar", out_f32, resf, mu, rst,
                  ALU.subtract, ALU.mult, waits=[tv], inc="v_e")
        toks.append(tb)
    return toks


def emit_xN_to_xT(p, C, b):
    nc, xN, xT = C["nc"], C["xN"], C["xT"]
    with sbt(nc, "txbuf", [128, 4 * S], BF16) as buf:
        ring = Ring(4)
        for dc in range(D // 128):
            i, fr = ring.acquire()
            sl = buf[:, i * S:(i + 1) * S]
            tp = p.dma("scalar", sl, xN[b, :, dc * 128:(dc + 1) * 128],
                       transpose=True, waits=fr)
            st = p.dma("sync", xT[b, dc * 128:(dc + 1) * 128, :], sl, waits=[tp])
            ring.release(i, st)
    p.fence()


def emit_P2(p, C, b):
    nc = C["nc"]
    xT, xN, x_in = C["xT"], C["xN"], C["x_in"]
    wqT, woT = C["wqT"], C["woT"]
    KAll, Vst = C["KAll"], C["Vst"]
    ident_sb, ones_sb = C["ident_sb"], C["ones_sb"]
    ps_a, ps_b, ps_c, ps_d = C["ps_a"], C["ps_b"], C["ps_c"], C["ps_d"]
    ps_o, ps_s, ps_t, ps_t2 = C["ps_o"], C["ps_s"], C["ps_t"], C["ps_t2"]

    with contextlib.ExitStack() as sco:
        mask_sb = sco.enter_context(sbt(nc, "mask_sb", [128, 4 * 512], BF16))
        sm = sco.enter_context(sbt(nc, "sm", [128, 16], F32))
        smb = sco.enter_context(sbt(nc, "smb", [128, 8], BF16))
        bnd = sco.enter_context(sbt(nc, "bnd", [128, 1], F32))
        p.dma("sync", mask_sb[:].rearrange("p (c m) -> p c m", c=4),
              C["mask_c"].ap().rearrange("(c p) m -> p c m", p=128))

        mKn, mKr = sm[0:1, 8:9], sm[0:1, 9:10]
        sqstack = contextlib.ExitStack()
        sqscr = sqstack.enter_context(sbt(nc, "sqscr", [64, S], BF16))
        for (dst, rlo, rhi, hs) in ((mKn, 0, 64, range(H)), (mKr, 64, 128, range(1))):
            t0 = p.op("vector", "memset", dst, 0.0, inc="v_e")
            mx_prev = None
            for h in hs:
                sq = p.op("scalar", "activation", sqscr[:, :],
                          KAll[rlo:rhi, h * S:(h + 1) * S], AF.Square,
                          waits=[Tok("v_e", p.led["v_e"]), Tok("t_e", p.led.get("t_e", 0))],
                          inc="s_e")
                for c4 in range(4):
                    mm = p.op("tensor", "matmul", ps_s[0:1, 0:512],
                              ones_sb[0:64, 0:1], sqscr[:, c4 * 512:(c4 + 1) * 512],
                              start=True, stop=True,
                              waits=[sq] + p.bank_use("s"), inc="t_e")
                    rd = p.op("vector", "tensor_reduce", sm[0:1, 10:11],
                              ps_s[0:1, 0:512], AX.X, ALU.max,
                              waits=[mm] + ([mx_prev] if mx_prev else []), inc="v_e")
                    p.bank_done("s", rd)
                    mx_prev = p.op("vector", "tensor_max", dst, dst, sm[0:1, 10:11],
                         waits=[rd], inc="v_e")

        p.fence()
        sqstack.close()

        for g in range(4):
            nkc = 4 * g + 4
            with contextlib.ExitStack() as sgrp:
                osb = sgrp.enter_context(sbt(nc, "osb", [128, 4 * D], BF16))
                sqq = contextlib.ExitStack()
                qg = sqq.enter_context(sbt(nc, "qg", [128, 16 * 512], BF16))
                with contextlib.ExitStack() as spr:
                    xg_sb = spr.enter_context(sbt(nc, "xg2", [128, 16 * 512], BF16))
                    wqb = spr.enter_context(sbt(nc, "wqb", [128, 2 * 2048], BF16))
                    t64b = spr.enter_context(sbt(nc, "t64b", [128, 512], BF16))
                    sqch = spr.enter_context(sbt(nc, "sqch", [128, 512], BF16))
                    xw = p.dma("gpsimd", xg_sb[:].rearrange("p (c m) -> p c m", c=16),
                               xT[b, :, g * 512:(g + 1) * 512].rearrange("(c p) m -> p c m", p=128))
                    wring = Ring(2)
                    for oc in range(16):
                        wi, wfr = wring.acquire()
                        wt = p.dma("gpsimd", wqb[:, wi * 2048:(wi + 1) * 2048]
                                   .rearrange("p (c m) -> p c m", c=16),
                                   C["wqT_bf"][:, oc * 128:(oc + 1) * 128]
                                   .rearrange("(c p) m -> p c m", p=128),
                                   waits=wfr + C["wtok"]["wq"])
                        key = "a" if oc % 2 == 0 else "b"
                        ps = ps_a if oc % 2 == 0 else ps_b
                        mm = None
                        for ic in range(16):
                            mm = p.op("tensor", "matmul", ps[:, 0:512],
                                      wqb[:, wi * 2048 + ic * 128: wi * 2048 + ic * 128 + 128],
                                      xg_sb[:, ic * 512:(ic + 1) * 512],
                                      start=(ic == 0), stop=(ic == 15),
                                      waits=([wt, xw] + p.bank_use(key) if ic == 0 else []),
                                      inc=("t_e" if ic == 15 else None))
                        ev = p.op("scalar", "activation", qg[:, oc * 512:(oc + 1) * 512],
                                  ps[:, 0:512], AF.Identity, waits=[mm], inc="s_e")
                        p.bank_done(key, ev)
                        rp = rope_rows(p, C, qg[64:128, oc * 512:(oc + 1) * 512],
                                       ps_s[64:128, 0:512], t64b, g, base=64, bank="s",
                                       src_tok=ev)
                        wring.release(wi, ev, rp)
                    mQn, mQr = sm[0:1, 11:12], sm[0:1, 12:13]
                    z0 = p.op("vector", "memset", sm[0:1, 11:13], 0.0, inc="v_e")
                    for oc in range(16):
                        sq = p.op("scalar", "activation", sqch[:, :],
                                  qg[:, oc * 512:(oc + 1) * 512], AF.Square,
                                  waits=[Tok("v_e", p.led["v_e"]),
                                         Tok("s_e", p.led["s_e"])], inc="s_e")
                        mmn = p.op("tensor", "matmul", ps_s[0:1, 0:512],
                                   ones_sb[0:64, 0:1], sqch[0:64, :],
                                   start=True, stop=True,
                                   waits=[sq] + p.bank_use("s"), inc="t_e")
                        mmr = p.op("tensor", "matmul", ps_o[0:1, 0:512],
                                   ones_sb[64:128, 0:1], sqch[64:128, :],
                                   start=True, stop=True,
                                   waits=[sq] + p.bank_use("o"), inc="t_e")
                        r1 = p.op("vector", "tensor_reduce", sm[0:1, 13:14],
                                  ps_s[0:1, 0:512], AX.X, ALU.max, waits=[mmn], inc="v_e")
                        p.bank_done("s", r1)
                        x1 = p.op("vector", "tensor_max", mQn, mQn, sm[0:1, 13:14],
                                  waits=[r1, z0], inc="v_e")
                        r2 = p.op("vector", "tensor_reduce", sm[0:1, 14:15],
                                  ps_o[0:1, 0:512], AX.X, ALU.max, waits=[mmr], inc="v_e")
                        p.bank_done("o", r2)
                        p.op("vector", "tensor_max", mQr, mQr, sm[0:1, 14:15],
                             waits=[r2, x1], inc="v_e")
                    def _sqrt_into(dst, srcv):
                        tv = newton_rsqrt(p, C, dst, srcv, 1, nparts=1,
                                          waits=[Tok("v_e", p.led["v_e"])])
                        return p.op("vector", "tensor_mul", dst, dst, srcv,
                                    waits=[tv], inc="v_e")
                    tqn = _sqrt_into(sm[0:1, 13:14], mQn)
                    tqr = _sqrt_into(sm[0:1, 14:15], mQr)
                    tkn = _sqrt_into(sm[0:1, 4:5], mKn)
                    tkr = _sqrt_into(sm[0:1, 5:6], mKr)
                    t1 = p.op("vector", "tensor_mul", sm[0:1, 6:7], sm[0:1, 13:14],
                              sm[0:1, 4:5], waits=[tqn, tkn], inc="v_e")
                    t2 = p.op("vector", "tensor_mul", sm[0:1, 7:8], sm[0:1, 14:15],
                              sm[0:1, 5:6], waits=[tqr, tkr], inc="v_e")
                    t3 = p.op("vector", "tensor_add", sm[0:1, 6:7], sm[0:1, 6:7],
                              sm[0:1, 7:8], waits=[t1, t2], inc="v_e")
                    t4 = p.op("vector", "tensor_scalar", smb[0:1, 0:1], sm[0:1, 6:7],
                              -SCALE, None, ALU.mult, waits=[t3], inc="v_e")
                    bmm = p.op("tensor", "matmul", ps_s[0:128, 0:1], ones_sb[0:1, :],
                               smb[0:1, 0:1], start=True, stop=True,
                               waits=[t4] + p.bank_use("s"), inc="t_e")
                    bcp = p.op("vector", "tensor_copy", bnd[:, 0:1], ps_s[:, 0:1],
                               waits=[bmm], inc="v_e")
                    p.bank_done("s", bcp)
                with contextlib.ExitStack() as shd:
                    pT = shd.enter_context(sbt(nc, "pT", [128, 2 * 16 * 512], BF16))
                    linv = shd.enter_context(sbt(nc, "linv", [128, 2], F32))
                    avdone = {0: None, 1: None}
                    qg_done = [Tok("s_e", p.led["s_e"]), Tok("v_e", p.led["v_e"])]
                    for h in range(H):
                        par = h % 2
                        poff = par * 16 * 512
                        exp_toks = []
                        for kc in range(nkc):
                            key = "a" if kc % 2 == 0 else "b"
                            ps = ps_a if kc % 2 == 0 else ps_b
                            w0 = p.bank_use(key) + (qg_done if kc == 0 else [])
                            mm = p.op("tensor", "matmul", ps[:, 0:512],
                                      KAll[:, h * S + kc * 128: h * S + kc * 128 + 128],
                                      qg[:, h * 512:(h + 1) * 512],
                                      start=True, stop=(kc < 4 * g),
                                      waits=w0, inc=("t_e" if kc < 4 * g else None))
                            if kc >= 4 * g:
                                mm = p.op("tensor", "matmul", ps[:, 0:512],
                                          ident_sb[:, :],
                                          mask_sb[:, (kc - 4 * g) * 512:(kc - 4 * g + 1) * 512],
                                          start=False, stop=True, inc="t_e")
                            et = p.op("scalar", "activation",
                                      pT[:, poff + kc * 512: poff + (kc + 1) * 512],
                                      ps[:, 0:512],
                                      AF.Exp, bias=bnd[:, 0:1], scale=SCALE,
                                      waits=[mm, bcp] +
                                            ([avdone[par]] if (kc == 0 and avdone[par])
                                             else []),
                                      inc="s_e")
                            p.bank_done(key, et)
                            exp_toks.append(et)
                        for qs in range(4):
                            mm = None
                            for kc in range(nkc):
                                mm = p.op("tensor", "matmul", ps_o[:, 0:130],
                                          pT[:, poff + kc * 512 + qs * 128:
                                             poff + kc * 512 + qs * 128 + 128],
                                          Vst[:, kc * 2080 + h * 130: kc * 2080 + h * 130 + 130],
                                          start=(kc == 0), stop=(kc == nkc - 1),
                                          waits=(exp_toks + p.bank_use("o")
                                                 if kc == 0 else []),
                                          inc=("t_e" if kc == nkc - 1 else None))
                            tl = p.op("vector", "tensor_scalar_max", linv[:, 0:1],
                                      ps_o[:, 128:129], 1e-35, waits=[mm], inc="v_e")
                            tl = p.op("vector", "reciprocal", linv[:, 0:1], linv[:, 0:1],
                                      waits=[tl], inc="v_e")
                            avdone[par] = p.op("vector", "tensor_scalar",
                                 osb[:, qs * D + h * 128: qs * D + h * 128 + 128],
                                 ps_o[:, 0:128], linv[:, 0:1], None, ALU.mult,
                                 waits=[tl], inc="v_e")
                            p.bank_done("o", avdone[par])
                p.waitonly("vector", [Tok("t_e", p.led["t_e"]), Tok("s_e", p.led["s_e"])])
                sqq.close()
                with contextlib.ExitStack() as swo:
                    otb = swo.enter_context(sbt(nc, "otb", [128, 16 * 128], BF16))
                    wob = swo.enter_context(sbt(nc, "wob", [128, 2 * 2048], BF16))
                    resf = swo.enter_context(sbt(nc, "resf", [128, D], F32))
                    xnew = swo.enter_context(sbt(nc, "xnew", [128, D], BF16))
                    xres = swo.enter_context(sbt(nc, "xres", [128, D], F32))
                    woring = Ring(2)
                    xnew_prev = None
                    for qs in range(4):
                        tt = g * 4 + qs
                        xr = p.dma("gpsimd", xres[:, :],
                                   x_in[b, tt * 128:(tt + 1) * 128, :],
                                   waits=[Tok("s_e", p.led["s_e"])])
                        cps = []
                        osb_tok = Tok("v_e", p.led["v_e"])
                        for h in range(H):
                            key = "t" if h % 2 == 0 else "t2"
                            pst = ps_t if h % 2 == 0 else ps_t2
                            tp = p.op("tensor", "transpose", pst[:, 0:128],
                                      osb[:, qs * D + h * 128: qs * D + h * 128 + 128],
                                      ident_sb[:, :],
                                      waits=[osb_tok] + p.bank_use(key), inc="t_e")
                            cp = p.op("vector", "tensor_copy", otb[:, h * 128:(h + 1) * 128],
                                      pst[:, 0:128], waits=[tp], inc="v_e")
                            p.bank_done(key, cp)
                            cps.append(cp)
                        mmf = []
                        for dc in range(16):
                            wi, wfr = woring.acquire()
                            wt = p.dma("gpsimd", wob[:, wi * 2048:(wi + 1) * 2048],
                                       C["woT_bf"][dc * 128:(dc + 1) * 128, :],
                                       waits=wfr + C["wtok"]["wo"])
                            for cb in range(4):
                                psx = (ps_a, ps_b, ps_c, ps_d)[cb]
                                key = "abcd"[cb]
                                mm = p.op("tensor", "matmul", psx[:, 0:512],
                                          otb[:, dc * 128:(dc + 1) * 128],
                                          wob[:, wi * 2048 + cb * 512: wi * 2048 + cb * 512 + 512],
                                          start=(dc == 0), stop=(dc == 15),
                                          waits=([wt, cps[-1]] + p.bank_use(key)
                                                 if dc == 0 else [wt]),
                                          inc=("t_e" if (dc == 15 or cb == 3) else None))
                                if dc == 15:
                                    mmf.append(mm)
                            wring_tok = mm
                            woring.release(wi, wring_tok)
                        ra = []
                        for cb in range(4):
                            psx = (ps_a, ps_b, ps_c, ps_d)[cb]
                            t = p.op("vector", "tensor_add", resf[:, cb * 512:(cb + 1) * 512],
                                     psx[:, 0:512], xres[:, cb * 512:(cb + 1) * 512],
                                     waits=[mmf[cb], xr], inc="v_e")
                            p.bank_done("abcd"[cb], t)
                            ra.append(t)
                        emit_ln(p, C, resf[:, :], xnew[:, :],
                                waits=ra + ([xnew_prev] if xnew_prev else []))
                        xnew_prev = p.dma("sync", xN[b, tt * 128:(tt + 1) * 128, :],
                                          xnew[:, :], waits=[Tok("v_e", p.led["v_e"])])
            p.waitonly("gpsimd", [Tok("v_e", p.led["v_e"]), Tok("t_e", p.led["t_e"]),
                                  Tok("s_e", p.led["s_e"]), xnew_prev])
            p.waitonly("sync", [Tok("v_e", p.led["v_e"]), Tok("t_e", p.led["t_e"]),
                                Tok("s_e", p.led["s_e"])])
    p.fence()
    emit_xN_to_xT(p, C, b)



def emit_P3(p, C, b):
    nc = C["nc"]
    xT, xN = C["xT"], C["xN"]
    wgate, wup, hT, wdown_bf = C["wgate"], C["wup"], C["hT"], C["wdown_bf"]
    ident_sb = C["ident_sb"]
    ps_a, ps_b, ps_c, ps_d = C["ps_a"], C["ps_b"], C["ps_c"], C["ps_d"]
    ps_t, ps_t2 = C["ps_t"], C["ps_t2"]

    with contextlib.ExitStack() as sA:
        xtb = sA.enter_context(sbt(nc, "xtb", [128, 2 * 2048], BF16))
        wgs = sA.enter_context(sbt(nc, "wgs", [128, 2 * 2 * 16 * 512], BF16))
        wus = sA.enter_context(sbt(nc, "wus", [128, 2 * 2 * 16 * 512], BF16))
        hwrk = sA.enter_context(sbt(nc, "hwrk", [128, 8 * 512], BF16))
        pw = sA.enter_context(sbt(nc, "pw", [128, 16 * 128], BF16))
        hTblk = sA.enter_context(sbt(nc, "hTblk", [128, 8 * S], BF16))
        wring = Ring(2)
        xring = Ring(2)
        hT_prev = None
        t2_prev = a9_prev = tp_prev = None
        banks = (ps_a, ps_b, ps_c, ps_d)
        for cbg in range(8):
            wi, wfr = wring.acquire()
            woff = wi * 2 * 16 * 512
            wgt = p.dma("gpsimd", wgs[:, woff:woff + 16384]
                        .rearrange("p (c m) -> p c m", c=16),
                        C["wgate"][:, cbg * 1024:(cbg + 1) * 1024]
                        .rearrange("(c p) m -> p c m", p=128),
                        waits=wfr)
            wut = p.dma("gpsimd", wus[:, woff:woff + 16384]
                        .rearrange("p (c m) -> p c m", c=16),
                        C["wup"][:, cbg * 1024:(cbg + 1) * 1024]
                        .rearrange("(c p) m -> p c m", p=128))
            last_sp = None
            for tt in range(16):
                xi, xfr = xring.acquire()
                xoff = xi * 2048
                xw = p.dma("gpsimd", xtb[:, xoff:xoff + 2048]
                           .rearrange("p (c m) -> p c m", c=16),
                           xT[b, :, tt * 128:(tt + 1) * 128]
                           .rearrange("(c p) m -> p c m", p=128), waits=xfr)
                mmx = [None] * 4
                for ic in range(16):
                    lhs = xtb[:, xoff + ic * 128: xoff + ic * 128 + 128]
                    for half in range(2):
                        mmx[half] = p.op("tensor", "matmul", banks[half][:, 0:512],
                                   lhs,
                                   wgs[:, woff + ic * 1024 + half * 512:
                                       woff + ic * 1024 + half * 512 + 512],
                                   start=(ic == 0), stop=(ic == 15),
                                   waits=([xw, wgt] + p.bank_use("abcd"[half])
                                          if ic == 0 else []),
                                   inc=("t_e" if ic == 15 else None))
                        mmx[2 + half] = p.op("tensor", "matmul", banks[2 + half][:, 0:512],
                                   lhs,
                                   wus[:, woff + ic * 1024 + half * 512:
                                       woff + ic * 1024 + half * 512 + 512],
                                   start=(ic == 0), stop=(ic == 15),
                                   waits=([xw, wut] + p.bank_use("abcd"[2 + half])
                                          if ic == 0 else []),
                                   inc=("t_e" if ic == 15 else None))
                hg = hwrk[:, 0:1024]
                h = hwrk[:, 1024:2048]
                sq = hwrk[:, 2048:3072]
                msk = hwrk[:, 3072:4096]
                t1a = p.op("scalar", "activation", hg[:, 0:512], banks[0][:, 0:512],
                           AF.Silu, waits=[mmx[0]] + ([t2_prev] if t2_prev else []),
                           inc="s_e")
                t1b = p.op("scalar", "activation", hg[:, 512:1024], banks[1][:, 0:512],
                           AF.Silu, waits=[mmx[1]], inc="s_e")
                p.bank_done("a", t1a)
                p.bank_done("b", t1b)
                t2a = p.op("vector", "tensor_mul", h[:, 0:512], hg[:, 0:512],
                           banks[2][:, 0:512],
                           waits=[t1a, mmx[2]] + ([tp_prev] if tp_prev else []),
                           inc="v_e")
                t2 = p.op("vector", "tensor_mul", h[:, 512:1024], hg[:, 512:1024],
                          banks[3][:, 0:512], waits=[t1b, mmx[3]], inc="v_e")
                p.bank_done("c", t2a)
                p.bank_done("d", t2)
                t2_prev = t2
                t3 = p.op("vector", "tensor_mul", sq, h, h,
                          waits=[t2a, t2] + ([a9_prev] if a9_prev else []), inc="v_e")
                v4 = sq.rearrange("p (g f) -> p g f", f=4)
                m01 = pw[:, 0:256]; m23 = pw[:, 256:512]
                l01 = pw[:, 512:768]; l23 = pw[:, 768:1024]
                u1 = pw[:, 1024:1280]; u2 = pw[:, 1280:1536]; m2 = pw[:, 1536:1792]
                a1 = p.op("vector", "tensor_max", m01, v4[:, :, 0], v4[:, :, 1], waits=[t3], inc="v_e")
                a2 = p.op("vector", "tensor_max", m23, v4[:, :, 2], v4[:, :, 3], inc="v_e")
                a3 = p.op("vector", "tensor_tensor", l01, v4[:, :, 0], v4[:, :, 1], ALU.min, inc="v_e")
                a4 = p.op("vector", "tensor_tensor", l23, v4[:, :, 2], v4[:, :, 3], ALU.min, inc="v_e")
                a5 = p.op("vector", "tensor_tensor", u1, m01, m23, ALU.min, waits=[a1, a2], inc="v_e")
                a6 = p.op("vector", "tensor_max", u2, l01, l23, waits=[a3, a4], inc="v_e")
                a7 = p.op("vector", "tensor_max", m2, u1, u2, waits=[a5, a6], inc="v_e")
                a8 = p.op("vector", "tensor_tensor", msk.rearrange("p (g f) -> p g f", f=4),
                          v4, m2.rearrange("p (g o) -> p g o", o=1).broadcast_to([128, 256, 4]),
                          ALU.is_ge, waits=[a7], inc="v_e")
                a9 = p.op("vector", "tensor_mul", h, h, msk, waits=[a8], inc="v_e")
                a9_prev = a9
                for j in range(8):
                    key = "t" if j % 2 == 0 else "t2"
                    pst = ps_t if j % 2 == 0 else ps_t2
                    tp = p.op("tensor", "transpose", pst[:, 0:128],
                              h[:, j * 128:(j + 1) * 128], ident_sb[:, :],
                              waits=[a9] + p.bank_use(key), inc="t_e")
                    last_sp = p.op("vector", "tensor_copy",
                                   hTblk[:, j * S + tt * 128: j * S + tt * 128 + 128],
                                   pst[:, 0:128],
                                   waits=[tp] + ([hT_prev] if (tt == 0 and hT_prev) else []),
                                   inc="v_e")
                    p.bank_done(key, last_sp)
                    tp_prev = tp
                xring.release(xi, *mmx)
            st = p.dma("sync", hT[b, cbg * 1024:(cbg + 1) * 1024, :]
                       .rearrange("(c p) t -> p c t", p=128),
                       hTblk[:].rearrange("p (c t) -> p c t", c=8),
                       waits=[Tok("v_e", p.led["v_e"])])
            hT_prev = st
            wring.release(wi, st)
        p.fence()

    with contextlib.ExitStack() as sB:
        wd_sb = sB.enter_context(sbt(nc, "wd_sb", [128, 64 * 1024], BF16))
        ht_sb = sB.enter_context(sbt(nc, "ht_sb", [128, 2 * 64 * 128], BF16))
        resH = sB.enter_context(sbt(nc, "resH", [128, 8 * 1024], BF16))
        resf = sB.enter_context(sbt(nc, "resf", [128, D], F32))
        xrs = sB.enter_context(sbt(nc, "xrs", [128, D], BF16))
        xnew = sB.enter_context(sbt(nc, "xnew", [128, D], BF16))
        htring = Ring(2)
        xn_prev = None
        for th in range(2):
          for dh in range(2):
            wdt = p.dma("gpsimd", wd_sb[:].rearrange("p (c m) -> p c m", c=64),
                        wdown_bf[:, dh * 1024:(dh + 1) * 1024]
                        .rearrange("(c p) m -> p c m", p=128),
                        waits=[Tok("t_e", p.led.get("t_e", 0))] + C["wtok"]["wd"])
            for tl in range(8):
                tt = th * 8 + tl
                hti, htfr = htring.acquire()
                hoff = hti * 8192
                htok = p.dma("gpsimd",
                             ht_sb[:, hoff:hoff + 8192].rearrange("p (c m) -> p c m", c=64),
                             hT[b, :, tt * 128:(tt + 1) * 128]
                             .rearrange("(c p) m -> p c m", p=128),
                             waits=htfr)
                mm0 = mm1 = None
                for ic in range(64):
                    lhs = ht_sb[:, hoff + ic * 128: hoff + ic * 128 + 128]
                    mm0 = p.op("tensor", "matmul", ps_a[:, 0:512], lhs,
                               wd_sb[:, ic * 1024:ic * 1024 + 512],
                               start=(ic == 0), stop=(ic == 63),
                               waits=([wdt, htok] + p.bank_use("a") if ic == 0 else []),
                               inc=("t_e" if ic == 63 else None))
                    mm1 = p.op("tensor", "matmul", ps_b[:, 0:512], lhs,
                               wd_sb[:, ic * 1024 + 512:ic * 1024 + 1024],
                               start=(ic == 0), stop=(ic == 63),
                               waits=([wdt, htok] + p.bank_use("b") if ic == 0 else []),
                               inc=("t_e" if ic == 63 else None))
                if dh == 0:
                    e0 = p.op("vector", "tensor_copy", resH[:, tl * 1024:tl * 1024 + 512],
                              ps_a[:, 0:512], waits=[mm0], inc="v_e")
                    e1 = p.op("vector", "tensor_copy", resH[:, tl * 1024 + 512:(tl + 1) * 1024],
                              ps_b[:, 0:512], waits=[mm1], inc="v_e")
                    p.bank_done("a", e0)
                    p.bank_done("b", e1)
                else:
                    xr = p.dma("gpsimd", xrs[:, :], xN[b, tt * 128:(tt + 1) * 128, :],
                               waits=[Tok("v_e", p.led["v_e"]),
                                      Tok("s_e", p.led.get("s_e", 0))])
                    r0 = p.op("vector", "tensor_add", resf[:, 0:1024],
                              resH[:, tl * 1024:(tl + 1) * 1024], xrs[:, 0:1024],
                              waits=[xr] + ([xn_prev] if xn_prev else []), inc="v_e")
                    r1 = p.op("vector", "tensor_add", resf[:, 1024:1536],
                              ps_a[:, 0:512], xrs[:, 1024:1536], waits=[mm0, xr], inc="v_e")
                    r2 = p.op("vector", "tensor_add", resf[:, 1536:2048],
                              ps_b[:, 0:512], xrs[:, 1536:2048], waits=[mm1, xr], inc="v_e")
                    p.bank_done("a", r1)
                    p.bank_done("b", r2)
                    emit_ln(p, C, resf[:, :], xnew[:, :], waits=[r0, r1, r2])
                    xn_prev = p.dma("sync", xN[b, tt * 128:(tt + 1) * 128, :], xnew[:, :],
                                    waits=[Tok("v_e", p.led["v_e"])])
                htring.release(hti, mm0, mm1)
        p.fence()
    emit_xN_to_xT(p, C, b)


def emit_precast(p, C, pairs, first=True):
    nc = C["nc"]
    if first:
        C["wtok"] = {}
    with sbt(nc, "pcb", [128, 4 * 4096], BF16) as buf:
        ring = Ring(4)
        for (name, srcap, dstap, rows, cols) in pairs:
            toks = []
            for r0 in range(0, rows, 128):
                for c0 in range(0, cols, 4096):
                    w = min(4096, cols - c0)
                    i, fr = ring.acquire()
                    sl = buf[:, i * 4096:i * 4096 + w]
                    ld = p.dma("gpsimd", sl, srcap[r0:r0 + 128, c0:c0 + w], waits=fr)
                    st = p.dma("sync", dstap[r0:r0 + 128, c0:c0 + w], sl, waits=[ld])
                    ring.release(i, st)
                    toks.append(st)
            C["wtok"][name] = toks
        alltoks = [t for ts in C["wtok"].values() for t in ts]
        for e in ENGS:
            p.waitonly(e, alltoks)



def emit_elu(p, C, dst, src_ps, bias_col, tmp1, tmp2, waits=()):
    ev = p.op("scalar", "activation", tmp1, src_ps, AF.Identity, bias=bias_col,
              waits=list(waits), inc="s_e")
    t1 = p.op("vector", "tensor_scalar_min", tmp2, tmp1, 0.0, waits=[ev], inc="v_e")
    t2 = p.op("scalar", "activation", tmp2, tmp2, AF.Exp, waits=[t1], inc="s_e")
    t3 = p.op("vector", "tensor_scalar_max", tmp1, tmp1, 0.0, waits=[t2], inc="v_e")
    t4 = p.op("vector", "scalar_tensor_tensor", dst, tmp2, -1.0, tmp1,
              ALU.add, ALU.add, waits=[t3], inc="v_e")
    return t4


def emit_P4(p, C, b):
    nc = C["nc"]
    xT, xN = C["xT"], C["xN"]
    Wqk, Wv, Wu, Wo1 = C["Wqk"], C["Wv"], C["Wu"], C["Wo1"]
    bqk_t, bv_r, bu_r, bo_r, lng_r, lnb_r = (C["bqk_t"], C["bv"], C["bu"],
                                             C["bo1"], C["ln_g"], C["ln_b"])
    ident_sb, ones_sb = C["ident_sb"], C["ones_sb"]
    ps_a, ps_b, ps_c, ps_d = C["ps_a"], C["ps_b"], C["ps_c"], C["ps_d"]
    ps_o, ps_s, ps_t, ps_t2 = C["ps_o"], C["ps_s"], C["ps_t"], C["ps_t2"]

    with contextlib.ExitStack() as sco:
        QT = sco.enter_context(sbt(nc, "QT", [128, 16 * S], BF16))
        uN = C["uN"]
        kv_sb = sco.enter_context(sbt(nc, "kv_sb", [128, 16 * 128], BF16))
        browb = sco.enter_context(sbt(nc, "browb", [1, 4 * D], BF16))
        bqk_sb = sco.enter_context(sbt(nc, "bqk_sb", [128, 16], F32))
        lngb = sco.enter_context(sbt(nc, "lngb", [128, 2 * D], BF16))
        browstack = contextlib.ExitStack()
        brow = browstack.enter_context(sbt(nc, "brow", [1, D], BF16))
        bq_t = p.dma("gpsimd", bqk_sb[:, :], bqk_t[:, :])
        br_t = p.dma("gpsimd", browb[0:1, 0:D], bv_r[None, :])
        br_u = p.dma("gpsimd", browb[0:1, D:2 * D], bu_r[None, :])
        br_o = p.dma("gpsimd", browb[0:1, 2 * D:3 * D], bo_r[None, :])
        br_g = p.dma("gpsimd", browb[0:1, 3 * D:4 * D], lng_r[None, :])
        br_b = p.dma("gpsimd", brow[0:1, 0:D], lnb_r[None, :])
        for (srow, dst0, wt0) in ((browb[0:1, 3 * D:4 * D], 0, br_g),
                                  (brow[0:1, 0:D], D, br_b)):
            for cb in range(4):
                m1 = p.op("tensor", "matmul", ps_s[:, 0:512], ones_sb[0:1, :],
                          srow[0:1, cb * 512:(cb + 1) * 512],
                          start=True, stop=True,
                          waits=[wt0] + p.bank_use("s"), inc="t_e")
                cpv = p.op("vector", "tensor_copy", lngb[:, dst0 + cb * 512: dst0 + cb * 512 + 512],
                     ps_s[:, 0:512], waits=[m1], inc="v_e")
                p.bank_done("s", cpv)

        p.fence()
        browstack.close()
        with contextlib.ExitStack() as sA:
            xg = sA.enter_context(sbt(nc, "xgA", [128, 16 * S], BF16))
            wqkb = sA.enter_context(sbt(nc, "wqkb", [128, 2 * 16 * 128], BF16))
            tA = sA.enter_context(sbt(nc, "tA", [128, 2 * 512], BF16))
            tB = sA.enter_context(sbt(nc, "tB", [128, 2 * 512], BF16))
            wring = Ring(2)
            e4_prev = {0: None, 1: None}
            xw = p.dma("gpsimd", xg[:].rearrange("p (c m) -> p c m", c=16),
                       xT[b].rearrange("(c p) m -> p c m", p=128),
                       waits=[Tok("v_e", p.led["v_e"])])
            for oc in range(16):
                wi, wfr = wring.acquire()
                woff = wi * 2048
                wt = p.dma("gpsimd", wqkb[:, woff:woff + 2048].rearrange("p (c m) -> p c m", c=16),
                           C["Wqk_bf"][:, oc * 128:(oc + 1) * 128].rearrange("(c p) m -> p c m", p=128),
                           waits=wfr + C["wtok"]["Wqk"])
                banks = (ps_a, ps_b, ps_c, ps_d)
                mmx = [None] * 4
                for ic in range(16):
                    for gt in range(4):
                        mmx[gt] = p.op("tensor", "matmul", banks[gt][:, 0:512],
                                  wqkb[:, woff + ic * 128: woff + ic * 128 + 128],
                                  xg[:, ic * S + gt * 512: ic * S + gt * 512 + 512],
                                  start=(ic == 0), stop=(ic == 15),
                                  waits=([wt, xw] + p.bank_use("abcd"[gt])
                                         if ic == 0 else []),
                                  inc=("t_e" if ic == 15 else None))
                e4s = []
                for gt in range(4):
                    par = gt % 2
                    e4 = emit_elu(p, C, QT[:, oc * S + gt * 512: oc * S + gt * 512 + 512],
                                  banks[gt][:, 0:512], bqk_sb[:, oc:oc + 1],
                                  tA[:, par * 512:(par + 1) * 512],
                                  tB[:, par * 512:(par + 1) * 512],
                                  waits=[mmx[gt], bq_t] +
                                        ([e4_prev[par]] if e4_prev[par] else []))
                    e4_prev[par] = e4
                    p.bank_done("abcd"[gt], e4)
                    e4s.append(e4)
                wring.release(wi, *e4s)
            p.soft_fence()

        with contextlib.ExitStack() as sB:
            xt2 = sB.enter_context(sbt(nc, "xt2b", [128, 2 * 16 * 128], BF16))
            wub = sB.enter_context(sbt(nc, "wub", [128, 16 * 2048], BF16))
            untile = sB.enter_context(sbt(nc, "untile", [128, D], BF16))
            wt = p.dma("gpsimd", wub[:].rearrange("p (c m) -> p c m", c=16),
                       C["Wu_bf"].rearrange("(c p) m -> p c m", p=128),
                       waits=C["wtok"]["Wu"])
            xring = Ring(2)
            un_prev = None
            for tt in range(16):
                xi, xfr = xring.acquire()
                xoff = xi * 2048
                xw = p.dma("gpsimd", xt2[:, xoff:xoff + 2048].rearrange("p (c m) -> p c m", c=16),
                           xT[b, :, tt * 128:(tt + 1) * 128].rearrange("(c p) m -> p c m", p=128),
                           waits=xfr)
                mmf = []
                for ic in range(16):
                    woff = ic * 2048
                    for cb in range(4):
                        psx = (ps_a, ps_b, ps_c, ps_d)[cb]
                        mm = p.op("tensor", "matmul", psx[:, 0:512],
                                  xt2[:, xoff + ic * 128: xoff + ic * 128 + 128],
                                  wub[:, woff + cb * 512: woff + cb * 512 + 512],
                                  start=(ic == 0), stop=False,
                                  waits=([wt, xw] + p.bank_use("abcd"[cb])
                                         if ic == 0 else []),
                                  inc=("t_e" if cb == 3 else None))
                        if ic == 15:
                            mmf.append(mm)
                evs = []
                for cb in range(4):
                    psx = (ps_a, ps_b, ps_c, ps_d)[cb]
                    mb = p.op("tensor", "matmul", psx[:, 0:512], ones_sb[0:1, :],
                              browb[0:1, D + cb * 512: D + cb * 512 + 512],
                              start=False, stop=True, waits=[br_u], inc="t_e")
                    ev = p.op("scalar", "activation",
                              untile[:, cb * 512:(cb + 1) * 512],
                              psx[:, 0:512], AF.Silu,
                              waits=[mb] + ([un_prev] if (cb == 0 and un_prev) else []),
                              inc="s_e")
                    p.bank_done("abcd"[cb], ev)
                    evs.append(ev)
                un_prev = p.dma("sync", uN[b, tt * 128:(tt + 1) * 128, :], untile[:, :],
                                waits=evs)
                xring.release(xi, *evs)
            p.soft_fence(extra=[un_prev])

        with contextlib.ExitStack() as sC:
            xt3 = sC.enter_context(sbt(nc, "xt3", [128, 2 * 16 * 128], BF16))
            wvb = sC.enter_context(sbt(nc, "wvb", [128, 16 * 2048], BF16))
            vt2 = sC.enter_context(sbt(nc, "vt", [128, 2 * D], BF16))
            kt2 = sC.enter_context(sbt(nc, "kt", [128, 2 * D], BF16))
            wvt = p.dma("gpsimd", wvb[:].rearrange("p (c m) -> p c m", c=16),
                        C["Wv_bf"].rearrange("(c p) m -> p c m", p=128),
                        waits=C["wtok"]["Wv"])
            xring = Ring(2)
            kvps = (ps_a, ps_b, ps_c, ps_d)
            for tt in range(16):
                vt = vt2[:, (tt % 2) * D:(tt % 2) * D + D]
                kt = kt2[:, (tt % 2) * D:(tt % 2) * D + D]
                xi, xfr = xring.acquire()
                xoff = xi * 2048
                xw = p.dma("gpsimd", xt3[:, xoff:xoff + 2048].rearrange("p (c m) -> p c m", c=16),
                           xT[b, :, tt * 128:(tt + 1) * 128].rearrange("(c p) m -> p c m", p=128),
                           waits=xfr)
                evs = []
                for half in range(2):
                    mm = None
                    for ic in range(16):
                        woff = ic * 2048 + half * 1024
                        for cb in range(2):
                            psx = (ps_o, ps_s)[cb]
                            key = ("o", "s")[cb]
                            mm = p.op("tensor", "matmul", psx[:, 0:512],
                                      xt3[:, xoff + ic * 128: xoff + ic * 128 + 128],
                                      wvb[:, woff + cb * 512: woff + cb * 512 + 512],
                                      start=(ic == 0), stop=False,
                                      waits=([wvt, xw] + p.bank_use(key)
                                             if ic == 0 else []),
                                      inc=("t_e" if cb == 1 else None))
                    for cb in range(2):
                        psx = (ps_o, ps_s)[cb]
                        mb = p.op("tensor", "matmul", psx[:, 0:512], ones_sb[0:1, :],
                                  browb[0:1, half * 1024 + cb * 512: half * 1024 + cb * 512 + 512],
                                  start=False, stop=True, waits=[br_t], inc="t_e")
                        ev = p.op("scalar", "activation",
                                  vt[:, half * 1024 + cb * 512: half * 1024 + cb * 512 + 512],
                                  psx[:, 0:512], AF.Silu, waits=[mb], inc="s_e")
                        p.bank_done(("o", "s")[cb], ev)
                        evs.append(ev)
                cps = []
                for dc in range(16):
                    key = "t" if dc % 2 == 0 else "t2"
                    pst = ps_t if dc % 2 == 0 else ps_t2
                    tp = p.op("tensor", "transpose", pst[:, 0:128],
                              QT[:, dc * S + tt * 128: dc * S + tt * 128 + 128],
                              ident_sb[:, :], waits=p.bank_use(key), inc="t_e")
                    cp = p.op("vector", "tensor_copy", kt[:, dc * 128:(dc + 1) * 128],
                              pst[:, 0:128], waits=[tp], inc="v_e")
                    p.bank_done(key, cp)
                    cps.append(cp)
                for h in range(H):
                    psx = kvps[h // 4]
                    co = (h % 4) * 128
                    bw = p.bank_use("abcd"[h // 4]) if (tt == 0 and h % 4 == 0) else []
                    p.op("tensor", "matmul", psx[:, co:co + 128],
                         kt[:, h * 128:(h + 1) * 128], vt[:, h * 128:(h + 1) * 128],
                         start=(tt == 0 and h % 4 == 0), stop=(tt == 15 and h % 4 == 3),
                         skip_group_check=True,
                         waits=[cps[h], evs[h // 4]] + bw,
                         inc=("t_e" if tt == 15 else None))
                xring.release(xi, *cps)
            for h in range(H):
                psx = kvps[h // 4]
                co = (h % 4) * 128
                cpv = p.op("vector", "tensor_copy", kv_sb[:, h * 128:(h + 1) * 128],
                     psx[:, co:co + 128],
                     waits=[Tok("t_e", p.led["t_e"])], inc="v_e")
                if h % 4 == 3:
                    p.bank_done("abcd"[h // 4], cpv)
            p.soft_fence()

        with contextlib.ExitStack() as sD:
            oTg = sD.enter_context(sbt(nc, "oTg", [128, 16 * 256], BF16))
            oN2 = sD.enter_context(sbt(nc, "oN", [128, 2 * D], BF16))
            o2T2 = sD.enter_context(sbt(nc, "o2T", [128, 2 * 16 * 128], BF16))
            wob = sD.enter_context(sbt(nc, "wob4", [128, 16 * 2048], BF16))
            resf = sD.enter_context(sbt(nc, "resf4", [128, D], F32))
            lnsc = resf
            wot = p.dma("gpsimd", wob[:].rearrange("p (c m) -> p c m", c=16),
                        C["Wo1_bf"].rearrange("(c p) m -> p c m", p=128),
                        waits=C["wtok"]["Wo1"])
            xrs = sD.enter_context(sbt(nc, "xrs4", [128, D], BF16))
            xnew = sD.enter_context(sbt(nc, "xnew4", [128, D], BF16))
            xn_prev = None
            kv_done = Tok("v_e", p.led["v_e"])
            for gt in range(8):
                oTg_free = Tok("t_e", p.led["t_e"])
                for h in range(H):
                    key = "o" if h % 2 == 0 else "s"
                    ps = ps_o if h % 2 == 0 else ps_s
                    mm = p.op("tensor", "matmul", ps[:, 0:256],
                              kv_sb[:, h * 128:(h + 1) * 128],
                              QT[:, h * S + gt * 256: h * S + gt * 256 + 256],
                              start=True, stop=True,
                              waits=p.bank_use(key) + [oTg_free]
                                    + ([kv_done] if gt == 0 else []),
                              inc="t_e")
                    cpv = p.op("vector", "tensor_copy", oTg[:, h * 256:(h + 1) * 256],
                         ps[:, 0:256], waits=[mm], inc="v_e")
                    p.bank_done(key, cpv)
                for j in range(2):
                    tt = gt * 2 + j
                    oN = oN2[:, (tt % 2) * D:(tt % 2) * D + D]
                    o2T = o2T2[:, (tt % 2) * D:(tt % 2) * D + D]
                    cps = []
                    oTg_tok = Tok("v_e", p.led["v_e"])
                    for dc in range(16):
                        key = "t" if dc % 2 == 0 else "t2"
                        pst = ps_t if dc % 2 == 0 else ps_t2
                        tp = p.op("tensor", "transpose", pst[:, 0:128],
                                  oTg[:, dc * 256 + j * 128: dc * 256 + j * 128 + 128],
                                  ident_sb[:, :], waits=[oTg_tok] + p.bank_use(key),
                                  inc="t_e")
                        cp = p.op("vector", "tensor_copy", oN[:, dc * 128:(dc + 1) * 128],
                                  pst[:, 0:128], waits=[tp], inc="v_e")
                        p.bank_done(key, cp)
                        cps.append(cp)
                    stat = C["stat"]
                    s1, s2, mu, rst = stat[:, 0:1], stat[:, 1:2], stat[:, 2:3], stat[:, 3:4]
                    t1 = p.op("vector", "tensor_reduce", s1, oN[:, :], AX.X, ALU.add,
                              waits=cps, inc="v_e")
                    t2 = p.op("scalar", "activation", o2T[:, :], oN[:, :], AF.Square,
                              accum_out=s2, waits=cps +
                              [Tok("t_e", p.led["t_e"])], inc="s_e")
                    tm = p.op("vector", "tensor_scalar", mu, s1, 1.0 / D, None, ALU.mult,
                              waits=[t1], inc="v_e")
                    tv = p.op("vector", "tensor_mul", stat[:, 4:5], mu, mu, waits=[tm], inc="v_e")
                    tv = p.op("vector", "tensor_scalar", stat[:, 4:5], stat[:, 4:5], LN_EPS,
                              None, ALU.subtract, waits=[tv], inc="v_e")
                    tv = p.op("vector", "scalar_tensor_tensor", stat[:, 5:6], s2, 1.0 / D,
                              stat[:, 4:5], ALU.mult, ALU.subtract, waits=[tv, t2], inc="v_e")
                    tv = newton_rsqrt(p, C, rst, stat[:, 5:6], 1, waits=[tv])
                    ta = p.op("vector", "scalar_tensor_tensor", lnsc[:, :], oN[:, :], mu,
                              lngb[:, 0:D], ALU.subtract, ALU.mult,
                              waits=[tv] + ([xn_prev] if xn_prev else []), inc="v_e")
                    tg2 = p.op("vector", "scalar_tensor_tensor", lnsc[:, :], lnsc[:, :], rst,
                              lngb[:, D:2 * D], ALU.mult, ALU.add,
                              waits=[ta], inc="v_e")
                    ut = p.dma("gpsimd", xrs[:, :], uN[b, tt * 128:(tt + 1) * 128, :],
                               waits=[Tok("v_e", p.led["v_e"])])
                    tu = p.op("vector", "tensor_mul", oN[:, :], lnsc[:, :], xrs[:, :],
                              waits=[tg2, ut], inc="v_e")
                    cps2 = []
                    for dc in range(16):
                        key = "t" if dc % 2 == 0 else "t2"
                        pst = ps_t if dc % 2 == 0 else ps_t2
                        tp = p.op("tensor", "transpose", pst[:, 0:128],
                                  oN[:, dc * 128:(dc + 1) * 128], ident_sb[:, :],
                                  waits=[tu] + p.bank_use(key), inc="t_e")
                        cp = p.op("vector", "tensor_copy", o2T[:, dc * 128:(dc + 1) * 128],
                                  pst[:, 0:128], waits=[tp], inc="v_e")
                        p.bank_done(key, cp)
                        cps2.append(cp)
                    xr = p.dma("gpsimd", xrs[:, :], xN[b, tt * 128:(tt + 1) * 128, :],
                               waits=[tu, Tok("v_e", p.led["v_e"])])
                    mmf = []
                    for dc in range(16):
                        woff = dc * 2048
                        for cb in range(4):
                            psx = (ps_a, ps_b, ps_c, ps_d)[cb]
                            mm = p.op("tensor", "matmul", psx[:, 0:512],
                                      o2T[:, dc * 128:(dc + 1) * 128],
                                      wob[:, woff + cb * 512: woff + cb * 512 + 512],
                                      start=(dc == 0), stop=False,
                                      waits=([wot, cps2[-1]] + p.bank_use("abcd"[cb])
                                             if dc == 0 else []),
                                      inc=("t_e" if cb == 3 else None))
                    ra = []
                    for cb in range(4):
                        psx = (ps_a, ps_b, ps_c, ps_d)[cb]
                        mb = p.op("tensor", "matmul", psx[:, 0:512], ones_sb[0:1, :],
                                  browb[0:1, 2 * D + cb * 512: 2 * D + cb * 512 + 512],
                                  start=False, stop=True, waits=[br_o], inc="t_e")
                        t = p.op("vector", "tensor_add", resf[:, cb * 512:(cb + 1) * 512],
                                 psx[:, 0:512], xrs[:, cb * 512:(cb + 1) * 512],
                                 waits=[mb, xr], inc="v_e")
                        p.bank_done("abcd"[cb], t)
                        ra.append(t)
                    emit_ln(p, C, resf[:, :], xnew[:, :],
                            waits=ra + ([xn_prev] if xn_prev else []))
                    xn_prev = p.dma("sync", xN[b, tt * 128:(tt + 1) * 128, :], xnew[:, :],
                                    waits=[Tok("v_e", p.led["v_e"])])
            p.fence()
    emit_xN_to_xT(p, C, b)



def emit_P5(p, C, moe_sched):
    nc = C["nc"]
    xT, xN, y_out = C["xT"], C["xN"], C["y_out"]
    gwT, WT = C["gwT"], C["WT"]
    atoms_p, imp_p, midx, eaw_r = C["atoms_p"], C["imp_p"], C["midx_s"], C["eaw_r"]
    o3T = C["o3T"]
    ident_sb, ones_sb = C["ident_sb"], C["ones_sb"]
    iota_rc_c, iota8_c = C["iota_rc_c"], C["iota8_c"]
    ps_a, ps_b, ps_c, ps_d = C["ps_a"], C["ps_b"], C["ps_c"], C["ps_d"]
    ps_o, ps_s, ps_t, ps_t2 = C["ps_o"], C["ps_s"], C["ps_t"], C["ps_t2"]
    sched, nslots = moe_sched["sched"], moe_sched["nslots"]

    with contextlib.ExitStack() as sco:
        wsb = sco.enter_context(sbt(nc, "wsb", [8, 16384], BF16))
        midx_f = sco.enter_context(sbt(nc, "midxf", [128, 128], F32))
        rows_f = sco.enter_context(sbt(nc, "rowsf", [128, 128], F32))
        cols_i = sco.enter_context(sbt(nc, "colsi", [128, 128], I32))
        iota_rc = sco.enter_context(sbt(nc, "iotarc", [128, 128], F32))
        rsh = sco.enter_context(sbt(nc, "rsh", [128, 1], F32))
        iota8 = sco.enter_context(sbt(nc, "iota8", [128, 8], F32))
        gw_sb = sco.enter_context(sbt(nc, "gw_sb", [128, 16 * 8], BF16))

        it1 = p.dma("sync", iota_rc[:, :], iota_rc_c.ap()[:, :])
        it2 = p.dma("sync", iota8[:, :], iota8_c.ap()[:, :])
        gwt = p.dma("gpsimd", gw_sb[:].rearrange("p (c m) -> p c m", c=16),
                    gwT.rearrange("(c p) m -> p c m", p=128))

        with contextlib.ExitStack() as spro:
            smw = spro.enter_context(sbt(nc, "smw", [8, 8], F32))
            e8 = spro.enter_context(sbt(nc, "e8", [8, 64], F32))
            e8b = spro.enter_context(sbt(nc, "e8b", [8, 64], BF16))
            smT = spro.enter_context(sbt(nc, "smT", [64, 8], BF16))
            atomb = spro.enter_context(sbt(nc, "atomb", [64, 2048], BF16))
            impb = spro.enter_context(sbt(nc, "impb", [8, 2048], F32))
            lt = p.dma("gpsimd", e8[:, :], eaw_r[:, :])
            t1 = p.op("vector", "tensor_reduce", smw[0:8, 0:1], e8[:, :], AX.X, ALU.max,
                      negate=True, waits=[lt], inc="v_e")
            t2 = p.op("scalar", "activation", e8b[:, :], e8[:, :], AF.Exp,
                      bias=smw[0:8, 0:1], accum_out=smw[0:8, 1:2], waits=[t1], inc="s_e")
            t3 = p.op("vector", "reciprocal", smw[0:8, 2:3], smw[0:8, 1:2], waits=[t2], inc="v_e")
            t4 = p.op("vector", "tensor_scalar", e8b[:, :], e8b[:, :], smw[0:8, 2:3], None,
                      ALU.mult, waits=[t3], inc="v_e")
            tmm = p.op("tensor", "transpose", ps_t[0:64, 0:8], e8b[:, :], ident_sb[0:8, 0:8],
                       waits=[t4, Tok("v_e", p.led["v_e"])], inc="t_e")
            tcp = p.op("vector", "tensor_copy", smT[:, :], ps_t[0:64, 0:8], waits=[tmm], inc="v_e")
            for blk in range(8):
                at = p.dma("gpsimd", atomb[:, :], atoms_p[:, blk * 2048:(blk + 1) * 2048],
                           waits=[Tok("v_e", p.led["v_e"])])
                imt = p.dma("gpsimd", impb[:, :], imp_p[:, blk * 2048:(blk + 1) * 2048],
                            waits=[Tok("s_e", p.led["s_e"])])
                for q4 in range(4):
                    mm = p.op("tensor", "matmul", ps_o[0:8, 0:512], smT[:, :],
                              atomb[:, q4 * 512:(q4 + 1) * 512], start=True, stop=True,
                              waits=[at, tcp] + p.bank_use("o"), inc="t_e")
                    sg = p.op("scalar", "activation", impb[0:8, q4 * 512:(q4 + 1) * 512],
                              impb[0:8, q4 * 512:(q4 + 1) * 512], AF.Sigmoid,
                              waits=[imt], inc="s_e")
                    wv = p.op("vector", "tensor_mul",
                         wsb[:, blk * 2048 + q4 * 512: blk * 2048 + q4 * 512 + 512],
                         ps_o[0:8, 0:512], impb[0:8, q4 * 512:(q4 + 1) * 512],
                         waits=[mm, sg], inc="v_e")
                    p.bank_done("o", wv)
            mi = spro.enter_context(sbt(nc, "mi", [128, 128], I32))
            mi2 = spro.enter_context(sbt(nc, "mi2", [128, 128], I32))
            mt = p.dma("gpsimd", mi[:, :], midx[:, :])
            c1 = p.op("vector", "tensor_single_scalar", mi2[:, :], mi[:, :], 11,
                      ALU.arith_shift_right, waits=[mt], inc="v_e")
            c2 = p.op("vector", "tensor_copy", rows_f[:, :], mi2[:, :], waits=[c1], inc="v_e")
            p.op("vector", "tensor_single_scalar", cols_i[:, :], mi[:, :], 2047,
                 ALU.bitwise_and, waits=[c2], inc="v_e")
            p.fence()

        xt_all = sco.enter_context(sbt(nc, "xt_all5", [128, 16 * S], BF16))
        coefT = sco.enter_context(sbt(nc, "coefT", [8, S], BF16))
        xgv = sco.enter_context(sbt(nc, "xgv", [128, nslots * S], BF16))
        xg1 = sco.enter_context(sbt(nc, "xg1", [128, 2 * S], BF16))
        max_nk = max((len(s) for s in sched), default=1)
        rh = sco.enter_context(sbt(nc, "rh", [128, max(4, max_nk) * 128], BF16))
        wtb = sco.enter_context(sbt(nc, "wtb", [128, 2 * 16 * 128], BF16))
        cwk = sco.enter_context(sbt(nc, "cwk", [128, 48], F32))
        cwku = sco.enter_context(sbt(nc, "cwku", [128, 8], mybir.dt.uint32))
        coefb = sco.enter_context(sbt(nc, "coefb", [128, 8], BF16))
        otile = sco.enter_context(sbt(nc, "otile", [128, 512], BF16))
        resf = sco.enter_context(sbt(nc, "resf5", [128, 2 * D], F32))
        onb = sco.enter_context(sbt(nc, "onb", [128, D], BF16))
        xrs = sco.enter_context(sbt(nc, "xrs5", [128, 2 * D], BF16))
        yout_f = resf

        for b in range(NB):
            xw = p.dma("gpsimd", xt_all[:].rearrange("p (c m) -> p c m", c=16),
                       xT[b].rearrange("(c p) m -> p c m", p=128),
                       waits=[Tok("v_e", p.led["v_e"]), Tok("t_e", p.led["t_e"])])
            e3_prev = None
            for tt in range(16):
                mm = None
                for ic in range(16):
                    mm = p.op("tensor", "matmul", ps_o[:, 0:8],
                              xt_all[:, ic * S + tt * 128: ic * S + tt * 128 + 128],
                              gw_sb[:, ic * 8:(ic + 1) * 8],
                              start=(ic == 0), stop=(ic == 15),
                              waits=([xw, gwt, it2] + p.bank_use("o") if ic == 0 else []),
                              inc=("t_e" if ic == 15 else None))
                g0 = p.op("vector", "tensor_scalar_min", cwk[:, 0:8], ps_o[:, 0:8],
                          50.0, waits=[mm] + ([e3_prev] if e3_prev else []), inc="v_e")
                p.bank_done("o", g0)
                g1 = p.op("vector", "tensor_scalar_max", cwk[:, 0:8], cwk[:, 0:8],
                          -50.0, waits=[g0], inc="v_e")
                g2 = p.op("vector", "max", cwk[:, 8:16], cwk[:, 0:8], waits=[g1], inc="v_e")
                g3 = p.op("vector", "max_index", cwku[:, 0:8], cwk[:, 8:16], cwk[:, 0:8],
                          waits=[g2], inc="v_e")
                g4 = p.op("vector", "tensor_sub", cwk[:, 16:17], cwk[:, 9:10], cwk[:, 8:9],
                          waits=[g3], inc="v_e")
                g5 = p.op("scalar", "activation", cwk[:, 17:18], cwk[:, 16:17], AF.Exp,
                          waits=[g4], inc="s_e")
                g6 = p.op("vector", "tensor_scalar", cwk[:, 18:19], cwk[:, 17:18], 1.0, None,
                          ALU.add, waits=[g5], inc="v_e")
                g7 = p.op("vector", "reciprocal", cwk[:, 19:20], cwk[:, 18:19],
                          waits=[g6], inc="v_e")
                g8 = p.op("vector", "tensor_scalar", cwk[:, 20:21], cwk[:, 19:20], -1.0, 1.0,
                          ALU.mult, ALU.add, waits=[g7], inc="v_e")
                g9 = p.op("vector", "tensor_copy", cwk[:, 21:22], cwku[:, 0:1],
                          waits=[g3], inc="v_e")
                g10 = p.op("vector", "tensor_copy", cwk[:, 22:23], cwku[:, 1:2],
                           waits=[g3], inc="v_e")
                e1 = p.op("vector", "tensor_scalar", cwk[:, 24:32], iota8[:, :],
                          cwk[:, 21:22], None, ALU.is_equal, waits=[g9], inc="v_e")
                e1b = p.op("vector", "tensor_scalar", cwk[:, 32:40], cwk[:, 24:32],
                           cwk[:, 19:20], None, ALU.mult, waits=[e1, g7], inc="v_e")
                e2 = p.op("vector", "tensor_scalar", cwk[:, 24:32], iota8[:, :],
                          cwk[:, 22:23], None, ALU.is_equal, waits=[g10, e1b], inc="v_e")
                e3 = p.op("vector", "scalar_tensor_tensor", coefb[:, :], cwk[:, 24:32],
                          cwk[:, 20:21], cwk[:, 32:40], ALU.mult, ALU.add,
                          waits=[e2, g8], inc="v_e")
                e3_prev = e3
                tmc = p.op("tensor", "transpose", ps_t[0:8, 0:128], coefb[:, :],
                           ident_sb[:, :], waits=[e3] + p.bank_use("t"), inc="t_e")
                ctc = p.op("vector", "tensor_copy", coefT[:, tt * 128:(tt + 1) * 128],
                     ps_t[0:8, 0:128], waits=[tmc], inc="v_e")
                p.bank_done("t", ctc)
            computed = {}
            slot_readers = {}
            st_toks = []
            xg1ring = Ring(2)
            wring = Ring(2)
            st_prev = None
            for rc in range(16):
                for k in sched[rc]:
                    if k in computed:
                        continue
                    slot = k % nslots
                    xgi, xgfr = xg1ring.acquire()
                    gt = p.dma("gpsimd", xg1[:, xgi * S:(xgi + 1) * S], C["xT_full"],
                               waits=xgfr + slot_readers.pop(slot, []),
                               indirect_in=cols_i[:, k:k + 1], indirect_axis=1,
                               element_offset=b * D * S)
                    vv = []
                    for tbp in range(4):
                        vkey = "o" if tbp % 2 == 0 else "s"
                        vps = ps_o if tbp % 2 == 0 else ps_s
                        mmv = p.op("tensor", "matmul", vps[:, 0:512],
                                   wsb[:, k * 128:(k + 1) * 128],
                                   coefT[:, tbp * 512:(tbp + 1) * 512],
                                   start=True, stop=True,
                                   waits=p.bank_use(vkey), inc="t_e")
                        xv = p.op("vector", "tensor_mul",
                                  xgv[:, slot * S + tbp * 512: slot * S + tbp * 512 + 512],
                                  xg1[:, xgi * S + tbp * 512: xgi * S + tbp * 512 + 512],
                                  vps[:, 0:512],
                                  waits=[mmv, gt], inc="v_e")
                        p.bank_done(vkey, xv)
                        vv.append(xv)
                    computed[k] = vv[-1]
                    xg1ring.release(xgi, *vv)
                rhtoks = []
                rsp = None
                for (j, k) in enumerate(sched[rc]):
                    rsp = p.op("vector", "tensor_single_scalar", rsh[:, 0:1],
                               rows_f[:, k:k + 1], float(rc * 128), ALU.subtract,
                               waits=[Tok("t_e", p.led["t_e"])] +
                                     ([rhtoks[-1]] if rhtoks else []),
                               inc="v_e")
                    rhtoks.append(p.op("vector", "tensor_scalar",
                                       rh[:, j * 128:(j + 1) * 128],
                                       iota_rc[:, :],
                                       rsh[:, 0:1], None, ALU.is_equal,
                                       waits=[it1, rsp], inc="v_e"))
                wi, wfr = wring.acquire()
                woff = wi * 2048
                wt = p.dma("gpsimd", wtb[:, woff:woff + 2048].rearrange("p (c m) -> p c m", c=16),
                           C["WT_bf"][:, rc * 128:(rc + 1) * 128].rearrange("(c p) m -> p c m", p=128),
                           waits=wfr + C["wtok"]["WT"])
                lastmm = []
                for tb in range(4):
                    psx = (ps_a, ps_b, ps_c, ps_d)[tb]
                    mm = None
                    nk = len(sched[rc])
                    for ic in range(16):
                        mm = p.op("tensor", "matmul", psx[:, 0:512],
                                  wtb[:, woff + ic * 128: woff + ic * 128 + 128],
                                  xt_all[:, ic * S + tb * 512: ic * S + tb * 512 + 512],
                                  start=(ic == 0), stop=(ic == 15 and nk == 0),
                                  waits=([wt, xw] + p.bank_use("abcd"[tb]) if ic == 0 else []),
                                  inc=("t_e" if (ic == 15 and nk == 0) else None))
                    for (j, k) in enumerate(sched[rc]):
                        slot = k % nslots
                        mm = p.op("tensor", "matmul", psx[:, 0:512],
                                  rh[:, j * 128:(j + 1) * 128],
                                  xgv[:, slot * S + tb * 512: slot * S + tb * 512 + 512],
                                  start=False, stop=(j == nk - 1),
                                  waits=[rhtoks[j], computed[k]],
                                  inc="t_e")
                        slot_readers.setdefault(slot, []).append(mm)
                    ev = p.op("vector", "tensor_copy", otile[:, 0:512], psx[:, 0:512],
                              waits=[mm] + ([st_prev] if st_prev else []),
                              inc="v_e")
                    p.bank_done("abcd"[tb], ev)
                    st_prev = p.dma("sync", o3T[rc * 128:(rc + 1) * 128, tb * 512:(tb + 1) * 512],
                                    otile[:, 0:512], waits=[ev])
                    st_toks.append(st_prev)
                    lastmm.append(mm)
                wring.release(wi, *lastmm)
            o3_done = list(st_toks)
            stat = C["stat"]
            fin_prev = {}
            for tt in range(16):
                par = tt % 2
                onbS = onb[:, :]
                xrsS = xrs[:, par * D:(par + 1) * D]
                resS = resf[:, par * D:(par + 1) * D]
                youtS = yout_f[:, par * D:(par + 1) * D]
                pv = fin_prev.get(par, {})
                tpd = p.dma("scalar", onbS, o3T[:, tt * 128:(tt + 1) * 128],
                            transpose=True,
                            waits=o3_done + fin_prev.get(1 - par, {}).get("ra", []) +
                                  pv.get("ra", []))
                xr = p.dma("gpsimd", xrsS, xN[b, tt * 128:(tt + 1) * 128, :],
                           waits=pv.get("sq", []))
                s1 = stat[:, 8 + par:9 + par]
                s2 = stat[:, 10 + par:11 + par]
                ra = p.op("vector", "scalar_tensor_tensor", resS, onbS, 1.0, xrsS,
                          ALU.mult, ALU.add, accum_out=s1,
                          waits=[tpd, xr] + pv.get("ap", []) + pv.get("yst", []),
                          inc="v_e")
                sq = p.op("scalar", "activation", xrsS, resS, AF.Square,
                          accum_out=s2, waits=[ra], inc="s_e")
                mu = stat[:, 12 + par:13 + par]
                rst = stat[:, 14 + par:15 + par]
                tm = p.op("vector", "tensor_scalar", mu, s1, 1.0 / D, None, ALU.mult,
                          waits=[ra] + pv.get("nw", []), inc="v_e")
                tmpc = stat[:, 4 + par:5 + par]
                varc = stat[:, 6 + par:7 + par]
                tv = p.op("vector", "tensor_mul", tmpc, mu, mu, waits=[tm], inc="v_e")
                tv = p.op("vector", "tensor_scalar", tmpc, tmpc, LN_EPS,
                          None, ALU.subtract, waits=[tv], inc="v_e")
                tv = p.op("vector", "scalar_tensor_tensor", varc, s2, 1.0 / D,
                          tmpc, ALU.mult, ALU.subtract, waits=[tv, sq], inc="v_e")
                nw = newton_rsqrt(p, C, rst, varc, 1, waits=[tv])
                ap = p.op("vector", "tensor_scalar", youtS, resS, mu, rst,
                          ALU.subtract, ALU.mult,
                          waits=[nw] + pv.get("yst", []), inc="v_e")
                yst = p.dma("sync", y_out[b, tt * 128:(tt + 1) * 128, :], youtS,
                            waits=[ap])
                fin_prev[par] = dict(ra=[ra], sq=[sq], ap=[ap], yst=[yst], nw=[nw])
            p.fence()


def dbg_dump(p, C, name, sbuf_ap, rows, cols):
    nc = C["nc"]
    t = nc.dram_tensor(name, [rows, cols], mybir.dt.float32 if sbuf_ap.dtype == F32 else BF16,
                       kind="ExternalOutput").ap()
    p.dma("sync", t[:, :], sbuf_ap, waits=p.all_toks())
    p.fence()


def build_program(moe_sched=None, debug=()):
    nc = bass.Bass("TRN2", target_bir_lowering=False)
    p = Plan(nc)

    x_in = nc.dram_tensor("x", [NB, S, D], F32, kind="ExternalInput").ap()
    wqT = nc.dram_tensor("wqT", [D, D], F32, kind="ExternalInput").ap()
    wkvT = nc.dram_tensor("wkvT", [D, R], F32, kind="ExternalInput").ap()
    wkrT = nc.dram_tensor("wkrT", [D, RD], F32, kind="ExternalInput").ap()
    wkuT = nc.dram_tensor("wkuT", [R, H * ND], F32, kind="ExternalInput").ap()
    wvuT = nc.dram_tensor("wvuT", [R, D], F32, kind="ExternalInput").ap()
    woT = nc.dram_tensor("woT", [D, D], F32, kind="ExternalInput").ap()
    wgate = nc.dram_tensor("wgate", [D, I8K], F32, kind="ExternalInput").ap()
    wup = nc.dram_tensor("wup", [D, I8K], F32, kind="ExternalInput").ap()
    wdown = nc.dram_tensor("wdown", [I8K, D], F32, kind="ExternalInput").ap()
    Wqk = nc.dram_tensor("Wqk", [D, D], F32, kind="ExternalInput").ap()
    Wv = nc.dram_tensor("Wv", [D, D], F32, kind="ExternalInput").ap()
    Wu = nc.dram_tensor("Wu", [D, D], F32, kind="ExternalInput").ap()
    Wo1 = nc.dram_tensor("Wo1", [D, D], F32, kind="ExternalInput").ap()
    bqk_t = nc.dram_tensor("bqk_t", [128, 16], F32, kind="ExternalInput").ap()
    bv = nc.dram_tensor("bv", [D], F32, kind="ExternalInput").ap()
    bu = nc.dram_tensor("bu", [D], F32, kind="ExternalInput").ap()
    bo1 = nc.dram_tensor("bo1", [D], F32, kind="ExternalInput").ap()
    ln_g = nc.dram_tensor("ln_g", [D], F32, kind="ExternalInput").ap()
    ln_b = nc.dram_tensor("ln_b", [D], F32, kind="ExternalInput").ap()
    gwT = nc.dram_tensor("gwT", [D, 8], F32, kind="ExternalInput").ap()
    WT = nc.dram_tensor("WT", [D, D], F32, kind="ExternalInput").ap()
    atoms_p = nc.dram_tensor("atoms_p", [64, 16384], F32, kind="ExternalInput").ap()
    eaw_r = nc.dram_tensor("eaw_r", [8, 64], F32, kind="ExternalInput").ap()
    imp_p = nc.dram_tensor("imp_p", [8, 16384], F32, kind="ExternalInput").ap()
    midx_s = nc.dram_tensor("midx_s", [128, 128], I32, kind="ExternalInput").ap()

    y_out = nc.dram_tensor("y", [NB, S, D], F32, kind="ExternalOutput").ap()

    ident_c = nc.inline_tensor(np.eye(128, dtype=ml_dtypes.bfloat16), name="c_ident")
    ones_c = nc.inline_tensor(np.ones((128, 128), dtype=ml_dtypes.bfloat16), name="c_ones")
    pos = np.arange(S, dtype=np.float64)
    freqs = 1.0 / (10000.0 ** (np.arange(0, RD, 2, dtype=np.float64) / RD))
    ang = pos[None, :] * freqs[:, None]
    cos_np = np.repeat(np.cos(ang), 2, axis=0).astype(np.float32)
    sin_np = np.repeat(np.sin(ang), 2, axis=0).astype(np.float32)
    cos_c = nc.inline_tensor(bfc(cos_np), name="c_cos")
    sin_c = nc.inline_tensor(bfc(sin_np), name="c_sin")
    Rm = np.zeros((RD, RD), dtype=np.float32)
    for i in range(0, RD, 2):
        Rm[i, i + 1] = -1.0
        Rm[i + 1, i] = 1.0
    rot_c = nc.inline_tensor(bfc(Rm.T.copy()), name="c_rot")
    mvars = np.zeros((4, 128, 512), dtype=np.float32)
    for dlt in range(4):
        for pk in range(128):
            mvars[dlt, pk, :dlt * 128 + pk] = NEG_BIG
    mask_c = nc.inline_tensor(bfc(mvars.reshape(4 * 128, 512)), name="c_mask")
    iota_rc_c = nc.inline_tensor(np.tile(np.arange(128, dtype=np.float32), (128, 1)),
                                 name="c_iotarc")
    iota8_c = nc.inline_tensor(np.tile(np.arange(8, dtype=np.float32), (128, 1)),
                               name="c_iota8")

    xT_h = nc.dram_tensor("xT", [NB, D, S], BF16,
                          kind=("ExternalOutput" if "xT" in debug else "Internal"))
    xT = xT_h.ap()
    xN = nc.dram_tensor("xN", [NB, S, D], BF16,
                        kind=("ExternalOutput" if "xN" in debug else "Internal")).ap()
    hT = nc.dram_tensor("hT", [NB, I8K, S], BF16, kind="Internal").ap()
    uN = nc.dram_tensor("uN", [NB, S, D], BF16, kind="Internal").ap()
    o3T = nc.dram_tensor("o3T", [D, S], BF16, kind="Internal").ap()
    wdown_bf = nc.dram_tensor("wdown_bf", [I8K, D], BF16, kind="Internal").ap()
    wqT_bf = nc.dram_tensor("wqT_bf", [D, D], BF16, kind="Internal").ap()
    woT_bf = nc.dram_tensor("woT_bf", [D, D], BF16, kind="Internal").ap()
    wgate_bf = nc.dram_tensor("wgate_bf", [D, I8K], BF16, kind="Internal").ap()
    wup_bf = nc.dram_tensor("wup_bf", [D, I8K], BF16, kind="Internal").ap()
    Wqk_bf = nc.dram_tensor("Wqk_bf", [D, D], BF16, kind="Internal").ap()
    Wv_bf = nc.dram_tensor("Wv_bf", [D, D], BF16, kind="Internal").ap()
    Wu_bf = nc.dram_tensor("Wu_bf", [D, D], BF16, kind="Internal").ap()
    Wo1_bf = nc.dram_tensor("Wo1_bf", [D, D], BF16, kind="Internal").ap()
    WT_bf = nc.dram_tensor("WT_bf", [D, D], BF16, kind="Internal").ap()

    consts = {}
    ident_sb = nc.alloc_sbuf_tensor("ident_sb", [128, 128], BF16)
    ones_sb = nc.alloc_sbuf_tensor("ones_sb", [128, 128], BF16)
    cos_sb = nc.alloc_sbuf_tensor("cos_sb", [128, S], BF16)
    sin_sb = nc.alloc_sbuf_tensor("sin_sb", [128, S], BF16)
    rot_sb = nc.alloc_sbuf_tensor("rot_sb", [128, 64], BF16)


    magic_c = nc.inline_tensor(np.full((128, 64), MAGIC, dtype=np.int32), name="c_magic")
    pm32 = nc.alloc_sbuf_tensor("pm32", [128, 32], F32)
    pmt32 = nc.alloc_sbuf_tensor("pmt32", [128, 32], F32)
    nr1 = nc.alloc_sbuf_tensor("nr1", [128, 64], F32)
    nr2 = nc.alloc_sbuf_tensor("nr2", [128, 64], F32)
    magic_sb = nc.alloc_sbuf_tensor("magic_sb", [128, 64], I32)
    stat = nc.alloc_sbuf_tensor("stat", [128, 16], F32)

    ps_a = nc.alloc_psum_tensor("ps_a", [128, 512], F32)
    ps_b = nc.alloc_psum_tensor("ps_b", [128, 512], F32)
    ps_c = nc.alloc_psum_tensor("ps_c", [128, 512], F32)
    ps_d = nc.alloc_psum_tensor("ps_d", [128, 512], F32)
    ps_o = nc.alloc_psum_tensor("ps_o", [128, 512], F32)
    ps_s = nc.alloc_psum_tensor("ps_s", [128, 512], F32)
    ps_t = nc.alloc_psum_tensor("ps_t", [128, 512], BF16)
    ps_t2 = nc.alloc_psum_tensor("ps_t2", [128, 512], BF16)

    C = dict(nc=nc, p=p, x_in=x_in, y_out=y_out,
             wqT=wqT, wkvT=wkvT, wkrT=wkrT, wkuT=wkuT, wvuT=wvuT, woT=woT,
             wgate=wgate, wup=wup, wdown=wdown, hT=hT, wdown_bf=wdown_bf,
             wqT_bf=wqT_bf, woT_bf=woT_bf, wgate_bf=wgate_bf, wup_bf=wup_bf,
             Wqk_bf=Wqk_bf, Wv_bf=Wv_bf, Wu_bf=Wu_bf, Wo1_bf=Wo1_bf, WT_bf=WT_bf,
             Wqk=Wqk, Wv=Wv, Wu=Wu, Wo1=Wo1, bqk_t=bqk_t, bv=bv, bu=bu, bo1=bo1,
             ln_g=ln_g, ln_b=ln_b, uN=uN, o3T=o3T,
             gwT=gwT, WT=WT, atoms_p=atoms_p, eaw_r=eaw_r, imp_p=imp_p, midx_s=midx_s,
             iota_rc_c=iota_rc_c, iota8_c=iota8_c,
             xT=xT, xN=xN, xT_full=xT_h.ap(),
             ident_sb=ident_sb, ones_sb=ones_sb, cos_sb=cos_sb, sin_sb=sin_sb,
             rot_sb=rot_sb, mask_c=mask_c, debug=debug,
             pm32=pm32, pmt32=pmt32, nr1=nr1, nr2=nr2, magic_sb=magic_sb,
             stat=stat,
             ps_a=ps_a, ps_b=ps_b, ps_c=ps_c, ps_d=ps_d, ps_o=ps_o, ps_s=ps_s,
             ps_t=ps_t, ps_t2=ps_t2)

    p.dma("sync", ident_sb[:], ident_c.ap()[:, :])
    p.dma("sync", ones_sb[:], ones_c.ap()[:, :])
    p.dma("sync", cos_sb[0:64, :], cos_c.ap()[:, :])
    p.dma("sync", cos_sb[64:128, :], cos_c.ap()[:, :])
    p.dma("sync", sin_sb[0:64, :], sin_c.ap()[:, :])
    p.dma("sync", sin_sb[64:128, :], sin_c.ap()[:, :])
    p.dma("sync", rot_sb[0:64, :], rot_c.ap()[:, :])
    p.dma("sync", rot_sb[64:128, :], rot_c.ap()[:, :])
    p.dma("sync", magic_sb[:], magic_c.ap()[:, :])
    p.fence()

    emit_P0b(p, C, 0)
    PRECAST_PAIRS = [
        ("wq", C["wqT"], C["wqT_bf"], D, D),
        ("wo", C["woT"], C["woT_bf"], D, D),
        ("Wqk", C["Wqk"], C["Wqk_bf"], D, D),
        ("wd", C["wdown"], C["wdown_bf"], I8K, D),
        ("Wv", C["Wv"], C["Wv_bf"], D, D),
        ("Wu", C["Wu"], C["Wu_bf"], D, D),
        ("Wo1", C["Wo1"], C["Wo1_bf"], D, D),
        ("WT", C["WT"], C["WT_bf"], D, D),
    ]
    for b in range(NB):
        with contextlib.ExitStack() as sbatch:
            KAll = sbatch.enter_context(sbt(nc, "KAll", [128, H * S], BF16))
            Vst = sbatch.enter_context(sbt(nc, "Vst", [128, 16 * 2080], BF16))
            C["KAll"], C["Vst"] = KAll, Vst
            emit_P1(p, C, b, None)
            if b == 0:
                emit_P0b(p, C, 1, qload="gpsimd")
                emit_precast(p, C, PRECAST_PAIRS[0:3])
            if "p1" in debug and b == 0:
                dbg_dump(p, C, "dbg_KAll", KAll[:, :], 128, H * S)
                dbg_dump(p, C, "dbg_Vst", Vst[:, :], 128, 16 * 2080)
                break
            emit_P2(p, C, b)
            if b == 0:
                emit_precast(p, C, PRECAST_PAIRS[3:4], first=False)
        if "p2" in debug and b == 0:
            break
        emit_P3(p, C, b)
        if b == 0:
            emit_precast(p, C, PRECAST_PAIRS[4:], first=False)
        if "p3" in debug and b == 0:
            break
        emit_P4(p, C, b)
        if "p4" in debug and b == 0:
            break
    if not any(d in debug for d in ("p1", "p2", "p3", "p4")):
        emit_P5(p, C, moe_sched)

    p.fence()

    with nc.Block() as block:
        @block.tensor
        def _(t):
            for f in p.seq["tensor"]:
                f(t)
        @block.vector
        def _(v):
            for f in p.seq["vector"]:
                f(v)
        @block.scalar
        def _(s):
            for f in p.seq["scalar"]:
                f(s)
        @block.gpsimd
        def _(g):
            for f in p.seq["gpsimd"]:
                f(g)
        @block.sync
        def _(s):
            for f in p.seq["sync"]:
                f(s)
    return nc


def host_prep(inputs):
    T = lambda a: np.ascontiguousarray(np.asarray(a).T)
    m = dict(
        wqT=T(inputs["L0_attn_wq"]),
        wkvT=T(inputs["L0_attn_w_kv_down"]),
        wkrT=T(inputs["L0_attn_w_k_rope"]),
        wkuT=T(inputs["L0_attn_w_k_up"]),
        wvuT=T(inputs["L0_attn_w_v_up"]),
        woT=T(inputs["L0_attn_wo"]),
        wgate=np.ascontiguousarray(inputs["L0_ffn_w_gate"]),
        wup=np.ascontiguousarray(inputs["L0_ffn_w_up"]),
        wdown=np.ascontiguousarray(inputs["L0_ffn_w_down"]),
        Wqk=np.ascontiguousarray(inputs["L1_attn_Wqk"]),
        Wv=np.ascontiguousarray(inputs["L1_attn_Wv"]),
        Wu=np.ascontiguousarray(inputs["L1_attn_Wu"]),
        Wo1=np.ascontiguousarray(inputs["L1_attn_Wo"]),
        bqk_t=np.ascontiguousarray(inputs["L1_attn_bqk"].reshape(16, 128).T),
        bv=np.ascontiguousarray(inputs["L1_attn_bv"]),
        bu=np.ascontiguousarray(inputs["L1_attn_bu"]),
        bo1=np.ascontiguousarray(inputs["L1_attn_bo"]),
        ln_g=np.ascontiguousarray(inputs["L1_attn_ln_g"]),
        ln_b=np.ascontiguousarray(inputs["L1_attn_ln_b"]),
    )
    return m


def moe_schedule(mask_idx):
    perm = np.argsort(mask_idx, kind="stable")
    ms = np.asarray(mask_idx)[perm]
    rows = ms // D
    sched = []
    for rc in range(16):
        ks = np.nonzero((rows // 128 == rc).reshape(128, 128).any(axis=1))[0]
        sched.append(list(int(k) for k in ks))
    nslots = 4
    for rc in range(16):
        if sched[rc]:
            nslots = max(nslots, sched[rc][-1] - sched[rc][0] + 1)
    nslots = min(nslots + 1, 128)
    return perm, sched, nslots


def kernel(**inputs):
    shared = host_prep(inputs)
    mask_idx = np.asarray(inputs["L1_ffn_mask_idx"])
    perm, sched, nslots = moe_schedule(mask_idx)
    ms = mask_idx[perm].astype(np.int32)
    shared["midx_s"] = np.ascontiguousarray(ms.reshape(128, 128).T)
    shared["atoms_p"] = np.ascontiguousarray(inputs["L1_ffn_atoms"][:, perm])
    shared["imp_p"] = np.ascontiguousarray(inputs["L1_ffn_importance"][:, perm])
    shared["eaw_r"] = np.ascontiguousarray(inputs["L1_ffn_expert_atom_weights"])
    shared["gwT"] = np.ascontiguousarray(inputs["L1_ffn_gate_w"].T)
    shared["WT"] = np.ascontiguousarray(inputs["L1_ffn_W"].T)
    x = np.ascontiguousarray(inputs["L0_attn_x"])
    nc = build_program(moe_sched=dict(sched=sched, nslots=nslots))
    in_maps = []
    for c in range(8):
        im = dict(shared)
        im["x"] = np.ascontiguousarray(x[2 * c:2 * c + 2])
        in_maps.append(im)
    res = run_bass_kernel_spmd(nc, in_maps, core_ids=list(range(8)))
    return np.concatenate([r["y"] for r in res.results], axis=0)


# version 16 (final)
# speedup vs baseline: 1.0168x; 1.0168x over previous
import contextlib
import numpy as np
import ml_dtypes
import concourse.bass as bass
import concourse.mybir as mybir
from concourse.bass_core_rust import IndirectOffsetOnAxis
from concourse.bass_utils import run_bass_kernel_spmd

F32 = mybir.dt.float32
BF16 = mybir.dt.bfloat16
I32 = mybir.dt.int32
AF = mybir.ActivationFunctionType
ALU = mybir.AluOpType
AX = mybir.AxisListType

NB = 2
S = 2048
D = 2048
H = 16
HD = 128
ND = 64
RD = 64
R = 512
I8K = 8192
SCALE = float(1.0 / np.sqrt(np.float32(128.0)).astype(np.float32))
LN_EPS = 1e-5
NEG_BIG = -30000.0
MAGIC = 0x5f3759df

ENGS = ("tensor", "vector", "scalar", "gpsimd", "sync")


class Tok:
    __slots__ = ("sem", "val")
    def __init__(self, sem, val):
        self.sem = sem
        self.val = val


class Plan:
    def __init__(self, nc):
        self.nc = nc
        self.seq = {e: [] for e in ENGS}
        self.sems = {}
        self.led = {}
        self.lastwait = {e: {} for e in ENGS}

    def sem(self, name):
        if name not in self.sems:
            self.sems[name] = self.nc.alloc_semaphore(name)
            self.led[name] = 0
        return self.sems[name]

    def _waits(self, eng_name, waits):
        out = []
        lw = self.lastwait[eng_name]
        for t in waits:
            if t is None:
                continue
            if lw.get(t.sem, 0) >= t.val:
                continue
            lw[t.sem] = t.val
            out.append((self.sems[t.sem], t.val))
        return out

    def op(self, eng_name, meth, *args, waits=(), inc=None, inc_k=1, **kwargs):
        w = self._waits(eng_name, waits)
        tok = None
        inch = None
        if inc is not None:
            self.sem(inc)
            self.led[inc] += inc_k
            tok = Tok(inc, self.led[inc])
            inch = self.sems[inc]
        def emit(eng, w=w, meth=meth, args=args, kwargs=kwargs, tok=tok, inch=inch, inc_k=inc_k):
            for (h, v) in w:
                eng.wait_ge(h, v)
            inst = getattr(eng, meth)(*args, **kwargs)
            if tok is not None:
                inst.then_inc(inch, inc_k)
            return inst
        self.seq[eng_name].append(emit)
        return tok

    def waitonly(self, eng_name, waits):
        w = self._waits(eng_name, waits)
        if not w:
            return
        def emit(eng, w=w):
            for (h, v) in w:
                eng.wait_ge(h, v)
        self.seq[eng_name].append(emit)

    POOLQ = {"gpsimd": "g", "sync": "y", "scalar": "c", "vector": "w", "tensor": "t"}

    def dma(self, eng_name, out, in_, waits=(), inc=None, pool=None, npool=8, **kwargs):
        pool = pool or self.POOLQ[eng_name]
        if not hasattr(self, "pool_idx"):
            self.pool_idx = {}
        i = self.pool_idx.get(pool, -1)
        i = (i + 1) % npool
        self.pool_idx[pool] = i
        name = f"{pool}dma{i}"
        self.sem(name)
        w = list(waits)
        if self.led[name] > 0:
            w.append(Tok(name, self.led[name]))
        ind = kwargs.pop("indirect_in", None)
        if ind is not None:
            ax = kwargs.pop("indirect_axis", 0)
            return self.op(eng_name, "indirect_dma_start", out, None, in_,
                           IndirectOffsetOnAxis(ind, ax),
                           waits=w, inc=name, inc_k=16, **kwargs)
        return self.op(eng_name, "dma_start", waits=w, inc=name, inc_k=16,
                       out=out, in_=in_, **kwargs)

    def bank_use(self, key):
        if not hasattr(self, "bankfree"):
            self.bankfree = {}
        return self.bankfree.pop(key, [])

    def bank_done(self, key, tok):
        if not hasattr(self, "bankfree"):
            self.bankfree = {}
        if tok is not None:
            self.bankfree.setdefault(key, []).append(tok)

    def fence(self):
        toks = [Tok(n, self.led[n]) for n in self.led]
        for e in ENGS:
            self.waitonly(e, toks)

    def all_toks(self):
        return [Tok(n, self.led[n]) for n in self.led]


class Ring:
    def __init__(self, n):
        self.n = n
        self.free = [[] for _ in range(n)]
        self.i = 0

    def acquire(self):
        i = self.i % self.n
        self.i += 1
        toks = self.free[i]
        self.free[i] = []
        return i, toks

    def release(self, i, *toks):
        self.free[i].extend(t for t in toks if t is not None)



_NAME_I = [0]
def sbt(nc, name, shape, dt):
    _NAME_I[0] += 1
    return nc.sbuf_tensor(f"{name}{_NAME_I[0]}", shape, dt)

def bfc(a):
    return np.asarray(a, dtype=np.float32).astype(ml_dtypes.bfloat16)



def emit_P0b(p, C, b, qload="gpsimd"):
    x_in, xT, xN, nc = C["x_in"], C["xT"], C["xN"], C["nc"]
    C.setdefault("x_done", {})[b] = []
    with sbt(nc, "p0buf", [128, 4 * D], BF16) as buf:
        ring = Ring(2)
        xn_toks = []
        for tt in range(S // 128):
            i, fr = ring.acquire()
            sl = buf[:, i * D:(i + 1) * D]
            ld = p.dma(qload, sl, x_in[b, tt * 128:(tt + 1) * 128, :], waits=fr)
            st = p.dma("sync", xN[b, tt * 128:(tt + 1) * 128, :], sl, waits=[ld])
            ring.release(i, st)
            xn_toks.append(st)
        ring2 = Ring(2)
        for dc in range(D // 128):
            i, fr = ring2.acquire()
            sl = buf[:, (2 + i) * D:(2 + i) * D + S]
            tp = p.dma("scalar", sl, xN[b, :, dc * 128:(dc + 1) * 128],
                       transpose=True, waits=fr + (xn_toks if dc == 0 else []))
            st = p.dma("sync", xT[b, dc * 128:(dc + 1) * 128, :], sl, waits=[tp])
            ring2.release(i, st)
            C["x_done"][b].append(st)
        for e in ENGS:
            p.waitonly(e, C["x_done"][b])


def rope_rows(p, C, buf_slice, psum_small, tmp, gt, base=0, bank="s", src_tok=None):
    cos_sb, sin_sb, rot_sb = C["cos_sb"], C["sin_sb"], C["rot_sb"]
    lo, hi = base, base + 64
    cs = cos_sb[lo:hi, gt * 512:(gt + 1) * 512]
    sn = sin_sb[lo:hi, gt * 512:(gt + 1) * 512]
    st = [src_tok] if src_tok else [Tok("v_e", p.led.get("v_e", 0)),
                                    Tok("s_e", p.led.get("s_e", 0))]
    mm = p.op("tensor", "matmul", psum_small, rot_sb[lo:hi, :], buf_slice,
              start=True, stop=True,
              waits=st + p.bank_use(bank), inc="t_e")
    t1 = p.op("vector", "tensor_mul", tmp[lo:hi, :], buf_slice, cs,
              waits=st + p.bank_use("ropetmp"), inc="v_e")
    t2 = p.op("vector", "tensor_mul", psum_small, psum_small, sn,
              waits=[mm], inc="v_e")
    t3 = p.op("vector", "tensor_add", buf_slice, tmp[lo:hi, :], psum_small,
              waits=[t1, t2], inc="v_e")
    p.bank_done(bank, t3)
    p.bank_done("ropetmp", t3)
    return t3


def emit_P1(p, C, b, stack):
    nc = C["nc"]
    xT = C["xT"]
    wkvT, wkuT, wvuT, wkrT = C["wkvT"], C["wkuT"], C["wvuT"], C["wkrT"]
    KAll, Vst = C["KAll"], C["Vst"]
    ps_a, ps_b, ps_c, ps_d, ps_s = (C["ps_a"], C["ps_b"], C["ps_c"], C["ps_d"],
                                    C["ps_s"])

    with contextlib.ExitStack() as sco:
        ckvT = sco.enter_context(sbt(nc, "ckvT", [128, 4 * S], BF16))
        krT = sco.enter_context(sbt(nc, "krT", [64, S], BF16))
        C["dbg_ckvT"], C["dbg_krT"] = ckvT, krT

        with contextlib.ExitStack() as stA:
            wkv_sb = stA.enter_context(sbt(nc, "wkv_sb", [128, 2 * 2048], BF16))
            wkr_sb = stA.enter_context(sbt(nc, "wkr_sb", [128, 16 * RD], BF16))
            xg_sb = stA.enter_context(sbt(nc, "xg_sb", [128, 16 * 1024], BF16))
            t64 = stA.enter_context(sbt(nc, "t64", [128, 512], BF16))

            wkr_t = p.dma("gpsimd", wkr_sb[:].rearrange("p (c m) -> p c m", c=16),
                          wkrT.rearrange("(c p) m -> p c m", p=128))
            wkvring = Ring(2)

            xring = Ring(1)
            for gp in range(2):
                xi, xfr = xring.acquire()
                xw = p.dma("gpsimd", xg_sb[:].rearrange("p (c m) -> p c m", c=16),
                           xT[b, :, gp * 1024:(gp + 1) * 1024].rearrange("(c p) m -> p c m", p=128),
                           waits=xfr + C["x_done"][b])
                consumed = []
                for oc in range(4):
                    wvi, wvfr = wkvring.acquire()
                    wvoff = wvi * 2048
                    wkv_t = p.dma("gpsimd",
                                  wkv_sb[:, wvoff:wvoff + 2048].rearrange("p (c m) -> p c m", c=16),
                                  wkvT[:, oc * 128:(oc + 1) * 128].rearrange("(c p) m -> p c m", p=128),
                                  waits=wvfr)
                    keys = ("a", "b") if oc % 2 == 0 else ("c", "d")
                    pss = (ps_a, ps_b) if oc % 2 == 0 else (ps_c, ps_d)
                    mmx = [None, None]
                    for ic in range(16):
                        for gh in range(2):
                            mmx[gh] = p.op("tensor", "matmul", pss[gh][:, 0:512],
                                      wkv_sb[:, wvoff + ic * 128: wvoff + ic * 128 + 128],
                                      xg_sb[:, ic * 1024 + gh * 512: ic * 1024 + gh * 512 + 512],
                                      start=(ic == 0), stop=(ic == 15),
                                      waits=([wkv_t, xw] + p.bank_use(keys[gh])
                                             if ic == 0 else []),
                                      inc=("t_e" if ic == 15 else None))
                    for gh in range(2):
                        gt = gp * 2 + gh
                        ev = p.op("vector", "tensor_copy",
                                  ckvT[:, oc * S + gt * 512: oc * S + gt * 512 + 512],
                                  pss[gh][:, 0:512], waits=[mmx[gh]], inc="v_e")
                        p.bank_done(keys[gh], ev)
                        consumed.append(ev)
                    wkvring.release(wvi, *[mmx[0], mmx[1]])
                for gh in range(2):
                    gt = gp * 2 + gh
                    mm = None
                    for ic in range(16):
                        mm = p.op("tensor", "matmul", ps_s[0:64, 0:512],
                                  wkr_sb[:, ic * RD:(ic + 1) * RD],
                                  xg_sb[:, ic * 1024 + gh * 512: ic * 1024 + gh * 512 + 512],
                                  start=(ic == 0), stop=(ic == 15),
                                  waits=([wkr_t] + p.bank_use("s") if ic == 0 else []),
                                  inc=("t_e" if ic == 15 else None))
                    ev = p.op("vector", "tensor_copy", krT[:, gt * 512:(gt + 1) * 512],
                              ps_s[0:64, 0:512], waits=[mm], inc="v_e")
                    p.bank_done("s", ev)
                    rp = rope_rows(p, C, krT[:, gt * 512:(gt + 1) * 512],
                                   ps_s[0:64, 0:512], t64, gt, base=0, bank="s",
                                   src_tok=ev)
                    consumed.append(rp)
                xring.release(xi, *consumed)
            p.fence()

        with contextlib.ExitStack() as stB:
            wku_sb = stB.enter_context(sbt(nc, "wku_sb", [128, 4 * 1024], BF16))
            wku_t = p.dma("gpsimd", wku_sb[:].rearrange("p (c m) -> p c m", c=4),
                          wkuT.rearrange("(c p) m -> p c m", p=128))
            kbanks = (ps_a, ps_b, ps_c, ps_d)
            for hp in range(8):
                mmx = [None] * 4
                for rc in range(4):
                    for gt in range(4):
                        mmx[gt] = p.op("tensor", "matmul", kbanks[gt][:, 0:512],
                                  wku_sb[:, rc * 1024 + hp * 128: rc * 1024 + hp * 128 + 128],
                                  ckvT[:, rc * S + gt * 512: rc * S + gt * 512 + 512],
                                  start=(rc == 0), stop=(rc == 3),
                                  waits=([wku_t] + p.bank_use("abcd"[gt])
                                         if rc == 0 else []),
                                  inc=("t_e" if rc == 3 else None))
                for gt in range(4):
                    p.op("vector", "tensor_copy",
                         KAll[0:64, (2 * hp) * S + gt * 512:(2 * hp) * S + gt * 512 + 512],
                         kbanks[gt][0:64, 0:512], waits=[mmx[gt]], inc="v_e")
                    ev = p.op("vector", "tensor_copy",
                         KAll[0:64, (2 * hp + 1) * S + gt * 512:(2 * hp + 1) * S + gt * 512 + 512],
                         kbanks[gt][64:128, 0:512], waits=[mmx[gt]], inc="v_e")
                    p.bank_done("abcd"[gt], ev)
            rw = Tok("v_e", p.led["v_e"])
            for h in range(H):
                p.op("vector", "tensor_copy", KAll[64:128, h * S:(h + 1) * S],
                     krT[:, :], waits=[rw], inc="v_e")
            p.fence()

        with contextlib.ExitStack() as stC:
            wvu_sb = stC.enter_context(sbt(nc, "wvu_sb", [128, 4 * D], BF16))
            wvu_t = p.dma("gpsimd", wvu_sb[:].rearrange("p (c m) -> p c m", c=4),
                          wvuT.rearrange("(c p) m -> p c m", p=128))
            ones_v = Vst[:].rearrange("p (a u) -> p a u", u=130)[:, :, 128:130]
            p.op("vector", "memset", ones_v, 1.0, inc="v_e")
            vbanks = (ps_a, ps_b, ps_c, ps_d)
            for tt in range(16):
                mmx = [None] * 4
                for rc in range(4):
                    for dc4 in range(4):
                        mmx[dc4] = p.op("tensor", "matmul", vbanks[dc4][:, 0:512],
                                  ckvT[:, rc * S + tt * 128: rc * S + tt * 128 + 128],
                                  wvu_sb[:, rc * D + dc4 * 512: rc * D + dc4 * 512 + 512],
                                  start=(rc == 0), stop=(rc == 3),
                                  waits=([wvu_t] + p.bank_use("abcd"[dc4])
                                         if rc == 0 else []),
                                  inc=("t_e" if rc == 3 else None))
                for dc4 in range(4):
                    dst = Vst[:, tt * 2080 + dc4 * 4 * 130: tt * 2080 + (dc4 * 4 + 4) * 130]
                    ev = p.op("vector", "tensor_copy",
                         dst.rearrange("p (h m) -> p h m", h=4)[:, :, 0:128],
                         vbanks[dc4][:, 0:512].rearrange("p (h m) -> p h m", h=4),
                         waits=[mmx[dc4]], inc="v_e")
                    p.bank_done("abcd"[dc4], ev)
            p.fence()
        if "p1" in C["debug"]:
            dbg_dump(p, C, "dbg_ckvT_o", ckvT[:, :], 128, 4 * S)
            dbg_dump(p, C, "dbg_krT_o", krT[:, :], 64, S)
    p.fence()




def part_max128(p, C, out11, in_col, nparts=128):
    pm, pmt = C["pm32"], C["pmt32"]
    ps = C["ps_s"]
    t0 = p.op("vector", "memset", pm[:, :], -3e38, inc="v_e")
    t1 = p.op("vector", "tensor_copy", pm[0:nparts, 0:1], in_col, waits=[t0], inc="v_e")
    t2 = p.op("vector", "transpose", pmt[:, :], pm[:, :], waits=[t1], inc="v_e")
    c1 = p.op("vector", "tensor_copy", ps[0:64, 256:288], pmt[64:128, 0:32],
              waits=[t2], inc="v_e")
    t3 = p.op("vector", "tensor_max", pmt[0:64, 0:32], pmt[0:64, 0:32], ps[0:64, 256:288],
              waits=[c1], inc="v_e")
    c2 = p.op("vector", "tensor_copy", ps[0:32, 256:288], pmt[32:64, 0:32],
              waits=[t3], inc="v_e")
    t4 = p.op("vector", "tensor_max", pmt[0:32, 0:32], pmt[0:32, 0:32], ps[0:32, 256:288],
              waits=[c2], inc="v_e")
    t5 = p.op("vector", "tensor_reduce", out11, pmt[0:1, 0:32], AX.X, ALU.max,
              waits=[t4], inc="v_e")
    return t5


def newton_rsqrt(p, C, dst, src, w, nparts=128, waits=()):
    nr1, nr2, magic = C["nr1"], C["nr2"], C["magic_sb"]
    y = dst
    t = nr1[0:nparts, 0:w]
    n2 = nr2[0:nparts, 0:w]
    tv = p.op("vector", "tensor_scalar", n2.bitcast(I32), src.bitcast(I32),
              1, None, ALU.arith_shift_right, waits=list(waits), inc="v_e")
    tv = p.op("vector", "tensor_tensor", y.bitcast(I32), magic[0:nparts, 0:w].bitcast(I32),
              n2.bitcast(I32), ALU.subtract, waits=[tv], inc="v_e")
    for _ in range(3):
        tv = p.op("vector", "tensor_mul", t, y, y, waits=[tv], inc="v_e")
        tv = p.op("vector", "tensor_mul", t, t, src, waits=[tv], inc="v_e")
        tv = p.op("vector", "tensor_scalar", t, t, -0.5, 1.5,
                  ALU.mult, ALU.add, waits=[tv], inc="v_e")
        tv = p.op("vector", "tensor_mul", y, y, t, waits=[tv], inc="v_e")
    return tv


def emit_ln(p, C, resf, xnew_slice, out_f32=None, waits=(), sqscratch=None):
    stat = C["stat"]
    if sqscratch is None:
        sqscratch = xnew_slice
    s1, s2, mu, rst = (stat[:, 0:1], stat[:, 1:2], stat[:, 2:3], stat[:, 3:4])
    t1 = p.op("vector", "tensor_reduce", s1, resf, AX.X, ALU.add, waits=list(waits), inc="v_e")
    t2 = p.op("scalar", "activation", sqscratch, resf, AF.Square,
              accum_out=s2, waits=list(waits), inc="s_e")
    tm = p.op("vector", "tensor_scalar", mu, s1, 1.0 / D, None, ALU.mult,
              waits=[t1], inc="v_e")
    tv = p.op("vector", "tensor_mul", stat[:, 4:5], mu, mu, waits=[tm], inc="v_e")
    tv = p.op("vector", "tensor_scalar", stat[:, 4:5], stat[:, 4:5], LN_EPS, None,
              ALU.subtract, waits=[tv], inc="v_e")
    tv = p.op("vector", "scalar_tensor_tensor", stat[:, 5:6], s2, 1.0 / D,
              stat[:, 4:5], ALU.mult, ALU.subtract, waits=[tv, t2], inc="v_e")
    tv = newton_rsqrt(p, C, rst, stat[:, 5:6], 1, waits=[tv])
    toks = []
    ta = p.op("vector", "tensor_scalar", xnew_slice, resf, mu, rst,
              ALU.subtract, ALU.mult, waits=[tv], inc="v_e")
    toks.append(ta)
    if out_f32 is not None:
        tb = p.op("vector", "tensor_scalar", out_f32, resf, mu, rst,
                  ALU.subtract, ALU.mult, waits=[tv], inc="v_e")
        toks.append(tb)
    return toks


def emit_xN_to_xT(p, C, b):
    nc, xN, xT = C["nc"], C["xN"], C["xT"]
    with sbt(nc, "txbuf", [128, 4 * S], BF16) as buf:
        ring = Ring(4)
        for dc in range(D // 128):
            i, fr = ring.acquire()
            sl = buf[:, i * S:(i + 1) * S]
            tp = p.dma("scalar", sl, xN[b, :, dc * 128:(dc + 1) * 128],
                       transpose=True, waits=fr)
            st = p.dma("sync", xT[b, dc * 128:(dc + 1) * 128, :], sl, waits=[tp])
            ring.release(i, st)
    p.fence()


def emit_P2(p, C, b):
    nc = C["nc"]
    xT, xN, x_in = C["xT"], C["xN"], C["x_in"]
    wqT, woT = C["wqT"], C["woT"]
    KAll, Vst = C["KAll"], C["Vst"]
    ident_sb, ones_sb = C["ident_sb"], C["ones_sb"]
    ps_a, ps_b, ps_c, ps_d = C["ps_a"], C["ps_b"], C["ps_c"], C["ps_d"]
    ps_o, ps_s, ps_t, ps_t2 = C["ps_o"], C["ps_s"], C["ps_t"], C["ps_t2"]

    with contextlib.ExitStack() as sco:
        mask_sb = sco.enter_context(sbt(nc, "mask_sb", [128, 4 * 512], BF16))
        sm = sco.enter_context(sbt(nc, "sm", [128, 16], F32))
        smb = sco.enter_context(sbt(nc, "smb", [128, 8], BF16))
        bnd = sco.enter_context(sbt(nc, "bnd", [128, 1], F32))
        p.dma("sync", mask_sb[:].rearrange("p (c m) -> p c m", c=4),
              C["mask_c"].ap().rearrange("(c p) m -> p c m", p=128))

        mKn, mKr = sm[0:1, 8:9], sm[0:1, 9:10]
        sqstack = contextlib.ExitStack()
        sqscr = sqstack.enter_context(sbt(nc, "sqscr", [64, S], BF16))
        for (dst, rlo, rhi, hs) in ((mKn, 0, 64, range(H)), (mKr, 64, 128, range(1))):
            t0 = p.op("vector", "memset", dst, 0.0, inc="v_e")
            mx_prev = None
            for h in hs:
                sq = p.op("scalar", "activation", sqscr[:, :],
                          KAll[rlo:rhi, h * S:(h + 1) * S], AF.Square,
                          waits=[Tok("v_e", p.led["v_e"]), Tok("t_e", p.led.get("t_e", 0))],
                          inc="s_e")
                for c4 in range(4):
                    mm = p.op("tensor", "matmul", ps_s[0:1, 0:512],
                              ones_sb[0:64, 0:1], sqscr[:, c4 * 512:(c4 + 1) * 512],
                              start=True, stop=True,
                              waits=[sq] + p.bank_use("s"), inc="t_e")
                    rd = p.op("vector", "tensor_reduce", sm[0:1, 10:11],
                              ps_s[0:1, 0:512], AX.X, ALU.max,
                              waits=[mm] + ([mx_prev] if mx_prev else []), inc="v_e")
                    p.bank_done("s", rd)
                    mx_prev = p.op("vector", "tensor_max", dst, dst, sm[0:1, 10:11],
                         waits=[rd], inc="v_e")

        p.fence()
        sqstack.close()

        for g in range(4):
            nkc = 4 * g + 4
            with contextlib.ExitStack() as sgrp:
                osb = sgrp.enter_context(sbt(nc, "osb", [128, 4 * D], BF16))
                sqq = contextlib.ExitStack()
                qg = sqq.enter_context(sbt(nc, "qg", [128, 16 * 512], BF16))
                with contextlib.ExitStack() as spr:
                    xg_sb = spr.enter_context(sbt(nc, "xg2", [128, 16 * 512], BF16))
                    wqb = spr.enter_context(sbt(nc, "wqb", [128, 2 * 2048], BF16))
                    t64b = spr.enter_context(sbt(nc, "t64b", [128, 512], BF16))
                    sqch = spr.enter_context(sbt(nc, "sqch", [128, 512], BF16))
                    xw = p.dma("gpsimd", xg_sb[:].rearrange("p (c m) -> p c m", c=16),
                               xT[b, :, g * 512:(g + 1) * 512].rearrange("(c p) m -> p c m", p=128))
                    wring = Ring(2)
                    for oc in range(16):
                        wi, wfr = wring.acquire()
                        wt = p.dma("gpsimd", wqb[:, wi * 2048:(wi + 1) * 2048]
                                   .rearrange("p (c m) -> p c m", c=16),
                                   C["wqT_bf"][:, oc * 128:(oc + 1) * 128]
                                   .rearrange("(c p) m -> p c m", p=128),
                                   waits=wfr + C["wtok"]["wq"])
                        key = "a" if oc % 2 == 0 else "b"
                        ps = ps_a if oc % 2 == 0 else ps_b
                        mm = None
                        for ic in range(16):
                            mm = p.op("tensor", "matmul", ps[:, 0:512],
                                      wqb[:, wi * 2048 + ic * 128: wi * 2048 + ic * 128 + 128],
                                      xg_sb[:, ic * 512:(ic + 1) * 512],
                                      start=(ic == 0), stop=(ic == 15),
                                      waits=([wt, xw] + p.bank_use(key) if ic == 0 else []),
                                      inc=("t_e" if ic == 15 else None))
                        ev = p.op("scalar", "activation", qg[:, oc * 512:(oc + 1) * 512],
                                  ps[:, 0:512], AF.Identity, waits=[mm], inc="s_e")
                        p.bank_done(key, ev)
                        rp = rope_rows(p, C, qg[64:128, oc * 512:(oc + 1) * 512],
                                       ps_s[64:128, 0:512], t64b, g, base=64, bank="s",
                                       src_tok=ev)
                        wring.release(wi, ev, rp)
                    mQn, mQr = sm[0:1, 11:12], sm[0:1, 12:13]
                    z0 = p.op("vector", "memset", sm[0:1, 11:13], 0.0, inc="v_e")
                    for oc in range(16):
                        sq = p.op("scalar", "activation", sqch[:, :],
                                  qg[:, oc * 512:(oc + 1) * 512], AF.Square,
                                  waits=[Tok("v_e", p.led["v_e"]),
                                         Tok("s_e", p.led["s_e"])], inc="s_e")
                        mmn = p.op("tensor", "matmul", ps_s[0:1, 0:512],
                                   ones_sb[0:64, 0:1], sqch[0:64, :],
                                   start=True, stop=True,
                                   waits=[sq] + p.bank_use("s"), inc="t_e")
                        mmr = p.op("tensor", "matmul", ps_o[0:1, 0:512],
                                   ones_sb[64:128, 0:1], sqch[64:128, :],
                                   start=True, stop=True,
                                   waits=[sq] + p.bank_use("o"), inc="t_e")
                        r1 = p.op("vector", "tensor_reduce", sm[0:1, 13:14],
                                  ps_s[0:1, 0:512], AX.X, ALU.max, waits=[mmn], inc="v_e")
                        p.bank_done("s", r1)
                        x1 = p.op("vector", "tensor_max", mQn, mQn, sm[0:1, 13:14],
                                  waits=[r1, z0], inc="v_e")
                        r2 = p.op("vector", "tensor_reduce", sm[0:1, 14:15],
                                  ps_o[0:1, 0:512], AX.X, ALU.max, waits=[mmr], inc="v_e")
                        p.bank_done("o", r2)
                        p.op("vector", "tensor_max", mQr, mQr, sm[0:1, 14:15],
                             waits=[r2, x1], inc="v_e")
                    def _sqrt_into(dst, srcv):
                        tv = newton_rsqrt(p, C, dst, srcv, 1, nparts=1,
                                          waits=[Tok("v_e", p.led["v_e"])])
                        return p.op("vector", "tensor_mul", dst, dst, srcv,
                                    waits=[tv], inc="v_e")
                    tqn = _sqrt_into(sm[0:1, 13:14], mQn)
                    tqr = _sqrt_into(sm[0:1, 14:15], mQr)
                    tkn = _sqrt_into(sm[0:1, 4:5], mKn)
                    tkr = _sqrt_into(sm[0:1, 5:6], mKr)
                    t1 = p.op("vector", "tensor_mul", sm[0:1, 6:7], sm[0:1, 13:14],
                              sm[0:1, 4:5], waits=[tqn, tkn], inc="v_e")
                    t2 = p.op("vector", "tensor_mul", sm[0:1, 7:8], sm[0:1, 14:15],
                              sm[0:1, 5:6], waits=[tqr, tkr], inc="v_e")
                    t3 = p.op("vector", "tensor_add", sm[0:1, 6:7], sm[0:1, 6:7],
                              sm[0:1, 7:8], waits=[t1, t2], inc="v_e")
                    t4 = p.op("vector", "tensor_scalar", smb[0:1, 0:1], sm[0:1, 6:7],
                              -SCALE, None, ALU.mult, waits=[t3], inc="v_e")
                    bmm = p.op("tensor", "matmul", ps_s[0:128, 0:1], ones_sb[0:1, :],
                               smb[0:1, 0:1], start=True, stop=True,
                               waits=[t4] + p.bank_use("s"), inc="t_e")
                    bcp = p.op("vector", "tensor_copy", bnd[:, 0:1], ps_s[:, 0:1],
                               waits=[bmm], inc="v_e")
                    p.bank_done("s", bcp)
                with contextlib.ExitStack() as shd:
                    pT = shd.enter_context(sbt(nc, "pT", [128, 2 * 16 * 512], BF16))
                    linv = shd.enter_context(sbt(nc, "linv", [128, 2], F32))
                    avdone = {0: None, 1: None}
                    qg_done = [Tok("s_e", p.led["s_e"]), Tok("v_e", p.led["v_e"])]
                    for h in range(H):
                        par = h % 2
                        poff = par * 16 * 512
                        exp_toks = []
                        for kc in range(nkc):
                            key = "a" if kc % 2 == 0 else "b"
                            ps = ps_a if kc % 2 == 0 else ps_b
                            w0 = p.bank_use(key) + (qg_done if kc == 0 else [])
                            mm = p.op("tensor", "matmul", ps[:, 0:512],
                                      KAll[:, h * S + kc * 128: h * S + kc * 128 + 128],
                                      qg[:, h * 512:(h + 1) * 512],
                                      start=True, stop=(kc < 4 * g),
                                      waits=w0, inc=("t_e" if kc < 4 * g else None))
                            if kc >= 4 * g:
                                mm = p.op("tensor", "matmul", ps[:, 0:512],
                                          ident_sb[:, :],
                                          mask_sb[:, (kc - 4 * g) * 512:(kc - 4 * g + 1) * 512],
                                          start=False, stop=True, inc="t_e")
                            et = p.op("scalar", "activation",
                                      pT[:, poff + kc * 512: poff + (kc + 1) * 512],
                                      ps[:, 0:512],
                                      AF.Exp, bias=bnd[:, 0:1], scale=SCALE,
                                      waits=[mm, bcp] +
                                            ([avdone[par]] if (kc == 0 and avdone[par])
                                             else []),
                                      inc="s_e")
                            p.bank_done(key, et)
                            exp_toks.append(et)
                        for qs in range(4):
                            mm = None
                            for kc in range(nkc):
                                mm = p.op("tensor", "matmul", ps_o[:, 0:130],
                                          pT[:, poff + kc * 512 + qs * 128:
                                             poff + kc * 512 + qs * 128 + 128],
                                          Vst[:, kc * 2080 + h * 130: kc * 2080 + h * 130 + 130],
                                          start=(kc == 0), stop=(kc == nkc - 1),
                                          waits=(exp_toks + p.bank_use("o")
                                                 if kc == 0 else []),
                                          inc=("t_e" if kc == nkc - 1 else None))
                            tl = p.op("vector", "tensor_scalar_max", linv[:, 0:1],
                                      ps_o[:, 128:129], 1e-35, waits=[mm], inc="v_e")
                            tl = p.op("vector", "reciprocal", linv[:, 0:1], linv[:, 0:1],
                                      waits=[tl], inc="v_e")
                            avdone[par] = p.op("vector", "tensor_scalar",
                                 osb[:, qs * D + h * 128: qs * D + h * 128 + 128],
                                 ps_o[:, 0:128], linv[:, 0:1], None, ALU.mult,
                                 waits=[tl], inc="v_e")
                            p.bank_done("o", avdone[par])
                p.waitonly("vector", [Tok("t_e", p.led["t_e"]), Tok("s_e", p.led["s_e"])])
                sqq.close()
                with contextlib.ExitStack() as swo:
                    otb = swo.enter_context(sbt(nc, "otb", [128, 16 * 128], BF16))
                    wob = swo.enter_context(sbt(nc, "wob", [128, 2 * 2048], BF16))
                    resf = swo.enter_context(sbt(nc, "resf", [128, D], F32))
                    xnew = swo.enter_context(sbt(nc, "xnew", [128, D], BF16))
                    xres = swo.enter_context(sbt(nc, "xres", [128, D], F32))
                    woring = Ring(2)
                    xnew_prev = None
                    for qs in range(4):
                        tt = g * 4 + qs
                        xr = p.dma("gpsimd", xres[:, :],
                                   x_in[b, tt * 128:(tt + 1) * 128, :],
                                   waits=[Tok("s_e", p.led["s_e"])])
                        cps = []
                        osb_tok = Tok("v_e", p.led["v_e"])
                        for h in range(H):
                            key = "t" if h % 2 == 0 else "t2"
                            pst = ps_t if h % 2 == 0 else ps_t2
                            tp = p.op("tensor", "transpose", pst[:, 0:128],
                                      osb[:, qs * D + h * 128: qs * D + h * 128 + 128],
                                      ident_sb[:, :],
                                      waits=[osb_tok] + p.bank_use(key), inc="t_e")
                            cp = p.op("vector", "tensor_copy", otb[:, h * 128:(h + 1) * 128],
                                      pst[:, 0:128], waits=[tp], inc="v_e")
                            p.bank_done(key, cp)
                            cps.append(cp)
                        mmf = []
                        for dc in range(16):
                            wi, wfr = woring.acquire()
                            wt = p.dma("gpsimd", wob[:, wi * 2048:(wi + 1) * 2048],
                                       C["woT_bf"][dc * 128:(dc + 1) * 128, :],
                                       waits=wfr + C["wtok"]["wo"])
                            for cb in range(4):
                                psx = (ps_a, ps_b, ps_c, ps_d)[cb]
                                key = "abcd"[cb]
                                mm = p.op("tensor", "matmul", psx[:, 0:512],
                                          otb[:, dc * 128:(dc + 1) * 128],
                                          wob[:, wi * 2048 + cb * 512: wi * 2048 + cb * 512 + 512],
                                          start=(dc == 0), stop=(dc == 15),
                                          waits=([wt, cps[-1]] + p.bank_use(key)
                                                 if dc == 0 else [wt]),
                                          inc=("t_e" if (dc == 15 or cb == 3) else None))
                                if dc == 15:
                                    mmf.append(mm)
                            wring_tok = mm
                            woring.release(wi, wring_tok)
                        ra = []
                        for cb in range(4):
                            psx = (ps_a, ps_b, ps_c, ps_d)[cb]
                            t = p.op("vector", "tensor_add", resf[:, cb * 512:(cb + 1) * 512],
                                     psx[:, 0:512], xres[:, cb * 512:(cb + 1) * 512],
                                     waits=[mmf[cb], xr], inc="v_e")
                            p.bank_done("abcd"[cb], t)
                            ra.append(t)
                        emit_ln(p, C, resf[:, :], xnew[:, :],
                                waits=ra + ([xnew_prev] if xnew_prev else []))
                        xnew_prev = p.dma("sync", xN[b, tt * 128:(tt + 1) * 128, :],
                                          xnew[:, :], waits=[Tok("v_e", p.led["v_e"])])
            p.waitonly("gpsimd", [Tok("v_e", p.led["v_e"]), Tok("t_e", p.led["t_e"]),
                                  Tok("s_e", p.led["s_e"]), xnew_prev])
            p.waitonly("sync", [Tok("v_e", p.led["v_e"]), Tok("t_e", p.led["t_e"]),
                                Tok("s_e", p.led["s_e"])])
    p.fence()
    emit_xN_to_xT(p, C, b)



def emit_P3(p, C, b):
    nc = C["nc"]
    xT, xN = C["xT"], C["xN"]
    wgate, wup, hT, wdown_bf = C["wgate"], C["wup"], C["hT"], C["wdown_bf"]
    ident_sb = C["ident_sb"]
    ps_a, ps_b, ps_c, ps_d = C["ps_a"], C["ps_b"], C["ps_c"], C["ps_d"]
    ps_t, ps_t2 = C["ps_t"], C["ps_t2"]

    with contextlib.ExitStack() as sA:
        xtb = sA.enter_context(sbt(nc, "xtb", [128, 2 * 2048], BF16))
        wgs = sA.enter_context(sbt(nc, "wgs", [128, 2 * 2 * 16 * 512], BF16))
        wus = sA.enter_context(sbt(nc, "wus", [128, 2 * 2 * 16 * 512], BF16))
        hwrk = sA.enter_context(sbt(nc, "hwrk", [128, 8 * 512], BF16))
        pw = sA.enter_context(sbt(nc, "pw", [128, 16 * 128], BF16))
        hTblk = sA.enter_context(sbt(nc, "hTblk", [128, 8 * S], BF16))
        wring = Ring(2)
        xring = Ring(2)
        hT_prev = None
        t2_prev = a9_prev = tp_prev = None
        banks = (ps_a, ps_b, ps_c, ps_d)
        for cbg in range(8):
            wi, wfr = wring.acquire()
            woff = wi * 2 * 16 * 512
            wgt = p.dma("gpsimd", wgs[:, woff:woff + 16384]
                        .rearrange("p (c m) -> p c m", c=16),
                        C["wgate"][:, cbg * 1024:(cbg + 1) * 1024]
                        .rearrange("(c p) m -> p c m", p=128),
                        waits=wfr)
            wut = p.dma("gpsimd", wus[:, woff:woff + 16384]
                        .rearrange("p (c m) -> p c m", c=16),
                        C["wup"][:, cbg * 1024:(cbg + 1) * 1024]
                        .rearrange("(c p) m -> p c m", p=128))
            last_sp = None
            for tt in range(16):
                xi, xfr = xring.acquire()
                xoff = xi * 2048
                xw = p.dma("gpsimd", xtb[:, xoff:xoff + 2048]
                           .rearrange("p (c m) -> p c m", c=16),
                           xT[b, :, tt * 128:(tt + 1) * 128]
                           .rearrange("(c p) m -> p c m", p=128), waits=xfr)
                mmx = [None] * 4
                for ic in range(16):
                    lhs = xtb[:, xoff + ic * 128: xoff + ic * 128 + 128]
                    for half in range(2):
                        mmx[half] = p.op("tensor", "matmul", banks[half][:, 0:512],
                                   lhs,
                                   wgs[:, woff + ic * 1024 + half * 512:
                                       woff + ic * 1024 + half * 512 + 512],
                                   start=(ic == 0), stop=(ic == 15),
                                   waits=([xw, wgt] + p.bank_use("abcd"[half])
                                          if ic == 0 else []),
                                   inc=("t_e" if ic == 15 else None))
                        mmx[2 + half] = p.op("tensor", "matmul", banks[2 + half][:, 0:512],
                                   lhs,
                                   wus[:, woff + ic * 1024 + half * 512:
                                       woff + ic * 1024 + half * 512 + 512],
                                   start=(ic == 0), stop=(ic == 15),
                                   waits=([xw, wut] + p.bank_use("abcd"[2 + half])
                                          if ic == 0 else []),
                                   inc=("t_e" if ic == 15 else None))
                hg = hwrk[:, 0:1024]
                h = hwrk[:, 1024:2048]
                sq = hwrk[:, 2048:3072]
                msk = hwrk[:, 3072:4096]
                t1a = p.op("scalar", "activation", hg[:, 0:512], banks[0][:, 0:512],
                           AF.Silu, waits=[mmx[0]] + ([t2_prev] if t2_prev else []),
                           inc="s_e")
                t1b = p.op("scalar", "activation", hg[:, 512:1024], banks[1][:, 0:512],
                           AF.Silu, waits=[mmx[1]], inc="s_e")
                p.bank_done("a", t1a)
                p.bank_done("b", t1b)
                t2a = p.op("vector", "tensor_mul", h[:, 0:512], hg[:, 0:512],
                           banks[2][:, 0:512],
                           waits=[t1a, mmx[2]] + ([tp_prev] if tp_prev else []),
                           inc="v_e")
                t2 = p.op("vector", "tensor_mul", h[:, 512:1024], hg[:, 512:1024],
                          banks[3][:, 0:512], waits=[t1b, mmx[3]], inc="v_e")
                p.bank_done("c", t2a)
                p.bank_done("d", t2)
                t2_prev = t2
                t3 = p.op("vector", "tensor_mul", sq, h, h,
                          waits=[t2a, t2] + ([a9_prev] if a9_prev else []), inc="v_e")
                v4 = sq.rearrange("p (g f) -> p g f", f=4)
                m01 = pw[:, 0:256]; m23 = pw[:, 256:512]
                l01 = pw[:, 512:768]; l23 = pw[:, 768:1024]
                u1 = pw[:, 1024:1280]; u2 = pw[:, 1280:1536]; m2 = pw[:, 1536:1792]
                a1 = p.op("vector", "tensor_max", m01, v4[:, :, 0], v4[:, :, 1], waits=[t3], inc="v_e")
                a2 = p.op("vector", "tensor_max", m23, v4[:, :, 2], v4[:, :, 3], inc="v_e")
                a3 = p.op("vector", "tensor_tensor", l01, v4[:, :, 0], v4[:, :, 1], ALU.min, inc="v_e")
                a4 = p.op("vector", "tensor_tensor", l23, v4[:, :, 2], v4[:, :, 3], ALU.min, inc="v_e")
                a5 = p.op("vector", "tensor_tensor", u1, m01, m23, ALU.min, waits=[a1, a2], inc="v_e")
                a6 = p.op("vector", "tensor_max", u2, l01, l23, waits=[a3, a4], inc="v_e")
                a7 = p.op("vector", "tensor_max", m2, u1, u2, waits=[a5, a6], inc="v_e")
                a8 = p.op("vector", "tensor_tensor", msk.rearrange("p (g f) -> p g f", f=4),
                          v4, m2.rearrange("p (g o) -> p g o", o=1).broadcast_to([128, 256, 4]),
                          ALU.is_ge, waits=[a7], inc="v_e")
                a9 = p.op("vector", "tensor_mul", h, h, msk, waits=[a8], inc="v_e")
                a9_prev = a9
                for j in range(8):
                    key = "t" if j % 2 == 0 else "t2"
                    pst = ps_t if j % 2 == 0 else ps_t2
                    tp = p.op("tensor", "transpose", pst[:, 0:128],
                              h[:, j * 128:(j + 1) * 128], ident_sb[:, :],
                              waits=[a9] + p.bank_use(key), inc="t_e")
                    last_sp = p.op("vector", "tensor_copy",
                                   hTblk[:, j * S + tt * 128: j * S + tt * 128 + 128],
                                   pst[:, 0:128],
                                   waits=[tp] + ([hT_prev] if (tt == 0 and hT_prev) else []),
                                   inc="v_e")
                    p.bank_done(key, last_sp)
                    tp_prev = tp
                xring.release(xi, *mmx)
            st = p.dma("sync", hT[b, cbg * 1024:(cbg + 1) * 1024, :]
                       .rearrange("(c p) t -> p c t", p=128),
                       hTblk[:].rearrange("p (c t) -> p c t", c=8),
                       waits=[Tok("v_e", p.led["v_e"])])
            hT_prev = st
            wring.release(wi, st)
        p.fence()

    with contextlib.ExitStack() as sB:
        wd_sb = sB.enter_context(sbt(nc, "wd_sb", [128, 64 * 1024], BF16))
        ht_sb = sB.enter_context(sbt(nc, "ht_sb", [128, 2 * 64 * 128], BF16))
        resH = sB.enter_context(sbt(nc, "resH", [128, 8 * 1024], BF16))
        resf = sB.enter_context(sbt(nc, "resf", [128, D], F32))
        xrs = sB.enter_context(sbt(nc, "xrs", [128, D], BF16))
        xnew = sB.enter_context(sbt(nc, "xnew", [128, D], BF16))
        htring = Ring(2)
        xn_prev = None
        for th in range(2):
          for dh in range(2):
            wdt = p.dma("gpsimd", wd_sb[:].rearrange("p (c m) -> p c m", c=64),
                        wdown_bf[:, dh * 1024:(dh + 1) * 1024]
                        .rearrange("(c p) m -> p c m", p=128),
                        waits=[Tok("t_e", p.led.get("t_e", 0))] + C["wtok"]["wd"])
            for tl in range(8):
                tt = th * 8 + tl
                hti, htfr = htring.acquire()
                hoff = hti * 8192
                htok = p.dma("gpsimd",
                             ht_sb[:, hoff:hoff + 8192].rearrange("p (c m) -> p c m", c=64),
                             hT[b, :, tt * 128:(tt + 1) * 128]
                             .rearrange("(c p) m -> p c m", p=128),
                             waits=htfr)
                mm0 = mm1 = None
                for ic in range(64):
                    lhs = ht_sb[:, hoff + ic * 128: hoff + ic * 128 + 128]
                    mm0 = p.op("tensor", "matmul", ps_a[:, 0:512], lhs,
                               wd_sb[:, ic * 1024:ic * 1024 + 512],
                               start=(ic == 0), stop=(ic == 63),
                               waits=([wdt, htok] + p.bank_use("a") if ic == 0 else []),
                               inc=("t_e" if ic == 63 else None))
                    mm1 = p.op("tensor", "matmul", ps_b[:, 0:512], lhs,
                               wd_sb[:, ic * 1024 + 512:ic * 1024 + 1024],
                               start=(ic == 0), stop=(ic == 63),
                               waits=([wdt, htok] + p.bank_use("b") if ic == 0 else []),
                               inc=("t_e" if ic == 63 else None))
                if dh == 0:
                    e0 = p.op("vector", "tensor_copy", resH[:, tl * 1024:tl * 1024 + 512],
                              ps_a[:, 0:512], waits=[mm0], inc="v_e")
                    e1 = p.op("vector", "tensor_copy", resH[:, tl * 1024 + 512:(tl + 1) * 1024],
                              ps_b[:, 0:512], waits=[mm1], inc="v_e")
                    p.bank_done("a", e0)
                    p.bank_done("b", e1)
                else:
                    xr = p.dma("gpsimd", xrs[:, :], xN[b, tt * 128:(tt + 1) * 128, :],
                               waits=[Tok("v_e", p.led["v_e"]),
                                      Tok("s_e", p.led.get("s_e", 0))])
                    r0 = p.op("vector", "tensor_add", resf[:, 0:1024],
                              resH[:, tl * 1024:(tl + 1) * 1024], xrs[:, 0:1024],
                              waits=[xr] + ([xn_prev] if xn_prev else []), inc="v_e")
                    r1 = p.op("vector", "tensor_add", resf[:, 1024:1536],
                              ps_a[:, 0:512], xrs[:, 1024:1536], waits=[mm0, xr], inc="v_e")
                    r2 = p.op("vector", "tensor_add", resf[:, 1536:2048],
                              ps_b[:, 0:512], xrs[:, 1536:2048], waits=[mm1, xr], inc="v_e")
                    p.bank_done("a", r1)
                    p.bank_done("b", r2)
                    emit_ln(p, C, resf[:, :], xnew[:, :], waits=[r0, r1, r2])
                    xn_prev = p.dma("sync", xN[b, tt * 128:(tt + 1) * 128, :], xnew[:, :],
                                    waits=[Tok("v_e", p.led["v_e"])])
                htring.release(hti, mm0, mm1)
        p.fence()
    emit_xN_to_xT(p, C, b)


def emit_precast(p, C, pairs, first=True):
    if first:
        C["wtok"] = {}
    for (name, srcap, dstap, rows, cols) in pairs:
        toks = []
        for r0 in range(0, rows, 1024):
            h = min(1024, rows - r0)
            toks.append(p.dma("gpsimd", dstap[r0:r0 + h, :], srcap[r0:r0 + h, :]))
        C["wtok"][name] = toks



def emit_elu(p, C, dst, src_ps, bias_col, tmp1, tmp2, waits=()):
    ev = p.op("scalar", "activation", tmp1, src_ps, AF.Identity, bias=bias_col,
              waits=list(waits), inc="s_e")
    t1 = p.op("vector", "tensor_scalar_min", tmp2, tmp1, 0.0, waits=[ev], inc="v_e")
    t2 = p.op("scalar", "activation", tmp2, tmp2, AF.Exp, waits=[t1], inc="s_e")
    t3 = p.op("vector", "tensor_scalar_max", tmp1, tmp1, 0.0, waits=[t2], inc="v_e")
    t4 = p.op("vector", "scalar_tensor_tensor", dst, tmp2, -1.0, tmp1,
              ALU.add, ALU.add, waits=[t3], inc="v_e")
    return t4


def emit_P4(p, C, b):
    nc = C["nc"]
    xT, xN = C["xT"], C["xN"]
    Wqk, Wv, Wu, Wo1 = C["Wqk"], C["Wv"], C["Wu"], C["Wo1"]
    bqk_t, bv_r, bu_r, bo_r, lng_r, lnb_r = (C["bqk_t"], C["bv"], C["bu"],
                                             C["bo1"], C["ln_g"], C["ln_b"])
    ident_sb, ones_sb = C["ident_sb"], C["ones_sb"]
    ps_a, ps_b, ps_c, ps_d = C["ps_a"], C["ps_b"], C["ps_c"], C["ps_d"]
    ps_o, ps_s, ps_t, ps_t2 = C["ps_o"], C["ps_s"], C["ps_t"], C["ps_t2"]

    with contextlib.ExitStack() as sco:
        QT = sco.enter_context(sbt(nc, "QT", [128, 16 * S], BF16))
        uN = C["uN"]
        kv_sb = sco.enter_context(sbt(nc, "kv_sb", [128, 16 * 128], BF16))
        browb = sco.enter_context(sbt(nc, "browb", [1, 4 * D], BF16))
        bqk_sb = sco.enter_context(sbt(nc, "bqk_sb", [128, 16], F32))
        lngb = sco.enter_context(sbt(nc, "lngb", [128, 2 * D], BF16))
        browstack = contextlib.ExitStack()
        brow = browstack.enter_context(sbt(nc, "brow", [1, D], BF16))
        bq_t = p.dma("gpsimd", bqk_sb[:, :], bqk_t[:, :])
        br_t = p.dma("gpsimd", browb[0:1, 0:D], bv_r[None, :])
        br_u = p.dma("gpsimd", browb[0:1, D:2 * D], bu_r[None, :])
        br_o = p.dma("gpsimd", browb[0:1, 2 * D:3 * D], bo_r[None, :])
        br_g = p.dma("gpsimd", browb[0:1, 3 * D:4 * D], lng_r[None, :])
        br_b = p.dma("gpsimd", brow[0:1, 0:D], lnb_r[None, :])
        for (srow, dst0, wt0) in ((browb[0:1, 3 * D:4 * D], 0, br_g),
                                  (brow[0:1, 0:D], D, br_b)):
            for cb in range(4):
                m1 = p.op("tensor", "matmul", ps_s[:, 0:512], ones_sb[0:1, :],
                          srow[0:1, cb * 512:(cb + 1) * 512],
                          start=True, stop=True,
                          waits=[wt0] + p.bank_use("s"), inc="t_e")
                cpv = p.op("vector", "tensor_copy", lngb[:, dst0 + cb * 512: dst0 + cb * 512 + 512],
                     ps_s[:, 0:512], waits=[m1], inc="v_e")
                p.bank_done("s", cpv)

        p.fence()
        browstack.close()
        with contextlib.ExitStack() as sA:
            xg = sA.enter_context(sbt(nc, "xgA", [128, 16 * S], BF16))
            wqkb = sA.enter_context(sbt(nc, "wqkb", [128, 2 * 16 * 128], BF16))
            tA = sA.enter_context(sbt(nc, "tA", [128, 2 * 512], BF16))
            tB = sA.enter_context(sbt(nc, "tB", [128, 2 * 512], BF16))
            wring = Ring(2)
            e4_prev = {0: None, 1: None}
            xw = p.dma("gpsimd", xg[:].rearrange("p (c m) -> p c m", c=16),
                       xT[b].rearrange("(c p) m -> p c m", p=128),
                       waits=[Tok("v_e", p.led["v_e"])])
            for oc in range(16):
                wi, wfr = wring.acquire()
                woff = wi * 2048
                wt = p.dma("gpsimd", wqkb[:, woff:woff + 2048].rearrange("p (c m) -> p c m", c=16),
                           C["Wqk_bf"][:, oc * 128:(oc + 1) * 128].rearrange("(c p) m -> p c m", p=128),
                           waits=wfr + C["wtok"]["Wqk"])
                banks = (ps_a, ps_b, ps_c, ps_d)
                mmx = [None] * 4
                for ic in range(16):
                    for gt in range(4):
                        mmx[gt] = p.op("tensor", "matmul", banks[gt][:, 0:512],
                                  wqkb[:, woff + ic * 128: woff + ic * 128 + 128],
                                  xg[:, ic * S + gt * 512: ic * S + gt * 512 + 512],
                                  start=(ic == 0), stop=(ic == 15),
                                  waits=([wt, xw] + p.bank_use("abcd"[gt])
                                         if ic == 0 else []),
                                  inc=("t_e" if ic == 15 else None))
                e4s = []
                for gt in range(4):
                    par = gt % 2
                    e4 = emit_elu(p, C, QT[:, oc * S + gt * 512: oc * S + gt * 512 + 512],
                                  banks[gt][:, 0:512], bqk_sb[:, oc:oc + 1],
                                  tA[:, par * 512:(par + 1) * 512],
                                  tB[:, par * 512:(par + 1) * 512],
                                  waits=[mmx[gt], bq_t] +
                                        ([e4_prev[par]] if e4_prev[par] else []))
                    e4_prev[par] = e4
                    p.bank_done("abcd"[gt], e4)
                    e4s.append(e4)
                wring.release(wi, *e4s)
            p.fence()

        with contextlib.ExitStack() as sB:
            xt2 = sB.enter_context(sbt(nc, "xt2b", [128, 2 * 16 * 128], BF16))
            wub = sB.enter_context(sbt(nc, "wub", [128, 16 * 2048], BF16))
            untile = sB.enter_context(sbt(nc, "untile", [128, D], BF16))
            wt = p.dma("gpsimd", wub[:].rearrange("p (c m) -> p c m", c=16),
                       C["Wu_bf"].rearrange("(c p) m -> p c m", p=128),
                       waits=C["wtok"]["Wu"])
            xring = Ring(2)
            un_prev = None
            for tt in range(16):
                xi, xfr = xring.acquire()
                xoff = xi * 2048
                xw = p.dma("gpsimd", xt2[:, xoff:xoff + 2048].rearrange("p (c m) -> p c m", c=16),
                           xT[b, :, tt * 128:(tt + 1) * 128].rearrange("(c p) m -> p c m", p=128),
                           waits=xfr)
                mmf = []
                for ic in range(16):
                    woff = ic * 2048
                    for cb in range(4):
                        psx = (ps_a, ps_b, ps_c, ps_d)[cb]
                        mm = p.op("tensor", "matmul", psx[:, 0:512],
                                  xt2[:, xoff + ic * 128: xoff + ic * 128 + 128],
                                  wub[:, woff + cb * 512: woff + cb * 512 + 512],
                                  start=(ic == 0), stop=False,
                                  waits=([wt, xw] + p.bank_use("abcd"[cb])
                                         if ic == 0 else []),
                                  inc=("t_e" if cb == 3 else None))
                        if ic == 15:
                            mmf.append(mm)
                evs = []
                for cb in range(4):
                    psx = (ps_a, ps_b, ps_c, ps_d)[cb]
                    mb = p.op("tensor", "matmul", psx[:, 0:512], ones_sb[0:1, :],
                              browb[0:1, D + cb * 512: D + cb * 512 + 512],
                              start=False, stop=True, waits=[br_u], inc="t_e")
                    ev = p.op("scalar", "activation",
                              untile[:, cb * 512:(cb + 1) * 512],
                              psx[:, 0:512], AF.Silu,
                              waits=[mb] + ([un_prev] if (cb == 0 and un_prev) else []),
                              inc="s_e")
                    p.bank_done("abcd"[cb], ev)
                    evs.append(ev)
                un_prev = p.dma("sync", uN[b, tt * 128:(tt + 1) * 128, :], untile[:, :],
                                waits=evs)
                xring.release(xi, *evs)
            p.fence()

        with contextlib.ExitStack() as sC:
            xt3 = sC.enter_context(sbt(nc, "xt3", [128, 2 * 16 * 128], BF16))
            wvb = sC.enter_context(sbt(nc, "wvb", [128, 16 * 2048], BF16))
            vt2 = sC.enter_context(sbt(nc, "vt", [128, 2 * D], BF16))
            kt2 = sC.enter_context(sbt(nc, "kt", [128, 2 * D], BF16))
            wvt = p.dma("gpsimd", wvb[:].rearrange("p (c m) -> p c m", c=16),
                        C["Wv_bf"].rearrange("(c p) m -> p c m", p=128),
                        waits=C["wtok"]["Wv"])
            xring = Ring(2)
            kvps = (ps_a, ps_b, ps_c, ps_d)
            for tt in range(16):
                vt = vt2[:, (tt % 2) * D:(tt % 2) * D + D]
                kt = kt2[:, (tt % 2) * D:(tt % 2) * D + D]
                xi, xfr = xring.acquire()
                xoff = xi * 2048
                xw = p.dma("gpsimd", xt3[:, xoff:xoff + 2048].rearrange("p (c m) -> p c m", c=16),
                           xT[b, :, tt * 128:(tt + 1) * 128].rearrange("(c p) m -> p c m", p=128),
                           waits=xfr)
                evs = []
                for half in range(2):
                    mm = None
                    for ic in range(16):
                        woff = ic * 2048 + half * 1024
                        for cb in range(2):
                            psx = (ps_o, ps_s)[cb]
                            key = ("o", "s")[cb]
                            mm = p.op("tensor", "matmul", psx[:, 0:512],
                                      xt3[:, xoff + ic * 128: xoff + ic * 128 + 128],
                                      wvb[:, woff + cb * 512: woff + cb * 512 + 512],
                                      start=(ic == 0), stop=False,
                                      waits=([wvt, xw] + p.bank_use(key)
                                             if ic == 0 else []),
                                      inc=("t_e" if cb == 1 else None))
                    for cb in range(2):
                        psx = (ps_o, ps_s)[cb]
                        mb = p.op("tensor", "matmul", psx[:, 0:512], ones_sb[0:1, :],
                                  browb[0:1, half * 1024 + cb * 512: half * 1024 + cb * 512 + 512],
                                  start=False, stop=True, waits=[br_t], inc="t_e")
                        ev = p.op("scalar", "activation",
                                  vt[:, half * 1024 + cb * 512: half * 1024 + cb * 512 + 512],
                                  psx[:, 0:512], AF.Silu, waits=[mb], inc="s_e")
                        p.bank_done(("o", "s")[cb], ev)
                        evs.append(ev)
                cps = []
                for dc in range(16):
                    key = "t" if dc % 2 == 0 else "t2"
                    pst = ps_t if dc % 2 == 0 else ps_t2
                    tp = p.op("tensor", "transpose", pst[:, 0:128],
                              QT[:, dc * S + tt * 128: dc * S + tt * 128 + 128],
                              ident_sb[:, :], waits=p.bank_use(key), inc="t_e")
                    cp = p.op("vector", "tensor_copy", kt[:, dc * 128:(dc + 1) * 128],
                              pst[:, 0:128], waits=[tp], inc="v_e")
                    p.bank_done(key, cp)
                    cps.append(cp)
                for h in range(H):
                    psx = kvps[h // 4]
                    co = (h % 4) * 128
                    bw = p.bank_use("abcd"[h // 4]) if (tt == 0 and h % 4 == 0) else []
                    p.op("tensor", "matmul", psx[:, co:co + 128],
                         kt[:, h * 128:(h + 1) * 128], vt[:, h * 128:(h + 1) * 128],
                         start=(tt == 0 and h % 4 == 0), stop=(tt == 15 and h % 4 == 3),
                         skip_group_check=True,
                         waits=[cps[h], evs[h // 4]] + bw,
                         inc=("t_e" if tt == 15 else None))
                xring.release(xi, *cps)
            for h in range(H):
                psx = kvps[h // 4]
                co = (h % 4) * 128
                cpv = p.op("vector", "tensor_copy", kv_sb[:, h * 128:(h + 1) * 128],
                     psx[:, co:co + 128],
                     waits=[Tok("t_e", p.led["t_e"])], inc="v_e")
                if h % 4 == 3:
                    p.bank_done("abcd"[h // 4], cpv)
            p.fence()

        with contextlib.ExitStack() as sD:
            oTg = sD.enter_context(sbt(nc, "oTg", [128, 16 * 256], BF16))
            oN2 = sD.enter_context(sbt(nc, "oN", [128, 2 * D], BF16))
            o2T2 = sD.enter_context(sbt(nc, "o2T", [128, 2 * 16 * 128], BF16))
            wob = sD.enter_context(sbt(nc, "wob4", [128, 16 * 2048], BF16))
            resf = sD.enter_context(sbt(nc, "resf4", [128, D], F32))
            lnsc = resf
            wot = p.dma("gpsimd", wob[:].rearrange("p (c m) -> p c m", c=16),
                        C["Wo1_bf"].rearrange("(c p) m -> p c m", p=128),
                        waits=C["wtok"]["Wo1"])
            xrs = sD.enter_context(sbt(nc, "xrs4", [128, D], BF16))
            xnew = sD.enter_context(sbt(nc, "xnew4", [128, D], BF16))
            xn_prev = None
            for gt in range(8):
                oTg_free = Tok("t_e", p.led["t_e"])
                for h in range(H):
                    key = "o" if h % 2 == 0 else "s"
                    ps = ps_o if h % 2 == 0 else ps_s
                    mm = p.op("tensor", "matmul", ps[:, 0:256],
                              kv_sb[:, h * 128:(h + 1) * 128],
                              QT[:, h * S + gt * 256: h * S + gt * 256 + 256],
                              start=True, stop=True,
                              waits=p.bank_use(key) + [oTg_free],
                              inc="t_e")
                    cpv = p.op("vector", "tensor_copy", oTg[:, h * 256:(h + 1) * 256],
                         ps[:, 0:256], waits=[mm], inc="v_e")
                    p.bank_done(key, cpv)
                for j in range(2):
                    tt = gt * 2 + j
                    oN = oN2[:, (tt % 2) * D:(tt % 2) * D + D]
                    o2T = o2T2[:, (tt % 2) * D:(tt % 2) * D + D]
                    cps = []
                    oTg_tok = Tok("v_e", p.led["v_e"])
                    for dc in range(16):
                        key = "t" if dc % 2 == 0 else "t2"
                        pst = ps_t if dc % 2 == 0 else ps_t2
                        tp = p.op("tensor", "transpose", pst[:, 0:128],
                                  oTg[:, dc * 256 + j * 128: dc * 256 + j * 128 + 128],
                                  ident_sb[:, :], waits=[oTg_tok] + p.bank_use(key),
                                  inc="t_e")
                        cp = p.op("vector", "tensor_copy", oN[:, dc * 128:(dc + 1) * 128],
                                  pst[:, 0:128], waits=[tp], inc="v_e")
                        p.bank_done(key, cp)
                        cps.append(cp)
                    stat = C["stat"]
                    s1, s2, mu, rst = stat[:, 0:1], stat[:, 1:2], stat[:, 2:3], stat[:, 3:4]
                    t1 = p.op("vector", "tensor_reduce", s1, oN[:, :], AX.X, ALU.add,
                              waits=cps, inc="v_e")
                    t2 = p.op("scalar", "activation", o2T[:, :], oN[:, :], AF.Square,
                              accum_out=s2, waits=cps +
                              [Tok("t_e", p.led["t_e"])], inc="s_e")
                    tm = p.op("vector", "tensor_scalar", mu, s1, 1.0 / D, None, ALU.mult,
                              waits=[t1], inc="v_e")
                    tv = p.op("vector", "tensor_mul", stat[:, 4:5], mu, mu, waits=[tm], inc="v_e")
                    tv = p.op("vector", "tensor_scalar", stat[:, 4:5], stat[:, 4:5], LN_EPS,
                              None, ALU.subtract, waits=[tv], inc="v_e")
                    tv = p.op("vector", "scalar_tensor_tensor", stat[:, 5:6], s2, 1.0 / D,
                              stat[:, 4:5], ALU.mult, ALU.subtract, waits=[tv, t2], inc="v_e")
                    tv = newton_rsqrt(p, C, rst, stat[:, 5:6], 1, waits=[tv])
                    ta = p.op("vector", "scalar_tensor_tensor", lnsc[:, :], oN[:, :], mu,
                              lngb[:, 0:D], ALU.subtract, ALU.mult,
                              waits=[tv] + ([xn_prev] if xn_prev else []), inc="v_e")
                    tg2 = p.op("vector", "scalar_tensor_tensor", lnsc[:, :], lnsc[:, :], rst,
                              lngb[:, D:2 * D], ALU.mult, ALU.add,
                              waits=[ta], inc="v_e")
                    ut = p.dma("gpsimd", xrs[:, :], uN[b, tt * 128:(tt + 1) * 128, :],
                               waits=[Tok("v_e", p.led["v_e"])])
                    tu = p.op("vector", "tensor_mul", oN[:, :], lnsc[:, :], xrs[:, :],
                              waits=[tg2, ut], inc="v_e")
                    cps2 = []
                    for dc in range(16):
                        key = "t" if dc % 2 == 0 else "t2"
                        pst = ps_t if dc % 2 == 0 else ps_t2
                        tp = p.op("tensor", "transpose", pst[:, 0:128],
                                  oN[:, dc * 128:(dc + 1) * 128], ident_sb[:, :],
                                  waits=[tu] + p.bank_use(key), inc="t_e")
                        cp = p.op("vector", "tensor_copy", o2T[:, dc * 128:(dc + 1) * 128],
                                  pst[:, 0:128], waits=[tp], inc="v_e")
                        p.bank_done(key, cp)
                        cps2.append(cp)
                    xr = p.dma("gpsimd", xrs[:, :], xN[b, tt * 128:(tt + 1) * 128, :],
                               waits=[tu, Tok("v_e", p.led["v_e"])])
                    mmf = []
                    for dc in range(16):
                        woff = dc * 2048
                        for cb in range(4):
                            psx = (ps_a, ps_b, ps_c, ps_d)[cb]
                            mm = p.op("tensor", "matmul", psx[:, 0:512],
                                      o2T[:, dc * 128:(dc + 1) * 128],
                                      wob[:, woff + cb * 512: woff + cb * 512 + 512],
                                      start=(dc == 0), stop=False,
                                      waits=([wot, cps2[-1]] + p.bank_use("abcd"[cb])
                                             if dc == 0 else []),
                                      inc=("t_e" if cb == 3 else None))
                    ra = []
                    for cb in range(4):
                        psx = (ps_a, ps_b, ps_c, ps_d)[cb]
                        mb = p.op("tensor", "matmul", psx[:, 0:512], ones_sb[0:1, :],
                                  browb[0:1, 2 * D + cb * 512: 2 * D + cb * 512 + 512],
                                  start=False, stop=True, waits=[br_o], inc="t_e")
                        t = p.op("vector", "tensor_add", resf[:, cb * 512:(cb + 1) * 512],
                                 psx[:, 0:512], xrs[:, cb * 512:(cb + 1) * 512],
                                 waits=[mb, xr], inc="v_e")
                        p.bank_done("abcd"[cb], t)
                        ra.append(t)
                    emit_ln(p, C, resf[:, :], xnew[:, :],
                            waits=ra + ([xn_prev] if xn_prev else []))
                    xn_prev = p.dma("sync", xN[b, tt * 128:(tt + 1) * 128, :], xnew[:, :],
                                    waits=[Tok("v_e", p.led["v_e"])])
            p.fence()
    emit_xN_to_xT(p, C, b)



def emit_P5(p, C, moe_sched):
    nc = C["nc"]
    xT, xN, y_out = C["xT"], C["xN"], C["y_out"]
    gwT, WT = C["gwT"], C["WT"]
    atoms_p, imp_p, midx, eaw_r = C["atoms_p"], C["imp_p"], C["midx_s"], C["eaw_r"]
    o3T = C["o3T"]
    ident_sb, ones_sb = C["ident_sb"], C["ones_sb"]
    iota_rc_c, iota8_c = C["iota_rc_c"], C["iota8_c"]
    ps_a, ps_b, ps_c, ps_d = C["ps_a"], C["ps_b"], C["ps_c"], C["ps_d"]
    ps_o, ps_s, ps_t, ps_t2 = C["ps_o"], C["ps_s"], C["ps_t"], C["ps_t2"]
    sched, nslots = moe_sched["sched"], moe_sched["nslots"]

    with contextlib.ExitStack() as sco:
        wsb = sco.enter_context(sbt(nc, "wsb", [8, 16384], BF16))
        midx_f = sco.enter_context(sbt(nc, "midxf", [128, 128], F32))
        rows_f = sco.enter_context(sbt(nc, "rowsf", [128, 128], F32))
        cols_i = sco.enter_context(sbt(nc, "colsi", [128, 128], I32))
        iota_rc = sco.enter_context(sbt(nc, "iotarc", [128, 128], F32))
        rsh = sco.enter_context(sbt(nc, "rsh", [128, 1], F32))
        iota8 = sco.enter_context(sbt(nc, "iota8", [128, 8], F32))
        gw_sb = sco.enter_context(sbt(nc, "gw_sb", [128, 16 * 8], BF16))

        it1 = p.dma("sync", iota_rc[:, :], iota_rc_c.ap()[:, :])
        it2 = p.dma("sync", iota8[:, :], iota8_c.ap()[:, :])
        gwt = p.dma("gpsimd", gw_sb[:].rearrange("p (c m) -> p c m", c=16),
                    gwT.rearrange("(c p) m -> p c m", p=128))

        with contextlib.ExitStack() as spro:
            smw = spro.enter_context(sbt(nc, "smw", [8, 8], F32))
            e8 = spro.enter_context(sbt(nc, "e8", [8, 64], F32))
            e8b = spro.enter_context(sbt(nc, "e8b", [8, 64], BF16))
            smT = spro.enter_context(sbt(nc, "smT", [64, 8], BF16))
            atomb = spro.enter_context(sbt(nc, "atomb", [64, 2048], BF16))
            impb = spro.enter_context(sbt(nc, "impb", [8, 2048], F32))
            lt = p.dma("gpsimd", e8[:, :], eaw_r[:, :])
            t1 = p.op("vector", "tensor_reduce", smw[0:8, 0:1], e8[:, :], AX.X, ALU.max,
                      negate=True, waits=[lt], inc="v_e")
            t2 = p.op("scalar", "activation", e8b[:, :], e8[:, :], AF.Exp,
                      bias=smw[0:8, 0:1], accum_out=smw[0:8, 1:2], waits=[t1], inc="s_e")
            t3 = p.op("vector", "reciprocal", smw[0:8, 2:3], smw[0:8, 1:2], waits=[t2], inc="v_e")
            t4 = p.op("vector", "tensor_scalar", e8b[:, :], e8b[:, :], smw[0:8, 2:3], None,
                      ALU.mult, waits=[t3], inc="v_e")
            tmm = p.op("tensor", "transpose", ps_t[0:64, 0:8], e8b[:, :], ident_sb[0:8, 0:8],
                       waits=[t4, Tok("v_e", p.led["v_e"])], inc="t_e")
            tcp = p.op("vector", "tensor_copy", smT[:, :], ps_t[0:64, 0:8], waits=[tmm], inc="v_e")
            for blk in range(8):
                at = p.dma("gpsimd", atomb[:, :], atoms_p[:, blk * 2048:(blk + 1) * 2048],
                           waits=[Tok("v_e", p.led["v_e"])])
                imt = p.dma("gpsimd", impb[:, :], imp_p[:, blk * 2048:(blk + 1) * 2048],
                            waits=[Tok("s_e", p.led["s_e"])])
                for q4 in range(4):
                    mm = p.op("tensor", "matmul", ps_o[0:8, 0:512], smT[:, :],
                              atomb[:, q4 * 512:(q4 + 1) * 512], start=True, stop=True,
                              waits=[at, tcp] + p.bank_use("o"), inc="t_e")
                    sg = p.op("scalar", "activation", impb[0:8, q4 * 512:(q4 + 1) * 512],
                              impb[0:8, q4 * 512:(q4 + 1) * 512], AF.Sigmoid,
                              waits=[imt], inc="s_e")
                    wv = p.op("vector", "tensor_mul",
                         wsb[:, blk * 2048 + q4 * 512: blk * 2048 + q4 * 512 + 512],
                         ps_o[0:8, 0:512], impb[0:8, q4 * 512:(q4 + 1) * 512],
                         waits=[mm, sg], inc="v_e")
                    p.bank_done("o", wv)
            mi = spro.enter_context(sbt(nc, "mi", [128, 128], I32))
            mi2 = spro.enter_context(sbt(nc, "mi2", [128, 128], I32))
            mt = p.dma("gpsimd", mi[:, :], midx[:, :])
            c1 = p.op("vector", "tensor_single_scalar", mi2[:, :], mi[:, :], 11,
                      ALU.arith_shift_right, waits=[mt], inc="v_e")
            c2 = p.op("vector", "tensor_copy", rows_f[:, :], mi2[:, :], waits=[c1], inc="v_e")
            p.op("vector", "tensor_single_scalar", cols_i[:, :], mi[:, :], 2047,
                 ALU.bitwise_and, waits=[c2], inc="v_e")
            p.fence()

        xt_all = sco.enter_context(sbt(nc, "xt_all5", [128, 16 * S], BF16))
        coefT = sco.enter_context(sbt(nc, "coefT", [8, S], BF16))
        xgv = sco.enter_context(sbt(nc, "xgv", [128, nslots * S], BF16))
        xg1 = sco.enter_context(sbt(nc, "xg1", [128, 2 * S], BF16))
        max_nk = max((len(s) for s in sched), default=1)
        rh = sco.enter_context(sbt(nc, "rh", [128, max(4, max_nk) * 128], BF16))
        wtb = sco.enter_context(sbt(nc, "wtb", [128, 2 * 16 * 128], BF16))
        cwk = sco.enter_context(sbt(nc, "cwk", [128, 48], F32))
        cwku = sco.enter_context(sbt(nc, "cwku", [128, 8], mybir.dt.uint32))
        coefb = sco.enter_context(sbt(nc, "coefb", [128, 8], BF16))
        otile = sco.enter_context(sbt(nc, "otile", [128, 512], BF16))
        resf = sco.enter_context(sbt(nc, "resf5", [128, 2 * D], F32))
        onb = sco.enter_context(sbt(nc, "onb", [128, D], BF16))
        xrs = sco.enter_context(sbt(nc, "xrs5", [128, 2 * D], BF16))
        yout_f = resf

        for b in range(NB):
            xw = p.dma("gpsimd", xt_all[:].rearrange("p (c m) -> p c m", c=16),
                       xT[b].rearrange("(c p) m -> p c m", p=128),
                       waits=[Tok("v_e", p.led["v_e"]), Tok("t_e", p.led["t_e"])])
            e3_prev = None
            for tt in range(16):
                mm = None
                for ic in range(16):
                    mm = p.op("tensor", "matmul", ps_o[:, 0:8],
                              xt_all[:, ic * S + tt * 128: ic * S + tt * 128 + 128],
                              gw_sb[:, ic * 8:(ic + 1) * 8],
                              start=(ic == 0), stop=(ic == 15),
                              waits=([xw, gwt, it2] + p.bank_use("o") if ic == 0 else []),
                              inc=("t_e" if ic == 15 else None))
                g0 = p.op("vector", "tensor_scalar_min", cwk[:, 0:8], ps_o[:, 0:8],
                          50.0, waits=[mm] + ([e3_prev] if e3_prev else []), inc="v_e")
                p.bank_done("o", g0)
                g1 = p.op("vector", "tensor_scalar_max", cwk[:, 0:8], cwk[:, 0:8],
                          -50.0, waits=[g0], inc="v_e")
                g2 = p.op("vector", "max", cwk[:, 8:16], cwk[:, 0:8], waits=[g1], inc="v_e")
                g3 = p.op("vector", "max_index", cwku[:, 0:8], cwk[:, 8:16], cwk[:, 0:8],
                          waits=[g2], inc="v_e")
                g4 = p.op("vector", "tensor_sub", cwk[:, 16:17], cwk[:, 9:10], cwk[:, 8:9],
                          waits=[g3], inc="v_e")
                g5 = p.op("scalar", "activation", cwk[:, 17:18], cwk[:, 16:17], AF.Exp,
                          waits=[g4], inc="s_e")
                g6 = p.op("vector", "tensor_scalar", cwk[:, 18:19], cwk[:, 17:18], 1.0, None,
                          ALU.add, waits=[g5], inc="v_e")
                g7 = p.op("vector", "reciprocal", cwk[:, 19:20], cwk[:, 18:19],
                          waits=[g6], inc="v_e")
                g8 = p.op("vector", "tensor_scalar", cwk[:, 20:21], cwk[:, 19:20], -1.0, 1.0,
                          ALU.mult, ALU.add, waits=[g7], inc="v_e")
                g9 = p.op("vector", "tensor_copy", cwk[:, 21:22], cwku[:, 0:1],
                          waits=[g3], inc="v_e")
                g10 = p.op("vector", "tensor_copy", cwk[:, 22:23], cwku[:, 1:2],
                           waits=[g3], inc="v_e")
                e1 = p.op("vector", "tensor_scalar", cwk[:, 24:32], iota8[:, :],
                          cwk[:, 21:22], None, ALU.is_equal, waits=[g9], inc="v_e")
                e1b = p.op("vector", "tensor_scalar", cwk[:, 32:40], cwk[:, 24:32],
                           cwk[:, 19:20], None, ALU.mult, waits=[e1, g7], inc="v_e")
                e2 = p.op("vector", "tensor_scalar", cwk[:, 24:32], iota8[:, :],
                          cwk[:, 22:23], None, ALU.is_equal, waits=[g10, e1b], inc="v_e")
                e3 = p.op("vector", "scalar_tensor_tensor", coefb[:, :], cwk[:, 24:32],
                          cwk[:, 20:21], cwk[:, 32:40], ALU.mult, ALU.add,
                          waits=[e2, g8], inc="v_e")
                e3_prev = e3
                tmc = p.op("tensor", "transpose", ps_t[0:8, 0:128], coefb[:, :],
                           ident_sb[:, :], waits=[e3] + p.bank_use("t"), inc="t_e")
                ctc = p.op("vector", "tensor_copy", coefT[:, tt * 128:(tt + 1) * 128],
                     ps_t[0:8, 0:128], waits=[tmc], inc="v_e")
                p.bank_done("t", ctc)
            computed = {}
            slot_readers = {}
            st_toks = []
            xg1ring = Ring(2)
            wring = Ring(2)
            st_prev = None
            for rc in range(16):
                for k in sched[rc]:
                    if k in computed:
                        continue
                    slot = k % nslots
                    xgi, xgfr = xg1ring.acquire()
                    gt = p.dma("gpsimd", xg1[:, xgi * S:(xgi + 1) * S], C["xT_full"],
                               waits=xgfr + slot_readers.pop(slot, []),
                               indirect_in=cols_i[:, k:k + 1], indirect_axis=1,
                               element_offset=b * D * S)
                    vv = []
                    for tbp in range(4):
                        vkey = "o" if tbp % 2 == 0 else "s"
                        vps = ps_o if tbp % 2 == 0 else ps_s
                        mmv = p.op("tensor", "matmul", vps[:, 0:512],
                                   wsb[:, k * 128:(k + 1) * 128],
                                   coefT[:, tbp * 512:(tbp + 1) * 512],
                                   start=True, stop=True,
                                   waits=p.bank_use(vkey), inc="t_e")
                        xv = p.op("vector", "tensor_mul",
                                  xgv[:, slot * S + tbp * 512: slot * S + tbp * 512 + 512],
                                  xg1[:, xgi * S + tbp * 512: xgi * S + tbp * 512 + 512],
                                  vps[:, 0:512],
                                  waits=[mmv, gt], inc="v_e")
                        p.bank_done(vkey, xv)
                        vv.append(xv)
                    computed[k] = vv[-1]
                    xg1ring.release(xgi, *vv)
                rhtoks = []
                rsp = None
                for (j, k) in enumerate(sched[rc]):
                    rsp = p.op("vector", "tensor_single_scalar", rsh[:, 0:1],
                               rows_f[:, k:k + 1], float(rc * 128), ALU.subtract,
                               waits=[Tok("t_e", p.led["t_e"])] +
                                     ([rhtoks[-1]] if rhtoks else []),
                               inc="v_e")
                    rhtoks.append(p.op("vector", "tensor_scalar",
                                       rh[:, j * 128:(j + 1) * 128],
                                       iota_rc[:, :],
                                       rsh[:, 0:1], None, ALU.is_equal,
                                       waits=[it1, rsp], inc="v_e"))
                wi, wfr = wring.acquire()
                woff = wi * 2048
                wt = p.dma("gpsimd", wtb[:, woff:woff + 2048].rearrange("p (c m) -> p c m", c=16),
                           C["WT_bf"][:, rc * 128:(rc + 1) * 128].rearrange("(c p) m -> p c m", p=128),
                           waits=wfr + C["wtok"]["WT"])
                lastmm = []
                for tb in range(4):
                    psx = (ps_a, ps_b, ps_c, ps_d)[tb]
                    mm = None
                    nk = len(sched[rc])
                    for ic in range(16):
                        mm = p.op("tensor", "matmul", psx[:, 0:512],
                                  wtb[:, woff + ic * 128: woff + ic * 128 + 128],
                                  xt_all[:, ic * S + tb * 512: ic * S + tb * 512 + 512],
                                  start=(ic == 0), stop=(ic == 15 and nk == 0),
                                  waits=([wt, xw] + p.bank_use("abcd"[tb]) if ic == 0 else []),
                                  inc=("t_e" if (ic == 15 and nk == 0) else None))
                    for (j, k) in enumerate(sched[rc]):
                        slot = k % nslots
                        mm = p.op("tensor", "matmul", psx[:, 0:512],
                                  rh[:, j * 128:(j + 1) * 128],
                                  xgv[:, slot * S + tb * 512: slot * S + tb * 512 + 512],
                                  start=False, stop=(j == nk - 1),
                                  waits=[rhtoks[j], computed[k]],
                                  inc="t_e")
                        slot_readers.setdefault(slot, []).append(mm)
                    ev = p.op("vector", "tensor_copy", otile[:, 0:512], psx[:, 0:512],
                              waits=[mm] + ([st_prev] if st_prev else []),
                              inc="v_e")
                    p.bank_done("abcd"[tb], ev)
                    st_prev = p.dma("sync", o3T[rc * 128:(rc + 1) * 128, tb * 512:(tb + 1) * 512],
                                    otile[:, 0:512], waits=[ev])
                    st_toks.append(st_prev)
                    lastmm.append(mm)
                wring.release(wi, *lastmm)
            o3_done = list(st_toks)
            stat = C["stat"]
            fin_prev = {}
            for tt in range(16):
                par = tt % 2
                onbS = onb[:, :]
                xrsS = xrs[:, par * D:(par + 1) * D]
                resS = resf[:, par * D:(par + 1) * D]
                youtS = yout_f[:, par * D:(par + 1) * D]
                pv = fin_prev.get(par, {})
                tpd = p.dma("scalar", onbS, o3T[:, tt * 128:(tt + 1) * 128],
                            transpose=True,
                            waits=o3_done + fin_prev.get(1 - par, {}).get("ra", []) +
                                  pv.get("ra", []))
                xr = p.dma("gpsimd", xrsS, xN[b, tt * 128:(tt + 1) * 128, :],
                           waits=pv.get("sq", []))
                s1 = stat[:, 8 + par:9 + par]
                s2 = stat[:, 10 + par:11 + par]
                ra = p.op("vector", "scalar_tensor_tensor", resS, onbS, 1.0, xrsS,
                          ALU.mult, ALU.add, accum_out=s1,
                          waits=[tpd, xr] + pv.get("ap", []) + pv.get("yst", []),
                          inc="v_e")
                sq = p.op("scalar", "activation", xrsS, resS, AF.Square,
                          accum_out=s2, waits=[ra], inc="s_e")
                mu = stat[:, 12 + par:13 + par]
                rst = stat[:, 14 + par:15 + par]
                tm = p.op("vector", "tensor_scalar", mu, s1, 1.0 / D, None, ALU.mult,
                          waits=[ra] + pv.get("nw", []), inc="v_e")
                tmpc = stat[:, 4 + par:5 + par]
                varc = stat[:, 6 + par:7 + par]
                tv = p.op("vector", "tensor_mul", tmpc, mu, mu, waits=[tm], inc="v_e")
                tv = p.op("vector", "tensor_scalar", tmpc, tmpc, LN_EPS,
                          None, ALU.subtract, waits=[tv], inc="v_e")
                tv = p.op("vector", "scalar_tensor_tensor", varc, s2, 1.0 / D,
                          tmpc, ALU.mult, ALU.subtract, waits=[tv, sq], inc="v_e")
                nw = newton_rsqrt(p, C, rst, varc, 1, waits=[tv])
                ap = p.op("vector", "tensor_scalar", youtS, resS, mu, rst,
                          ALU.subtract, ALU.mult,
                          waits=[nw] + pv.get("yst", []), inc="v_e")
                yst = p.dma("sync", y_out[b, tt * 128:(tt + 1) * 128, :], youtS,
                            waits=[ap])
                fin_prev[par] = dict(ra=[ra], sq=[sq], ap=[ap], yst=[yst], nw=[nw])
            p.fence()


def dbg_dump(p, C, name, sbuf_ap, rows, cols):
    nc = C["nc"]
    t = nc.dram_tensor(name, [rows, cols], mybir.dt.float32 if sbuf_ap.dtype == F32 else BF16,
                       kind="ExternalOutput").ap()
    p.dma("sync", t[:, :], sbuf_ap, waits=p.all_toks())
    p.fence()


def build_program(moe_sched=None, debug=()):
    nc = bass.Bass("TRN2", target_bir_lowering=False)
    p = Plan(nc)

    x_in = nc.dram_tensor("x", [NB, S, D], F32, kind="ExternalInput").ap()
    wqT = nc.dram_tensor("wqT", [D, D], F32, kind="ExternalInput").ap()
    wkvT = nc.dram_tensor("wkvT", [D, R], F32, kind="ExternalInput").ap()
    wkrT = nc.dram_tensor("wkrT", [D, RD], F32, kind="ExternalInput").ap()
    wkuT = nc.dram_tensor("wkuT", [R, H * ND], F32, kind="ExternalInput").ap()
    wvuT = nc.dram_tensor("wvuT", [R, D], F32, kind="ExternalInput").ap()
    woT = nc.dram_tensor("woT", [D, D], F32, kind="ExternalInput").ap()
    wgate = nc.dram_tensor("wgate", [D, I8K], F32, kind="ExternalInput").ap()
    wup = nc.dram_tensor("wup", [D, I8K], F32, kind="ExternalInput").ap()
    wdown = nc.dram_tensor("wdown", [I8K, D], F32, kind="ExternalInput").ap()
    Wqk = nc.dram_tensor("Wqk", [D, D], F32, kind="ExternalInput").ap()
    Wv = nc.dram_tensor("Wv", [D, D], F32, kind="ExternalInput").ap()
    Wu = nc.dram_tensor("Wu", [D, D], F32, kind="ExternalInput").ap()
    Wo1 = nc.dram_tensor("Wo1", [D, D], F32, kind="ExternalInput").ap()
    bqk_t = nc.dram_tensor("bqk_t", [128, 16], F32, kind="ExternalInput").ap()
    bv = nc.dram_tensor("bv", [D], F32, kind="ExternalInput").ap()
    bu = nc.dram_tensor("bu", [D], F32, kind="ExternalInput").ap()
    bo1 = nc.dram_tensor("bo1", [D], F32, kind="ExternalInput").ap()
    ln_g = nc.dram_tensor("ln_g", [D], F32, kind="ExternalInput").ap()
    ln_b = nc.dram_tensor("ln_b", [D], F32, kind="ExternalInput").ap()
    gwT = nc.dram_tensor("gwT", [D, 8], F32, kind="ExternalInput").ap()
    WT = nc.dram_tensor("WT", [D, D], F32, kind="ExternalInput").ap()
    atoms_p = nc.dram_tensor("atoms_p", [64, 16384], F32, kind="ExternalInput").ap()
    eaw_r = nc.dram_tensor("eaw_r", [8, 64], F32, kind="ExternalInput").ap()
    imp_p = nc.dram_tensor("imp_p", [8, 16384], F32, kind="ExternalInput").ap()
    midx_s = nc.dram_tensor("midx_s", [128, 128], I32, kind="ExternalInput").ap()

    y_out = nc.dram_tensor("y", [NB, S, D], F32, kind="ExternalOutput").ap()

    ident_c = nc.inline_tensor(np.eye(128, dtype=ml_dtypes.bfloat16), name="c_ident")
    ones_c = nc.inline_tensor(np.ones((128, 128), dtype=ml_dtypes.bfloat16), name="c_ones")
    pos = np.arange(S, dtype=np.float64)
    freqs = 1.0 / (10000.0 ** (np.arange(0, RD, 2, dtype=np.float64) / RD))
    ang = pos[None, :] * freqs[:, None]
    cos_np = np.repeat(np.cos(ang), 2, axis=0).astype(np.float32)
    sin_np = np.repeat(np.sin(ang), 2, axis=0).astype(np.float32)
    cos_c = nc.inline_tensor(bfc(cos_np), name="c_cos")
    sin_c = nc.inline_tensor(bfc(sin_np), name="c_sin")
    Rm = np.zeros((RD, RD), dtype=np.float32)
    for i in range(0, RD, 2):
        Rm[i, i + 1] = -1.0
        Rm[i + 1, i] = 1.0
    rot_c = nc.inline_tensor(bfc(Rm.T.copy()), name="c_rot")
    mvars = np.zeros((4, 128, 512), dtype=np.float32)
    for dlt in range(4):
        for pk in range(128):
            mvars[dlt, pk, :dlt * 128 + pk] = NEG_BIG
    mask_c = nc.inline_tensor(bfc(mvars.reshape(4 * 128, 512)), name="c_mask")
    iota_rc_c = nc.inline_tensor(np.tile(np.arange(128, dtype=np.float32), (128, 1)),
                                 name="c_iotarc")
    iota8_c = nc.inline_tensor(np.tile(np.arange(8, dtype=np.float32), (128, 1)),
                               name="c_iota8")

    xT_h = nc.dram_tensor("xT", [NB, D, S], BF16,
                          kind=("ExternalOutput" if "xT" in debug else "Internal"))
    xT = xT_h.ap()
    xN = nc.dram_tensor("xN", [NB, S, D], BF16,
                        kind=("ExternalOutput" if "xN" in debug else "Internal")).ap()
    hT = nc.dram_tensor("hT", [NB, I8K, S], BF16, kind="Internal").ap()
    uN = nc.dram_tensor("uN", [NB, S, D], BF16, kind="Internal").ap()
    o3T = nc.dram_tensor("o3T", [D, S], BF16, kind="Internal").ap()
    wdown_bf = nc.dram_tensor("wdown_bf", [I8K, D], BF16, kind="Internal").ap()
    wqT_bf = nc.dram_tensor("wqT_bf", [D, D], BF16, kind="Internal").ap()
    woT_bf = nc.dram_tensor("woT_bf", [D, D], BF16, kind="Internal").ap()
    wgate_bf = nc.dram_tensor("wgate_bf", [D, I8K], BF16, kind="Internal").ap()
    wup_bf = nc.dram_tensor("wup_bf", [D, I8K], BF16, kind="Internal").ap()
    Wqk_bf = nc.dram_tensor("Wqk_bf", [D, D], BF16, kind="Internal").ap()
    Wv_bf = nc.dram_tensor("Wv_bf", [D, D], BF16, kind="Internal").ap()
    Wu_bf = nc.dram_tensor("Wu_bf", [D, D], BF16, kind="Internal").ap()
    Wo1_bf = nc.dram_tensor("Wo1_bf", [D, D], BF16, kind="Internal").ap()
    WT_bf = nc.dram_tensor("WT_bf", [D, D], BF16, kind="Internal").ap()

    consts = {}
    ident_sb = nc.alloc_sbuf_tensor("ident_sb", [128, 128], BF16)
    ones_sb = nc.alloc_sbuf_tensor("ones_sb", [128, 128], BF16)
    cos_sb = nc.alloc_sbuf_tensor("cos_sb", [128, S], BF16)
    sin_sb = nc.alloc_sbuf_tensor("sin_sb", [128, S], BF16)
    rot_sb = nc.alloc_sbuf_tensor("rot_sb", [128, 64], BF16)


    magic_c = nc.inline_tensor(np.full((128, 64), MAGIC, dtype=np.int32), name="c_magic")
    pm32 = nc.alloc_sbuf_tensor("pm32", [128, 32], F32)
    pmt32 = nc.alloc_sbuf_tensor("pmt32", [128, 32], F32)
    nr1 = nc.alloc_sbuf_tensor("nr1", [128, 64], F32)
    nr2 = nc.alloc_sbuf_tensor("nr2", [128, 64], F32)
    magic_sb = nc.alloc_sbuf_tensor("magic_sb", [128, 64], I32)
    stat = nc.alloc_sbuf_tensor("stat", [128, 16], F32)

    ps_a = nc.alloc_psum_tensor("ps_a", [128, 512], F32)
    ps_b = nc.alloc_psum_tensor("ps_b", [128, 512], F32)
    ps_c = nc.alloc_psum_tensor("ps_c", [128, 512], F32)
    ps_d = nc.alloc_psum_tensor("ps_d", [128, 512], F32)
    ps_o = nc.alloc_psum_tensor("ps_o", [128, 512], F32)
    ps_s = nc.alloc_psum_tensor("ps_s", [128, 512], F32)
    ps_t = nc.alloc_psum_tensor("ps_t", [128, 512], BF16)
    ps_t2 = nc.alloc_psum_tensor("ps_t2", [128, 512], BF16)

    C = dict(nc=nc, p=p, x_in=x_in, y_out=y_out,
             wqT=wqT, wkvT=wkvT, wkrT=wkrT, wkuT=wkuT, wvuT=wvuT, woT=woT,
             wgate=wgate, wup=wup, wdown=wdown, hT=hT, wdown_bf=wdown_bf,
             wqT_bf=wqT_bf, woT_bf=woT_bf, wgate_bf=wgate_bf, wup_bf=wup_bf,
             Wqk_bf=Wqk_bf, Wv_bf=Wv_bf, Wu_bf=Wu_bf, Wo1_bf=Wo1_bf, WT_bf=WT_bf,
             Wqk=Wqk, Wv=Wv, Wu=Wu, Wo1=Wo1, bqk_t=bqk_t, bv=bv, bu=bu, bo1=bo1,
             ln_g=ln_g, ln_b=ln_b, uN=uN, o3T=o3T,
             gwT=gwT, WT=WT, atoms_p=atoms_p, eaw_r=eaw_r, imp_p=imp_p, midx_s=midx_s,
             iota_rc_c=iota_rc_c, iota8_c=iota8_c,
             xT=xT, xN=xN, xT_full=xT_h.ap(),
             ident_sb=ident_sb, ones_sb=ones_sb, cos_sb=cos_sb, sin_sb=sin_sb,
             rot_sb=rot_sb, mask_c=mask_c, debug=debug,
             pm32=pm32, pmt32=pmt32, nr1=nr1, nr2=nr2, magic_sb=magic_sb,
             stat=stat,
             ps_a=ps_a, ps_b=ps_b, ps_c=ps_c, ps_d=ps_d, ps_o=ps_o, ps_s=ps_s,
             ps_t=ps_t, ps_t2=ps_t2)

    p.dma("sync", ident_sb[:], ident_c.ap()[:, :])
    p.dma("sync", ones_sb[:], ones_c.ap()[:, :])
    p.dma("sync", cos_sb[0:64, :], cos_c.ap()[:, :])
    p.dma("sync", cos_sb[64:128, :], cos_c.ap()[:, :])
    p.dma("sync", sin_sb[0:64, :], sin_c.ap()[:, :])
    p.dma("sync", sin_sb[64:128, :], sin_c.ap()[:, :])
    p.dma("sync", rot_sb[0:64, :], rot_c.ap()[:, :])
    p.dma("sync", rot_sb[64:128, :], rot_c.ap()[:, :])
    p.dma("sync", magic_sb[:], magic_c.ap()[:, :])
    p.fence()

    emit_P0b(p, C, 0)
    PRECAST_PAIRS = [
        ("wq", C["wqT"], C["wqT_bf"], D, D),
        ("wo", C["woT"], C["woT_bf"], D, D),
        ("Wqk", C["Wqk"], C["Wqk_bf"], D, D),
        ("wd", C["wdown"], C["wdown_bf"], I8K, D),
        ("Wv", C["Wv"], C["Wv_bf"], D, D),
        ("Wu", C["Wu"], C["Wu_bf"], D, D),
        ("Wo1", C["Wo1"], C["Wo1_bf"], D, D),
        ("WT", C["WT"], C["WT_bf"], D, D),
    ]
    for b in range(NB):
        with contextlib.ExitStack() as sbatch:
            KAll = sbatch.enter_context(sbt(nc, "KAll", [128, H * S], BF16))
            Vst = sbatch.enter_context(sbt(nc, "Vst", [128, 16 * 2080], BF16))
            C["KAll"], C["Vst"] = KAll, Vst
            emit_P1(p, C, b, None)
            if b == 0:
                emit_P0b(p, C, 1, qload="gpsimd")
                emit_precast(p, C, PRECAST_PAIRS[0:3])
            if "p1" in debug and b == 0:
                dbg_dump(p, C, "dbg_KAll", KAll[:, :], 128, H * S)
                dbg_dump(p, C, "dbg_Vst", Vst[:, :], 128, 16 * 2080)
                break
            emit_P2(p, C, b)
            if b == 0:
                emit_precast(p, C, PRECAST_PAIRS[3:4], first=False)
        if "p2" in debug and b == 0:
            break
        emit_P3(p, C, b)
        if b == 0:
            emit_precast(p, C, PRECAST_PAIRS[4:], first=False)
        if "p3" in debug and b == 0:
            break
        emit_P4(p, C, b)
        if "p4" in debug and b == 0:
            break
    if not any(d in debug for d in ("p1", "p2", "p3", "p4")):
        emit_P5(p, C, moe_sched)

    p.fence()

    with nc.Block() as block:
        @block.tensor
        def _(t):
            for f in p.seq["tensor"]:
                f(t)
        @block.vector
        def _(v):
            for f in p.seq["vector"]:
                f(v)
        @block.scalar
        def _(s):
            for f in p.seq["scalar"]:
                f(s)
        @block.gpsimd
        def _(g):
            for f in p.seq["gpsimd"]:
                f(g)
        @block.sync
        def _(s):
            for f in p.seq["sync"]:
                f(s)
    return nc


def host_prep(inputs):
    T = lambda a: np.ascontiguousarray(np.asarray(a).T)
    m = dict(
        wqT=T(inputs["L0_attn_wq"]),
        wkvT=T(inputs["L0_attn_w_kv_down"]),
        wkrT=T(inputs["L0_attn_w_k_rope"]),
        wkuT=T(inputs["L0_attn_w_k_up"]),
        wvuT=T(inputs["L0_attn_w_v_up"]),
        woT=T(inputs["L0_attn_wo"]),
        wgate=np.ascontiguousarray(inputs["L0_ffn_w_gate"]),
        wup=np.ascontiguousarray(inputs["L0_ffn_w_up"]),
        wdown=np.ascontiguousarray(inputs["L0_ffn_w_down"]),
        Wqk=np.ascontiguousarray(inputs["L1_attn_Wqk"]),
        Wv=np.ascontiguousarray(inputs["L1_attn_Wv"]),
        Wu=np.ascontiguousarray(inputs["L1_attn_Wu"]),
        Wo1=np.ascontiguousarray(inputs["L1_attn_Wo"]),
        bqk_t=np.ascontiguousarray(inputs["L1_attn_bqk"].reshape(16, 128).T),
        bv=np.ascontiguousarray(inputs["L1_attn_bv"]),
        bu=np.ascontiguousarray(inputs["L1_attn_bu"]),
        bo1=np.ascontiguousarray(inputs["L1_attn_bo"]),
        ln_g=np.ascontiguousarray(inputs["L1_attn_ln_g"]),
        ln_b=np.ascontiguousarray(inputs["L1_attn_ln_b"]),
    )
    return m


def moe_schedule(mask_idx):
    perm = np.argsort(mask_idx, kind="stable")
    ms = np.asarray(mask_idx)[perm]
    rows = ms // D
    sched = []
    for rc in range(16):
        ks = np.nonzero((rows // 128 == rc).reshape(128, 128).any(axis=1))[0]
        sched.append(list(int(k) for k in ks))
    nslots = 4
    for rc in range(16):
        if sched[rc]:
            nslots = max(nslots, sched[rc][-1] - sched[rc][0] + 1)
    nslots = min(nslots + 1, 128)
    return perm, sched, nslots


def kernel(**inputs):
    shared = host_prep(inputs)
    mask_idx = np.asarray(inputs["L1_ffn_mask_idx"])
    perm, sched, nslots = moe_schedule(mask_idx)
    ms = mask_idx[perm].astype(np.int32)
    shared["midx_s"] = np.ascontiguousarray(ms.reshape(128, 128).T)
    shared["atoms_p"] = np.ascontiguousarray(inputs["L1_ffn_atoms"][:, perm])
    shared["imp_p"] = np.ascontiguousarray(inputs["L1_ffn_importance"][:, perm])
    shared["eaw_r"] = np.ascontiguousarray(inputs["L1_ffn_expert_atom_weights"])
    shared["gwT"] = np.ascontiguousarray(inputs["L1_ffn_gate_w"].T)
    shared["WT"] = np.ascontiguousarray(inputs["L1_ffn_W"].T)
    x = np.ascontiguousarray(inputs["L0_attn_x"])
    nc = build_program(moe_sched=dict(sched=sched, nslots=nslots))
    in_maps = []
    for c in range(8):
        im = dict(shared)
        im["x"] = np.ascontiguousarray(x[2 * c:2 * c + 2])
        in_maps.append(im)
    res = run_bass_kernel_spmd(nc, in_maps, core_ids=list(range(8)))
    return np.concatenate([r["y"] for r in res.results], axis=0)


# version 17
# speedup vs baseline: 1.0176x; 1.0008x over previous
import contextlib
import numpy as np
import ml_dtypes
import concourse.bass as bass
import concourse.mybir as mybir
from concourse.bass_core_rust import IndirectOffsetOnAxis
from concourse.bass_utils import run_bass_kernel_spmd

F32 = mybir.dt.float32
BF16 = mybir.dt.bfloat16
I32 = mybir.dt.int32
AF = mybir.ActivationFunctionType
ALU = mybir.AluOpType
AX = mybir.AxisListType

NB = 2
S = 2048
D = 2048
H = 16
HD = 128
ND = 64
RD = 64
R = 512
I8K = 8192
SCALE = float(1.0 / np.sqrt(np.float32(128.0)).astype(np.float32))
LN_EPS = 1e-5
NEG_BIG = -30000.0
MAGIC = 0x5f3759df

ENGS = ("tensor", "vector", "scalar", "gpsimd", "sync")


class Tok:
    __slots__ = ("sem", "val")
    def __init__(self, sem, val):
        self.sem = sem
        self.val = val


class Plan:
    def __init__(self, nc):
        self.nc = nc
        self.seq = {e: [] for e in ENGS}
        self.sems = {}
        self.led = {}
        self.lastwait = {e: {} for e in ENGS}

    def sem(self, name):
        if name not in self.sems:
            self.sems[name] = self.nc.alloc_semaphore(name)
            self.led[name] = 0
        return self.sems[name]

    def _waits(self, eng_name, waits):
        out = []
        lw = self.lastwait[eng_name]
        for t in waits:
            if t is None:
                continue
            if lw.get(t.sem, 0) >= t.val:
                continue
            lw[t.sem] = t.val
            out.append((self.sems[t.sem], t.val))
        return out

    def op(self, eng_name, meth, *args, waits=(), inc=None, inc_k=1, **kwargs):
        w = self._waits(eng_name, waits)
        tok = None
        inch = None
        if inc is not None:
            self.sem(inc)
            self.led[inc] += inc_k
            tok = Tok(inc, self.led[inc])
            inch = self.sems[inc]
        def emit(eng, w=w, meth=meth, args=args, kwargs=kwargs, tok=tok, inch=inch, inc_k=inc_k):
            for (h, v) in w:
                eng.wait_ge(h, v)
            inst = getattr(eng, meth)(*args, **kwargs)
            if tok is not None:
                inst.then_inc(inch, inc_k)
            return inst
        self.seq[eng_name].append(emit)
        return tok

    def waitonly(self, eng_name, waits):
        w = self._waits(eng_name, waits)
        if not w:
            return
        def emit(eng, w=w):
            for (h, v) in w:
                eng.wait_ge(h, v)
        self.seq[eng_name].append(emit)

    POOLQ = {"gpsimd": "g", "sync": "y", "scalar": "c", "vector": "w", "tensor": "t"}

    def dma(self, eng_name, out, in_, waits=(), inc=None, pool=None, npool=8, **kwargs):
        pool = pool or self.POOLQ[eng_name]
        if not hasattr(self, "pool_idx"):
            self.pool_idx = {}
        i = self.pool_idx.get(pool, -1)
        i = (i + 1) % npool
        self.pool_idx[pool] = i
        name = f"{pool}dma{i}"
        self.sem(name)
        w = list(waits)
        if self.led[name] > 0:
            w.append(Tok(name, self.led[name]))
        ind = kwargs.pop("indirect_in", None)
        if ind is not None:
            ax = kwargs.pop("indirect_axis", 0)
            return self.op(eng_name, "indirect_dma_start", out, None, in_,
                           IndirectOffsetOnAxis(ind, ax),
                           waits=w, inc=name, inc_k=16, **kwargs)
        return self.op(eng_name, "dma_start", waits=w, inc=name, inc_k=16,
                       out=out, in_=in_, **kwargs)

    def bank_use(self, key):
        if not hasattr(self, "bankfree"):
            self.bankfree = {}
        return self.bankfree.pop(key, [])

    def bank_done(self, key, tok):
        if not hasattr(self, "bankfree"):
            self.bankfree = {}
        if tok is not None:
            self.bankfree.setdefault(key, []).append(tok)

    def fence(self):
        toks = [Tok(n, self.led[n]) for n in self.led]
        for e in ENGS:
            self.waitonly(e, toks)

    def all_toks(self):
        return [Tok(n, self.led[n]) for n in self.led]


class Ring:
    def __init__(self, n):
        self.n = n
        self.free = [[] for _ in range(n)]
        self.i = 0

    def acquire(self):
        i = self.i % self.n
        self.i += 1
        toks = self.free[i]
        self.free[i] = []
        return i, toks

    def release(self, i, *toks):
        self.free[i].extend(t for t in toks if t is not None)



_NAME_I = [0]
def sbt(nc, name, shape, dt):
    _NAME_I[0] += 1
    return nc.sbuf_tensor(f"{name}{_NAME_I[0]}", shape, dt)

def bfc(a):
    return np.asarray(a, dtype=np.float32).astype(ml_dtypes.bfloat16)



def emit_P0b(p, C, b, qload="gpsimd"):
    x_in, xT, xN, nc = C["x_in"], C["xT"], C["xN"], C["nc"]
    C.setdefault("x_done", {})[b] = []
    with sbt(nc, "p0buf", [128, 4 * D], BF16) as buf:
        ring = Ring(2)
        xn_toks = []
        for tt in range(S // 128):
            i, fr = ring.acquire()
            sl = buf[:, i * D:(i + 1) * D]
            ld = p.dma(qload, sl, x_in[b, tt * 128:(tt + 1) * 128, :], waits=fr)
            st = p.dma("sync", xN[b, tt * 128:(tt + 1) * 128, :], sl, waits=[ld])
            ring.release(i, st)
            xn_toks.append(st)
        ring2 = Ring(2)
        for dc in range(D // 128):
            i, fr = ring2.acquire()
            sl = buf[:, (2 + i) * D:(2 + i) * D + S]
            tp = p.dma("scalar", sl, xN[b, :, dc * 128:(dc + 1) * 128],
                       transpose=True, waits=fr + (xn_toks if dc == 0 else []))
            st = p.dma("sync", xT[b, dc * 128:(dc + 1) * 128, :], sl, waits=[tp])
            ring2.release(i, st)
            C["x_done"][b].append(st)
        for e in ENGS:
            p.waitonly(e, C["x_done"][b])


def rope_rows(p, C, buf_slice, psum_small, tmp, gt, base=0, bank="s", src_tok=None):
    cos_sb, sin_sb, rot_sb = C["cos_sb"], C["sin_sb"], C["rot_sb"]
    lo, hi = base, base + 64
    cs = cos_sb[lo:hi, gt * 512:(gt + 1) * 512]
    sn = sin_sb[lo:hi, gt * 512:(gt + 1) * 512]
    st = [src_tok] if src_tok else [Tok("v_e", p.led.get("v_e", 0)),
                                    Tok("s_e", p.led.get("s_e", 0))]
    mm = p.op("tensor", "matmul", psum_small, rot_sb[lo:hi, :], buf_slice,
              start=True, stop=True,
              waits=st + p.bank_use(bank), inc="t_e")
    t1 = p.op("vector", "tensor_mul", tmp[lo:hi, :], buf_slice, cs,
              waits=st + p.bank_use("ropetmp"), inc="v_e")
    t2 = p.op("vector", "tensor_mul", psum_small, psum_small, sn,
              waits=[mm], inc="v_e")
    t3 = p.op("vector", "tensor_add", buf_slice, tmp[lo:hi, :], psum_small,
              waits=[t1, t2], inc="v_e")
    p.bank_done(bank, t3)
    p.bank_done("ropetmp", t3)
    return t3


def emit_P1(p, C, b, stack):
    nc = C["nc"]
    xT = C["xT"]
    wkvT, wkuT, wvuT, wkrT = C["wkvT"], C["wkuT"], C["wvuT"], C["wkrT"]
    KAll, Vst = C["KAll"], C["Vst"]
    ps_a, ps_b, ps_c, ps_d, ps_s = (C["ps_a"], C["ps_b"], C["ps_c"], C["ps_d"],
                                    C["ps_s"])

    with contextlib.ExitStack() as sco:
        ckvT = sco.enter_context(sbt(nc, "ckvT", [128, 4 * S], BF16))
        krT = sco.enter_context(sbt(nc, "krT", [64, S], BF16))
        C["dbg_ckvT"], C["dbg_krT"] = ckvT, krT

        with contextlib.ExitStack() as stA:
            wkv_sb = stA.enter_context(sbt(nc, "wkv_sb", [128, 2 * 2048], BF16))
            wkr_sb = stA.enter_context(sbt(nc, "wkr_sb", [128, 16 * RD], BF16))
            xg_sb = stA.enter_context(sbt(nc, "xg_sb", [128, 16 * 1024], BF16))
            t64 = stA.enter_context(sbt(nc, "t64", [128, 512], BF16))

            wkr_t = p.dma("gpsimd", wkr_sb[:].rearrange("p (c m) -> p c m", c=16),
                          wkrT.rearrange("(c p) m -> p c m", p=128))
            wkvring = Ring(2)

            xring = Ring(1)
            for gp in range(2):
                xi, xfr = xring.acquire()
                xw = p.dma("gpsimd", xg_sb[:].rearrange("p (c m) -> p c m", c=16),
                           xT[b, :, gp * 1024:(gp + 1) * 1024].rearrange("(c p) m -> p c m", p=128),
                           waits=xfr + C["x_done"][b])
                consumed = []
                for oc in range(4):
                    wvi, wvfr = wkvring.acquire()
                    wvoff = wvi * 2048
                    wkv_t = p.dma("gpsimd",
                                  wkv_sb[:, wvoff:wvoff + 2048].rearrange("p (c m) -> p c m", c=16),
                                  wkvT[:, oc * 128:(oc + 1) * 128].rearrange("(c p) m -> p c m", p=128),
                                  waits=wvfr)
                    keys = ("a", "b") if oc % 2 == 0 else ("c", "d")
                    pss = (ps_a, ps_b) if oc % 2 == 0 else (ps_c, ps_d)
                    mmx = [None, None]
                    for ic in range(16):
                        for gh in range(2):
                            mmx[gh] = p.op("tensor", "matmul", pss[gh][:, 0:512],
                                      wkv_sb[:, wvoff + ic * 128: wvoff + ic * 128 + 128],
                                      xg_sb[:, ic * 1024 + gh * 512: ic * 1024 + gh * 512 + 512],
                                      start=(ic == 0), stop=(ic == 15),
                                      waits=([wkv_t, xw] + p.bank_use(keys[gh])
                                             if ic == 0 else []),
                                      inc=("t_e" if ic == 15 else None))
                    for gh in range(2):
                        gt = gp * 2 + gh
                        ev = p.op("vector", "tensor_copy",
                                  ckvT[:, oc * S + gt * 512: oc * S + gt * 512 + 512],
                                  pss[gh][:, 0:512], waits=[mmx[gh]], inc="v_e")
                        p.bank_done(keys[gh], ev)
                        consumed.append(ev)
                    wkvring.release(wvi, *[mmx[0], mmx[1]])
                for gh in range(2):
                    gt = gp * 2 + gh
                    mm = None
                    for ic in range(16):
                        mm = p.op("tensor", "matmul", ps_s[0:64, 0:512],
                                  wkr_sb[:, ic * RD:(ic + 1) * RD],
                                  xg_sb[:, ic * 1024 + gh * 512: ic * 1024 + gh * 512 + 512],
                                  start=(ic == 0), stop=(ic == 15),
                                  waits=([wkr_t] + p.bank_use("s") if ic == 0 else []),
                                  inc=("t_e" if ic == 15 else None))
                    ev = p.op("vector", "tensor_copy", krT[:, gt * 512:(gt + 1) * 512],
                              ps_s[0:64, 0:512], waits=[mm], inc="v_e")
                    p.bank_done("s", ev)
                    rp = rope_rows(p, C, krT[:, gt * 512:(gt + 1) * 512],
                                   ps_s[0:64, 0:512], t64, gt, base=0, bank="s",
                                   src_tok=ev)
                    consumed.append(rp)
                xring.release(xi, *consumed)
            p.fence()

        with contextlib.ExitStack() as stB:
            wku_sb = stB.enter_context(sbt(nc, "wku_sb", [128, 4 * 1024], BF16))
            wku_t = p.dma("gpsimd", wku_sb[:].rearrange("p (c m) -> p c m", c=4),
                          wkuT.rearrange("(c p) m -> p c m", p=128))
            kbanks = (ps_a, ps_b, ps_c, ps_d)
            for hp in range(8):
                mmx = [None] * 4
                for rc in range(4):
                    for gt in range(4):
                        mmx[gt] = p.op("tensor", "matmul", kbanks[gt][:, 0:512],
                                  wku_sb[:, rc * 1024 + hp * 128: rc * 1024 + hp * 128 + 128],
                                  ckvT[:, rc * S + gt * 512: rc * S + gt * 512 + 512],
                                  start=(rc == 0), stop=(rc == 3),
                                  waits=([wku_t] + p.bank_use("abcd"[gt])
                                         if rc == 0 else []),
                                  inc=("t_e" if rc == 3 else None))
                for gt in range(4):
                    p.op("vector", "tensor_copy",
                         KAll[0:64, (2 * hp) * S + gt * 512:(2 * hp) * S + gt * 512 + 512],
                         kbanks[gt][0:64, 0:512], waits=[mmx[gt]], inc="v_e")
                    ev = p.op("vector", "tensor_copy",
                         KAll[0:64, (2 * hp + 1) * S + gt * 512:(2 * hp + 1) * S + gt * 512 + 512],
                         kbanks[gt][64:128, 0:512], waits=[mmx[gt]], inc="v_e")
                    p.bank_done("abcd"[gt], ev)
            rw = Tok("v_e", p.led["v_e"])
            for h in range(H):
                p.op("vector", "tensor_copy", KAll[64:128, h * S:(h + 1) * S],
                     krT[:, :], waits=[rw], inc="v_e")
            p.fence()

        with contextlib.ExitStack() as stC:
            wvu_sb = stC.enter_context(sbt(nc, "wvu_sb", [128, 4 * D], BF16))
            wvu_t = p.dma("gpsimd", wvu_sb[:].rearrange("p (c m) -> p c m", c=4),
                          wvuT.rearrange("(c p) m -> p c m", p=128))
            ones_v = Vst[:].rearrange("p (a u) -> p a u", u=130)[:, :, 128:130]
            p.op("vector", "memset", ones_v, 1.0, inc="v_e")
            vbanks = (ps_a, ps_b, ps_c, ps_d)
            for tt in range(16):
                mmx = [None] * 4
                for rc in range(4):
                    for dc4 in range(4):
                        mmx[dc4] = p.op("tensor", "matmul", vbanks[dc4][:, 0:512],
                                  ckvT[:, rc * S + tt * 128: rc * S + tt * 128 + 128],
                                  wvu_sb[:, rc * D + dc4 * 512: rc * D + dc4 * 512 + 512],
                                  start=(rc == 0), stop=(rc == 3),
                                  waits=([wvu_t] + p.bank_use("abcd"[dc4])
                                         if rc == 0 else []),
                                  inc=("t_e" if rc == 3 else None))
                for dc4 in range(4):
                    dst = Vst[:, tt * 2080 + dc4 * 4 * 130: tt * 2080 + (dc4 * 4 + 4) * 130]
                    ev = p.op("vector", "tensor_copy",
                         dst.rearrange("p (h m) -> p h m", h=4)[:, :, 0:128],
                         vbanks[dc4][:, 0:512].rearrange("p (h m) -> p h m", h=4),
                         waits=[mmx[dc4]], inc="v_e")
                    p.bank_done("abcd"[dc4], ev)
            p.fence()
        if "p1" in C["debug"]:
            dbg_dump(p, C, "dbg_ckvT_o", ckvT[:, :], 128, 4 * S)
            dbg_dump(p, C, "dbg_krT_o", krT[:, :], 64, S)
    p.fence()




def part_max128(p, C, out11, in_col, nparts=128):
    pm, pmt = C["pm32"], C["pmt32"]
    ps = C["ps_s"]
    t0 = p.op("vector", "memset", pm[:, :], -3e38, inc="v_e")
    t1 = p.op("vector", "tensor_copy", pm[0:nparts, 0:1], in_col, waits=[t0], inc="v_e")
    t2 = p.op("vector", "transpose", pmt[:, :], pm[:, :], waits=[t1], inc="v_e")
    c1 = p.op("vector", "tensor_copy", ps[0:64, 256:288], pmt[64:128, 0:32],
              waits=[t2], inc="v_e")
    t3 = p.op("vector", "tensor_max", pmt[0:64, 0:32], pmt[0:64, 0:32], ps[0:64, 256:288],
              waits=[c1], inc="v_e")
    c2 = p.op("vector", "tensor_copy", ps[0:32, 256:288], pmt[32:64, 0:32],
              waits=[t3], inc="v_e")
    t4 = p.op("vector", "tensor_max", pmt[0:32, 0:32], pmt[0:32, 0:32], ps[0:32, 256:288],
              waits=[c2], inc="v_e")
    t5 = p.op("vector", "tensor_reduce", out11, pmt[0:1, 0:32], AX.X, ALU.max,
              waits=[t4], inc="v_e")
    return t5


def newton_rsqrt(p, C, dst, src, w, nparts=128, waits=()):
    nr1, nr2, magic = C["nr1"], C["nr2"], C["magic_sb"]
    y = dst
    t = nr1[0:nparts, 0:w]
    n2 = nr2[0:nparts, 0:w]
    tv = p.op("vector", "tensor_scalar", n2.bitcast(I32), src.bitcast(I32),
              1, None, ALU.arith_shift_right, waits=list(waits), inc="v_e")
    tv = p.op("vector", "tensor_tensor", y.bitcast(I32), magic[0:nparts, 0:w].bitcast(I32),
              n2.bitcast(I32), ALU.subtract, waits=[tv], inc="v_e")
    for _ in range(3):
        tv = p.op("vector", "tensor_mul", t, y, y, waits=[tv], inc="v_e")
        tv = p.op("vector", "tensor_mul", t, t, src, waits=[tv], inc="v_e")
        tv = p.op("vector", "tensor_scalar", t, t, -0.5, 1.5,
                  ALU.mult, ALU.add, waits=[tv], inc="v_e")
        tv = p.op("vector", "tensor_mul", y, y, t, waits=[tv], inc="v_e")
    return tv


def emit_ln(p, C, resf, xnew_slice, out_f32=None, waits=(), sqscratch=None):
    stat = C["stat"]
    if sqscratch is None:
        sqscratch = xnew_slice
    s1, s2, mu, rst = (stat[:, 0:1], stat[:, 1:2], stat[:, 2:3], stat[:, 3:4])
    t1 = p.op("vector", "tensor_reduce", s1, resf, AX.X, ALU.add, waits=list(waits), inc="v_e")
    t2 = p.op("scalar", "activation", sqscratch, resf, AF.Square,
              accum_out=s2, waits=list(waits), inc="s_e")
    tm = p.op("vector", "tensor_scalar", mu, s1, 1.0 / D, None, ALU.mult,
              waits=[t1], inc="v_e")
    tv = p.op("vector", "tensor_mul", stat[:, 4:5], mu, mu, waits=[tm], inc="v_e")
    tv = p.op("vector", "tensor_scalar", stat[:, 4:5], stat[:, 4:5], LN_EPS, None,
              ALU.subtract, waits=[tv], inc="v_e")
    tv = p.op("vector", "scalar_tensor_tensor", stat[:, 5:6], s2, 1.0 / D,
              stat[:, 4:5], ALU.mult, ALU.subtract, waits=[tv, t2], inc="v_e")
    tv = newton_rsqrt(p, C, rst, stat[:, 5:6], 1, waits=[tv])
    toks = []
    ta = p.op("vector", "tensor_scalar", xnew_slice, resf, mu, rst,
              ALU.subtract, ALU.mult, waits=[tv], inc="v_e")
    toks.append(ta)
    if out_f32 is not None:
        tb = p.op("vector", "tensor_scalar", out_f32, resf, mu, rst,
                  ALU.subtract, ALU.mult, waits=[tv], inc="v_e")
        toks.append(tb)
    return toks


def emit_xN_to_xT(p, C, b):
    nc, xN, xT = C["nc"], C["xN"], C["xT"]
    with sbt(nc, "txbuf", [128, 4 * S], BF16) as buf:
        ring = Ring(4)
        for dc in range(D // 128):
            i, fr = ring.acquire()
            sl = buf[:, i * S:(i + 1) * S]
            tp = p.dma("scalar", sl, xN[b, :, dc * 128:(dc + 1) * 128],
                       transpose=True, waits=fr)
            st = p.dma("sync", xT[b, dc * 128:(dc + 1) * 128, :], sl, waits=[tp])
            ring.release(i, st)
    p.fence()


def emit_P2(p, C, b):
    nc = C["nc"]
    xT, xN, x_in = C["xT"], C["xN"], C["x_in"]
    wqT, woT = C["wqT"], C["woT"]
    KAll, Vst = C["KAll"], C["Vst"]
    ident_sb, ones_sb = C["ident_sb"], C["ones_sb"]
    ps_a, ps_b, ps_c, ps_d = C["ps_a"], C["ps_b"], C["ps_c"], C["ps_d"]
    ps_o, ps_s, ps_t, ps_t2 = C["ps_o"], C["ps_s"], C["ps_t"], C["ps_t2"]

    with contextlib.ExitStack() as sco:
        mask_sb = sco.enter_context(sbt(nc, "mask_sb", [128, 4 * 512], BF16))
        sm = sco.enter_context(sbt(nc, "sm", [128, 16], F32))
        smb = sco.enter_context(sbt(nc, "smb", [128, 8], BF16))
        bnd = sco.enter_context(sbt(nc, "bnd", [128, 1], F32))
        p.dma("sync", mask_sb[:].rearrange("p (c m) -> p c m", c=4),
              C["mask_c"].ap().rearrange("(c p) m -> p c m", p=128))

        mKn, mKr = sm[0:1, 8:9], sm[0:1, 9:10]
        sqstack = contextlib.ExitStack()
        sqscr = sqstack.enter_context(sbt(nc, "sqscr", [64, S], BF16))
        for (dst, rlo, rhi, hs) in ((mKn, 0, 64, range(H)), (mKr, 64, 128, range(1))):
            t0 = p.op("vector", "memset", dst, 0.0, inc="v_e")
            mx_prev = None
            for h in hs:
                sq = p.op("scalar", "activation", sqscr[:, :],
                          KAll[rlo:rhi, h * S:(h + 1) * S], AF.Square,
                          waits=[Tok("v_e", p.led["v_e"]), Tok("t_e", p.led.get("t_e", 0))],
                          inc="s_e")
                for c4 in range(4):
                    mm = p.op("tensor", "matmul", ps_s[0:1, 0:512],
                              ones_sb[0:64, 0:1], sqscr[:, c4 * 512:(c4 + 1) * 512],
                              start=True, stop=True,
                              waits=[sq] + p.bank_use("s"), inc="t_e")
                    rd = p.op("vector", "tensor_reduce", sm[0:1, 10:11],
                              ps_s[0:1, 0:512], AX.X, ALU.max,
                              waits=[mm] + ([mx_prev] if mx_prev else []), inc="v_e")
                    p.bank_done("s", rd)
                    mx_prev = p.op("vector", "tensor_max", dst, dst, sm[0:1, 10:11],
                         waits=[rd], inc="v_e")

        p.fence()
        sqstack.close()

        for g in range(4):
            nkc = 4 * g + 4
            with contextlib.ExitStack() as sgrp:
                osb = sgrp.enter_context(sbt(nc, "osb", [128, 4 * D], BF16))
                sqq = contextlib.ExitStack()
                qg = sqq.enter_context(sbt(nc, "qg", [128, 16 * 512], BF16))
                with contextlib.ExitStack() as spr:
                    xg_sb = spr.enter_context(sbt(nc, "xg2", [128, 16 * 512], BF16))
                    wqb = spr.enter_context(sbt(nc, "wqb", [128, 2 * 2048], BF16))
                    t64b = spr.enter_context(sbt(nc, "t64b", [128, 512], BF16))
                    sqch = spr.enter_context(sbt(nc, "sqch", [128, 512], BF16))
                    xw = p.dma("gpsimd", xg_sb[:].rearrange("p (c m) -> p c m", c=16),
                               xT[b, :, g * 512:(g + 1) * 512].rearrange("(c p) m -> p c m", p=128))
                    wring = Ring(2)
                    for oc in range(16):
                        wi, wfr = wring.acquire()
                        wt = p.dma("gpsimd", wqb[:, wi * 2048:(wi + 1) * 2048]
                                   .rearrange("p (c m) -> p c m", c=16),
                                   C["wqT_bf"][:, oc * 128:(oc + 1) * 128]
                                   .rearrange("(c p) m -> p c m", p=128),
                                   waits=wfr + C["wtok"]["wq"])
                        key = "a" if oc % 2 == 0 else "b"
                        ps = ps_a if oc % 2 == 0 else ps_b
                        mm = None
                        for ic in range(16):
                            mm = p.op("tensor", "matmul", ps[:, 0:512],
                                      wqb[:, wi * 2048 + ic * 128: wi * 2048 + ic * 128 + 128],
                                      xg_sb[:, ic * 512:(ic + 1) * 512],
                                      start=(ic == 0), stop=(ic == 15),
                                      waits=([wt, xw] + p.bank_use(key) if ic == 0 else []),
                                      inc=("t_e" if ic == 15 else None))
                        ev = p.op("scalar", "activation", qg[:, oc * 512:(oc + 1) * 512],
                                  ps[:, 0:512], AF.Identity, waits=[mm], inc="s_e")
                        p.bank_done(key, ev)
                        rp = rope_rows(p, C, qg[64:128, oc * 512:(oc + 1) * 512],
                                       ps_s[64:128, 0:512], t64b, g, base=64, bank="s",
                                       src_tok=ev)
                        wring.release(wi, ev, rp)
                    mQn, mQr = sm[0:1, 11:12], sm[0:1, 12:13]
                    z0 = p.op("vector", "memset", sm[0:1, 11:13], 0.0, inc="v_e")
                    for oc in range(16):
                        sq = p.op("scalar", "activation", sqch[:, :],
                                  qg[:, oc * 512:(oc + 1) * 512], AF.Square,
                                  waits=[Tok("v_e", p.led["v_e"]),
                                         Tok("s_e", p.led["s_e"])], inc="s_e")
                        mmn = p.op("tensor", "matmul", ps_s[0:1, 0:512],
                                   ones_sb[0:64, 0:1], sqch[0:64, :],
                                   start=True, stop=True,
                                   waits=[sq] + p.bank_use("s"), inc="t_e")
                        mmr = p.op("tensor", "matmul", ps_o[0:1, 0:512],
                                   ones_sb[64:128, 0:1], sqch[64:128, :],
                                   start=True, stop=True,
                                   waits=[sq] + p.bank_use("o"), inc="t_e")
                        r1 = p.op("vector", "tensor_reduce", sm[0:1, 13:14],
                                  ps_s[0:1, 0:512], AX.X, ALU.max, waits=[mmn], inc="v_e")
                        p.bank_done("s", r1)
                        x1 = p.op("vector", "tensor_max", mQn, mQn, sm[0:1, 13:14],
                                  waits=[r1, z0], inc="v_e")
                        r2 = p.op("vector", "tensor_reduce", sm[0:1, 14:15],
                                  ps_o[0:1, 0:512], AX.X, ALU.max, waits=[mmr], inc="v_e")
                        p.bank_done("o", r2)
                        p.op("vector", "tensor_max", mQr, mQr, sm[0:1, 14:15],
                             waits=[r2, x1], inc="v_e")
                    def _sqrt_into(dst, srcv):
                        tv = newton_rsqrt(p, C, dst, srcv, 1, nparts=1,
                                          waits=[Tok("v_e", p.led["v_e"])])
                        return p.op("vector", "tensor_mul", dst, dst, srcv,
                                    waits=[tv], inc="v_e")
                    tqn = _sqrt_into(sm[0:1, 13:14], mQn)
                    tqr = _sqrt_into(sm[0:1, 14:15], mQr)
                    tkn = _sqrt_into(sm[0:1, 4:5], mKn)
                    tkr = _sqrt_into(sm[0:1, 5:6], mKr)
                    t1 = p.op("vector", "tensor_mul", sm[0:1, 6:7], sm[0:1, 13:14],
                              sm[0:1, 4:5], waits=[tqn, tkn], inc="v_e")
                    t2 = p.op("vector", "tensor_mul", sm[0:1, 7:8], sm[0:1, 14:15],
                              sm[0:1, 5:6], waits=[tqr, tkr], inc="v_e")
                    t3 = p.op("vector", "tensor_add", sm[0:1, 6:7], sm[0:1, 6:7],
                              sm[0:1, 7:8], waits=[t1, t2], inc="v_e")
                    t4 = p.op("vector", "tensor_scalar", smb[0:1, 0:1], sm[0:1, 6:7],
                              -SCALE, None, ALU.mult, waits=[t3], inc="v_e")
                    bmm = p.op("tensor", "matmul", ps_s[0:128, 0:1], ones_sb[0:1, :],
                               smb[0:1, 0:1], start=True, stop=True,
                               waits=[t4] + p.bank_use("s"), inc="t_e")
                    bcp = p.op("vector", "tensor_copy", bnd[:, 0:1], ps_s[:, 0:1],
                               waits=[bmm], inc="v_e")
                    p.bank_done("s", bcp)
                with contextlib.ExitStack() as shd:
                    pT = shd.enter_context(sbt(nc, "pT", [128, 2 * 16 * 512], BF16))
                    linv = shd.enter_context(sbt(nc, "linv", [128, 2], F32))
                    avdone = {0: None, 1: None}
                    qg_done = [Tok("s_e", p.led["s_e"]), Tok("v_e", p.led["v_e"])]
                    for h in range(H):
                        par = h % 2
                        poff = par * 16 * 512
                        exp_toks = []
                        for kc in range(nkc):
                            key = "a" if kc % 2 == 0 else "b"
                            ps = ps_a if kc % 2 == 0 else ps_b
                            w0 = p.bank_use(key) + (qg_done if kc == 0 else [])
                            mm = p.op("tensor", "matmul", ps[:, 0:512],
                                      KAll[:, h * S + kc * 128: h * S + kc * 128 + 128],
                                      qg[:, h * 512:(h + 1) * 512],
                                      start=True, stop=(kc < 4 * g),
                                      waits=w0, inc=("t_e" if kc < 4 * g else None))
                            if kc >= 4 * g:
                                mm = p.op("tensor", "matmul", ps[:, 0:512],
                                          ident_sb[:, :],
                                          mask_sb[:, (kc - 4 * g) * 512:(kc - 4 * g + 1) * 512],
                                          start=False, stop=True, inc="t_e")
                            et = p.op("scalar", "activation",
                                      pT[:, poff + kc * 512: poff + (kc + 1) * 512],
                                      ps[:, 0:512],
                                      AF.Exp, bias=bnd[:, 0:1], scale=SCALE,
                                      waits=[mm, bcp] +
                                            ([avdone[par]] if (kc == 0 and avdone[par])
                                             else []),
                                      inc="s_e")
                            p.bank_done(key, et)
                            exp_toks.append(et)
                        for qs in range(4):
                            mm = None
                            for kc in range(nkc):
                                mm = p.op("tensor", "matmul", ps_o[:, 0:130],
                                          pT[:, poff + kc * 512 + qs * 128:
                                             poff + kc * 512 + qs * 128 + 128],
                                          Vst[:, kc * 2080 + h * 130: kc * 2080 + h * 130 + 130],
                                          start=(kc == 0), stop=(kc == nkc - 1),
                                          waits=(exp_toks + p.bank_use("o")
                                                 if kc == 0 else []),
                                          inc=("t_e" if kc == nkc - 1 else None))
                            tl = p.op("vector", "tensor_scalar_max", linv[:, 0:1],
                                      ps_o[:, 128:129], 1e-35, waits=[mm], inc="v_e")
                            tl = p.op("vector", "reciprocal", linv[:, 0:1], linv[:, 0:1],
                                      waits=[tl], inc="v_e")
                            avdone[par] = p.op("vector", "tensor_scalar",
                                 osb[:, qs * D + h * 128: qs * D + h * 128 + 128],
                                 ps_o[:, 0:128], linv[:, 0:1], None, ALU.mult,
                                 waits=[tl], inc="v_e")
                            p.bank_done("o", avdone[par])
                p.waitonly("vector", [Tok("t_e", p.led["t_e"]), Tok("s_e", p.led["s_e"])])
                sqq.close()
                with contextlib.ExitStack() as swo:
                    otb = swo.enter_context(sbt(nc, "otb", [128, 16 * 128], BF16))
                    wob = swo.enter_context(sbt(nc, "wob", [128, 2 * 2048], BF16))
                    resf = swo.enter_context(sbt(nc, "resf", [128, D], F32))
                    xnew = swo.enter_context(sbt(nc, "xnew", [128, D], BF16))
                    xres = swo.enter_context(sbt(nc, "xres", [128, D], F32))
                    woring = Ring(2)
                    xnew_prev = None
                    for qs in range(4):
                        tt = g * 4 + qs
                        xr = p.dma("gpsimd", xres[:, :],
                                   x_in[b, tt * 128:(tt + 1) * 128, :],
                                   waits=[Tok("s_e", p.led["s_e"])])
                        cps = []
                        osb_tok = Tok("v_e", p.led["v_e"])
                        for h in range(H):
                            key = "t" if h % 2 == 0 else "t2"
                            pst = ps_t if h % 2 == 0 else ps_t2
                            tp = p.op("tensor", "transpose", pst[:, 0:128],
                                      osb[:, qs * D + h * 128: qs * D + h * 128 + 128],
                                      ident_sb[:, :],
                                      waits=[osb_tok] + p.bank_use(key), inc="t_e")
                            cp = p.op("vector", "tensor_copy", otb[:, h * 128:(h + 1) * 128],
                                      pst[:, 0:128], waits=[tp], inc="v_e")
                            p.bank_done(key, cp)
                            cps.append(cp)
                        mmf = []
                        for dc in range(16):
                            wi, wfr = woring.acquire()
                            wt = p.dma("gpsimd", wob[:, wi * 2048:(wi + 1) * 2048],
                                       C["woT_bf"][dc * 128:(dc + 1) * 128, :],
                                       waits=wfr + C["wtok"]["wo"])
                            for cb in range(4):
                                psx = (ps_a, ps_b, ps_c, ps_d)[cb]
                                key = "abcd"[cb]
                                mm = p.op("tensor", "matmul", psx[:, 0:512],
                                          otb[:, dc * 128:(dc + 1) * 128],
                                          wob[:, wi * 2048 + cb * 512: wi * 2048 + cb * 512 + 512],
                                          start=(dc == 0), stop=(dc == 15),
                                          waits=([wt, cps[-1]] + p.bank_use(key)
                                                 if dc == 0 else [wt]),
                                          inc=("t_e" if (dc == 15 or cb == 3) else None))
                                if dc == 15:
                                    mmf.append(mm)
                            wring_tok = mm
                            woring.release(wi, wring_tok)
                        ra = []
                        for cb in range(4):
                            psx = (ps_a, ps_b, ps_c, ps_d)[cb]
                            t = p.op("vector", "tensor_add", resf[:, cb * 512:(cb + 1) * 512],
                                     psx[:, 0:512], xres[:, cb * 512:(cb + 1) * 512],
                                     waits=[mmf[cb], xr], inc="v_e")
                            p.bank_done("abcd"[cb], t)
                            ra.append(t)
                        emit_ln(p, C, resf[:, :], xnew[:, :],
                                waits=ra + ([xnew_prev] if xnew_prev else []))
                        xnew_prev = p.dma("sync", xN[b, tt * 128:(tt + 1) * 128, :],
                                          xnew[:, :], waits=[Tok("v_e", p.led["v_e"])])
            p.waitonly("gpsimd", [Tok("v_e", p.led["v_e"]), Tok("t_e", p.led["t_e"]),
                                  Tok("s_e", p.led["s_e"]), xnew_prev])
            p.waitonly("sync", [Tok("v_e", p.led["v_e"]), Tok("t_e", p.led["t_e"]),
                                Tok("s_e", p.led["s_e"])])
    p.fence()
    emit_xN_to_xT(p, C, b)



def emit_P3(p, C, b):
    nc = C["nc"]
    xT, xN = C["xT"], C["xN"]
    wgate, wup, hT, wdown_bf = C["wgate"], C["wup"], C["hT"], C["wdown_bf"]
    ident_sb = C["ident_sb"]
    ps_a, ps_b, ps_c, ps_d = C["ps_a"], C["ps_b"], C["ps_c"], C["ps_d"]
    ps_t, ps_t2 = C["ps_t"], C["ps_t2"]

    with contextlib.ExitStack() as sA:
        xtb = sA.enter_context(sbt(nc, "xtb", [128, 2 * 2048], BF16))
        wgs = sA.enter_context(sbt(nc, "wgs", [128, 2 * 2 * 16 * 512], BF16))
        wus = sA.enter_context(sbt(nc, "wus", [128, 2 * 2 * 16 * 512], BF16))
        hwrk = sA.enter_context(sbt(nc, "hwrk", [128, 8 * 512], BF16))
        pw = sA.enter_context(sbt(nc, "pw", [128, 16 * 128], BF16))
        hTblk = sA.enter_context(sbt(nc, "hTblk", [128, 8 * S], BF16))
        wring = Ring(2)
        xring = Ring(2)
        hT_prev = None
        t2_prev = a9_prev = tp_prev = None
        banks = (ps_a, ps_b, ps_c, ps_d)
        for cbg in range(8):
            wi, wfr = wring.acquire()
            woff = wi * 2 * 16 * 512
            wgt = p.dma("gpsimd", wgs[:, woff:woff + 16384]
                        .rearrange("p (c m) -> p c m", c=16),
                        C["wgate"][:, cbg * 1024:(cbg + 1) * 1024]
                        .rearrange("(c p) m -> p c m", p=128),
                        waits=wfr)
            wut = p.dma("gpsimd", wus[:, woff:woff + 16384]
                        .rearrange("p (c m) -> p c m", c=16),
                        C["wup"][:, cbg * 1024:(cbg + 1) * 1024]
                        .rearrange("(c p) m -> p c m", p=128))
            last_sp = None
            for tt in range(16):
                xi, xfr = xring.acquire()
                xoff = xi * 2048
                xw = p.dma("gpsimd", xtb[:, xoff:xoff + 2048]
                           .rearrange("p (c m) -> p c m", c=16),
                           xT[b, :, tt * 128:(tt + 1) * 128]
                           .rearrange("(c p) m -> p c m", p=128), waits=xfr)
                mmx = [None] * 4
                for ic in range(16):
                    lhs = xtb[:, xoff + ic * 128: xoff + ic * 128 + 128]
                    for half in range(2):
                        mmx[half] = p.op("tensor", "matmul", banks[half][:, 0:512],
                                   lhs,
                                   wgs[:, woff + ic * 1024 + half * 512:
                                       woff + ic * 1024 + half * 512 + 512],
                                   start=(ic == 0), stop=(ic == 15),
                                   waits=([xw, wgt] + p.bank_use("abcd"[half])
                                          if ic == 0 else []),
                                   inc=("t_e" if ic == 15 else None))
                        mmx[2 + half] = p.op("tensor", "matmul", banks[2 + half][:, 0:512],
                                   lhs,
                                   wus[:, woff + ic * 1024 + half * 512:
                                       woff + ic * 1024 + half * 512 + 512],
                                   start=(ic == 0), stop=(ic == 15),
                                   waits=([xw, wut] + p.bank_use("abcd"[2 + half])
                                          if ic == 0 else []),
                                   inc=("t_e" if ic == 15 else None))
                hg = hwrk[:, 0:1024]
                h = hwrk[:, 1024:2048]
                sq = hwrk[:, 2048:3072]
                msk = hwrk[:, 3072:4096]
                t1a = p.op("scalar", "activation", hg[:, 0:512], banks[0][:, 0:512],
                           AF.Silu, waits=[mmx[0]] + ([t2_prev] if t2_prev else []),
                           inc="s_e")
                t1b = p.op("scalar", "activation", hg[:, 512:1024], banks[1][:, 0:512],
                           AF.Silu, waits=[mmx[1]], inc="s_e")
                p.bank_done("a", t1a)
                p.bank_done("b", t1b)
                t2a = p.op("vector", "tensor_mul", h[:, 0:512], hg[:, 0:512],
                           banks[2][:, 0:512],
                           waits=[t1a, mmx[2]] + ([tp_prev] if tp_prev else []),
                           inc="v_e")
                t2 = p.op("vector", "tensor_mul", h[:, 512:1024], hg[:, 512:1024],
                          banks[3][:, 0:512], waits=[t1b, mmx[3]], inc="v_e")
                p.bank_done("c", t2a)
                p.bank_done("d", t2)
                t2_prev = t2
                t3 = p.op("vector", "tensor_mul", sq, h, h,
                          waits=[t2a, t2] + ([a9_prev] if a9_prev else []), inc="v_e")
                v4 = sq.rearrange("p (g f) -> p g f", f=4)
                m01 = pw[:, 0:256]; m23 = pw[:, 256:512]
                l01 = pw[:, 512:768]; l23 = pw[:, 768:1024]
                u1 = pw[:, 1024:1280]; u2 = pw[:, 1280:1536]; m2 = pw[:, 1536:1792]
                a1 = p.op("vector", "tensor_max", m01, v4[:, :, 0], v4[:, :, 1], waits=[t3], inc="v_e")
                a2 = p.op("vector", "tensor_max", m23, v4[:, :, 2], v4[:, :, 3], inc="v_e")
                a3 = p.op("vector", "tensor_tensor", l01, v4[:, :, 0], v4[:, :, 1], ALU.min, inc="v_e")
                a4 = p.op("vector", "tensor_tensor", l23, v4[:, :, 2], v4[:, :, 3], ALU.min, inc="v_e")
                a5 = p.op("vector", "tensor_tensor", u1, m01, m23, ALU.min, waits=[a1, a2], inc="v_e")
                a6 = p.op("vector", "tensor_max", u2, l01, l23, waits=[a3, a4], inc="v_e")
                a7 = p.op("vector", "tensor_max", m2, u1, u2, waits=[a5, a6], inc="v_e")
                a8 = p.op("vector", "tensor_tensor", msk.rearrange("p (g f) -> p g f", f=4),
                          v4, m2.rearrange("p (g o) -> p g o", o=1).broadcast_to([128, 256, 4]),
                          ALU.is_ge, waits=[a7], inc="v_e")
                a9 = p.op("vector", "tensor_mul", h, h, msk, waits=[a8], inc="v_e")
                a9_prev = a9
                for j in range(8):
                    key = "t" if j % 2 == 0 else "t2"
                    pst = ps_t if j % 2 == 0 else ps_t2
                    tp = p.op("tensor", "transpose", pst[:, 0:128],
                              h[:, j * 128:(j + 1) * 128], ident_sb[:, :],
                              waits=[a9] + p.bank_use(key), inc="t_e")
                    last_sp = p.op("vector", "tensor_copy",
                                   hTblk[:, j * S + tt * 128: j * S + tt * 128 + 128],
                                   pst[:, 0:128],
                                   waits=[tp] + ([hT_prev] if (tt == 0 and hT_prev) else []),
                                   inc="v_e")
                    p.bank_done(key, last_sp)
                    tp_prev = tp
                xring.release(xi, *mmx)
            st = p.dma("sync", hT[b, cbg * 1024:(cbg + 1) * 1024, :]
                       .rearrange("(c p) t -> p c t", p=128),
                       hTblk[:].rearrange("p (c t) -> p c t", c=8),
                       waits=[Tok("v_e", p.led["v_e"])])
            hT_prev = st
            wring.release(wi, st)
        p.fence()

    with contextlib.ExitStack() as sB:
        wd_sb = sB.enter_context(sbt(nc, "wd_sb", [128, 64 * 1024], BF16))
        ht_sb = sB.enter_context(sbt(nc, "ht_sb", [128, 2 * 64 * 128], BF16))
        resH = sB.enter_context(sbt(nc, "resH", [128, 8 * 1024], BF16))
        resf = sB.enter_context(sbt(nc, "resf", [128, D], F32))
        xrs = sB.enter_context(sbt(nc, "xrs", [128, D], BF16))
        xnew = sB.enter_context(sbt(nc, "xnew", [128, D], BF16))
        htring = Ring(2)
        xn_prev = None
        for th in range(2):
          for dh in range(2):
            wdt = p.dma("gpsimd", wd_sb[:].rearrange("p (c m) -> p c m", c=64),
                        wdown_bf[:, dh * 1024:(dh + 1) * 1024]
                        .rearrange("(c p) m -> p c m", p=128),
                        waits=[Tok("t_e", p.led.get("t_e", 0))] + C["wtok"]["wd"])
            for tl in range(8):
                tt = th * 8 + tl
                hti, htfr = htring.acquire()
                hoff = hti * 8192
                htok = p.dma("gpsimd",
                             ht_sb[:, hoff:hoff + 8192].rearrange("p (c m) -> p c m", c=64),
                             hT[b, :, tt * 128:(tt + 1) * 128]
                             .rearrange("(c p) m -> p c m", p=128),
                             waits=htfr)
                mm0 = mm1 = None
                for ic in range(64):
                    lhs = ht_sb[:, hoff + ic * 128: hoff + ic * 128 + 128]
                    mm0 = p.op("tensor", "matmul", ps_a[:, 0:512], lhs,
                               wd_sb[:, ic * 1024:ic * 1024 + 512],
                               start=(ic == 0), stop=(ic == 63),
                               waits=([wdt, htok] + p.bank_use("a") if ic == 0 else []),
                               inc=("t_e" if ic == 63 else None))
                    mm1 = p.op("tensor", "matmul", ps_b[:, 0:512], lhs,
                               wd_sb[:, ic * 1024 + 512:ic * 1024 + 1024],
                               start=(ic == 0), stop=(ic == 63),
                               waits=([wdt, htok] + p.bank_use("b") if ic == 0 else []),
                               inc=("t_e" if ic == 63 else None))
                if dh == 0:
                    e0 = p.op("vector", "tensor_copy", resH[:, tl * 1024:tl * 1024 + 512],
                              ps_a[:, 0:512], waits=[mm0], inc="v_e")
                    e1 = p.op("vector", "tensor_copy", resH[:, tl * 1024 + 512:(tl + 1) * 1024],
                              ps_b[:, 0:512], waits=[mm1], inc="v_e")
                    p.bank_done("a", e0)
                    p.bank_done("b", e1)
                else:
                    xr = p.dma("gpsimd", xrs[:, :], xN[b, tt * 128:(tt + 1) * 128, :],
                               waits=[Tok("v_e", p.led["v_e"]),
                                      Tok("s_e", p.led.get("s_e", 0))])
                    r0 = p.op("vector", "tensor_add", resf[:, 0:1024],
                              resH[:, tl * 1024:(tl + 1) * 1024], xrs[:, 0:1024],
                              waits=[xr] + ([xn_prev] if xn_prev else []), inc="v_e")
                    r1 = p.op("vector", "tensor_add", resf[:, 1024:1536],
                              ps_a[:, 0:512], xrs[:, 1024:1536], waits=[mm0, xr], inc="v_e")
                    r2 = p.op("vector", "tensor_add", resf[:, 1536:2048],
                              ps_b[:, 0:512], xrs[:, 1536:2048], waits=[mm1, xr], inc="v_e")
                    p.bank_done("a", r1)
                    p.bank_done("b", r2)
                    emit_ln(p, C, resf[:, :], xnew[:, :], waits=[r0, r1, r2])
                    xn_prev = p.dma("sync", xN[b, tt * 128:(tt + 1) * 128, :], xnew[:, :],
                                    waits=[Tok("v_e", p.led["v_e"])])
                htring.release(hti, mm0, mm1)
        p.fence()
    emit_xN_to_xT(p, C, b)


def emit_precast(p, C, pairs, first=True):
    if first:
        C["wtok"] = {}
    for (name, srcap, dstap, rows, cols) in pairs:
        toks = []
        for r0 in range(0, rows, 1024):
            h = min(1024, rows - r0)
            toks.append(p.dma("gpsimd", dstap[r0:r0 + h, :], srcap[r0:r0 + h, :]))
        C["wtok"][name] = toks



def emit_elu(p, C, dst, src_ps, bias_col, tmp1, tmp2, waits=()):
    ev = p.op("scalar", "activation", tmp1, src_ps, AF.Identity, bias=bias_col,
              waits=list(waits), inc="s_e")
    t1 = p.op("vector", "tensor_scalar_min", tmp2, tmp1, 0.0, waits=[ev], inc="v_e")
    t2 = p.op("scalar", "activation", tmp2, tmp2, AF.Exp, waits=[t1], inc="s_e")
    t3 = p.op("vector", "tensor_scalar_max", tmp1, tmp1, 0.0, waits=[t2], inc="v_e")
    t4 = p.op("vector", "scalar_tensor_tensor", dst, tmp2, -1.0, tmp1,
              ALU.add, ALU.add, waits=[t3], inc="v_e")
    return t4


def emit_P4(p, C, b):
    nc = C["nc"]
    xT, xN = C["xT"], C["xN"]
    Wqk, Wv, Wu, Wo1 = C["Wqk"], C["Wv"], C["Wu"], C["Wo1"]
    bqk_t, bv_r, bu_r, bo_r, lng_r, lnb_r = (C["bqk_t"], C["bv"], C["bu"],
                                             C["bo1"], C["ln_g"], C["ln_b"])
    ident_sb, ones_sb = C["ident_sb"], C["ones_sb"]
    ps_a, ps_b, ps_c, ps_d = C["ps_a"], C["ps_b"], C["ps_c"], C["ps_d"]
    ps_o, ps_s, ps_t, ps_t2 = C["ps_o"], C["ps_s"], C["ps_t"], C["ps_t2"]

    with contextlib.ExitStack() as sco:
        QT = sco.enter_context(sbt(nc, "QT", [128, 16 * S], BF16))
        uN = C["uN"]
        kv_sb = sco.enter_context(sbt(nc, "kv_sb", [128, 16 * 128], BF16))
        browb = sco.enter_context(sbt(nc, "browb", [1, 4 * D], BF16))
        bqk_sb = sco.enter_context(sbt(nc, "bqk_sb", [128, 16], F32))
        lngb = sco.enter_context(sbt(nc, "lngb", [128, 2 * D], BF16))
        browstack = contextlib.ExitStack()
        brow = browstack.enter_context(sbt(nc, "brow", [1, D], BF16))
        bq_t = p.dma("gpsimd", bqk_sb[:, :], bqk_t[:, :])
        br_t = p.dma("gpsimd", browb[0:1, 0:D], bv_r[None, :])
        br_u = p.dma("gpsimd", browb[0:1, D:2 * D], bu_r[None, :])
        br_o = p.dma("gpsimd", browb[0:1, 2 * D:3 * D], bo_r[None, :])
        br_g = p.dma("gpsimd", browb[0:1, 3 * D:4 * D], lng_r[None, :])
        br_b = p.dma("gpsimd", brow[0:1, 0:D], lnb_r[None, :])
        for (srow, dst0, wt0) in ((browb[0:1, 3 * D:4 * D], 0, br_g),
                                  (brow[0:1, 0:D], D, br_b)):
            for cb in range(4):
                m1 = p.op("tensor", "matmul", ps_s[:, 0:512], ones_sb[0:1, :],
                          srow[0:1, cb * 512:(cb + 1) * 512],
                          start=True, stop=True,
                          waits=[wt0] + p.bank_use("s"), inc="t_e")
                cpv = p.op("vector", "tensor_copy", lngb[:, dst0 + cb * 512: dst0 + cb * 512 + 512],
                     ps_s[:, 0:512], waits=[m1], inc="v_e")
                p.bank_done("s", cpv)

        p.fence()
        browstack.close()
        with contextlib.ExitStack() as sA:
            xg = sA.enter_context(sbt(nc, "xgA", [128, 16 * S], BF16))
            wqkb = sA.enter_context(sbt(nc, "wqkb", [128, 2 * 16 * 128], BF16))
            tA = sA.enter_context(sbt(nc, "tA", [128, 2 * 512], BF16))
            tB = sA.enter_context(sbt(nc, "tB", [128, 2 * 512], BF16))
            wring = Ring(2)
            e4_prev = {0: None, 1: None}
            xw = p.dma("gpsimd", xg[:].rearrange("p (c m) -> p c m", c=16),
                       xT[b].rearrange("(c p) m -> p c m", p=128),
                       waits=[Tok("v_e", p.led["v_e"])])
            for oc in range(16):
                wi, wfr = wring.acquire()
                woff = wi * 2048
                wt = p.dma("gpsimd", wqkb[:, woff:woff + 2048].rearrange("p (c m) -> p c m", c=16),
                           C["Wqk_bf"][:, oc * 128:(oc + 1) * 128].rearrange("(c p) m -> p c m", p=128),
                           waits=wfr + C["wtok"]["Wqk"])
                banks = (ps_a, ps_b, ps_c, ps_d)
                mmx = [None] * 4
                for ic in range(16):
                    for gt in range(4):
                        mmx[gt] = p.op("tensor", "matmul", banks[gt][:, 0:512],
                                  wqkb[:, woff + ic * 128: woff + ic * 128 + 128],
                                  xg[:, ic * S + gt * 512: ic * S + gt * 512 + 512],
                                  start=(ic == 0), stop=(ic == 15),
                                  waits=([wt, xw] + p.bank_use("abcd"[gt])
                                         if ic == 0 else []),
                                  inc=("t_e" if ic == 15 else None))
                e4s = []
                for gt in range(4):
                    par = gt % 2
                    e4 = emit_elu(p, C, QT[:, oc * S + gt * 512: oc * S + gt * 512 + 512],
                                  banks[gt][:, 0:512], bqk_sb[:, oc:oc + 1],
                                  tA[:, par * 512:(par + 1) * 512],
                                  tB[:, par * 512:(par + 1) * 512],
                                  waits=[mmx[gt], bq_t] +
                                        ([e4_prev[par]] if e4_prev[par] else []))
                    e4_prev[par] = e4
                    p.bank_done("abcd"[gt], e4)
                    e4s.append(e4)
                wring.release(wi, *e4s)
            p.fence()

        with contextlib.ExitStack() as sB:
            xt2 = sB.enter_context(sbt(nc, "xt2b", [128, 2 * 16 * 128], BF16))
            wub = sB.enter_context(sbt(nc, "wub", [128, 16 * 2048], BF16))
            untile = sB.enter_context(sbt(nc, "untile", [128, D], BF16))
            wt = p.dma("gpsimd", wub[:].rearrange("p (c m) -> p c m", c=16),
                       C["Wu_bf"].rearrange("(c p) m -> p c m", p=128),
                       waits=C["wtok"]["Wu"])
            xring = Ring(2)
            un_prev = None
            for tt in range(16):
                xi, xfr = xring.acquire()
                xoff = xi * 2048
                xw = p.dma("gpsimd", xt2[:, xoff:xoff + 2048].rearrange("p (c m) -> p c m", c=16),
                           xT[b, :, tt * 128:(tt + 1) * 128].rearrange("(c p) m -> p c m", p=128),
                           waits=xfr)
                mmf = []
                for ic in range(16):
                    woff = ic * 2048
                    for cb in range(4):
                        psx = (ps_a, ps_b, ps_c, ps_d)[cb]
                        mm = p.op("tensor", "matmul", psx[:, 0:512],
                                  xt2[:, xoff + ic * 128: xoff + ic * 128 + 128],
                                  wub[:, woff + cb * 512: woff + cb * 512 + 512],
                                  start=(ic == 0), stop=False,
                                  waits=([wt, xw] + p.bank_use("abcd"[cb])
                                         if ic == 0 else []),
                                  inc=("t_e" if cb == 3 else None))
                        if ic == 15:
                            mmf.append(mm)
                evs = []
                for cb in range(4):
                    psx = (ps_a, ps_b, ps_c, ps_d)[cb]
                    mb = p.op("tensor", "matmul", psx[:, 0:512], ones_sb[0:1, :],
                              browb[0:1, D + cb * 512: D + cb * 512 + 512],
                              start=False, stop=True, waits=[br_u], inc="t_e")
                    ev = p.op("scalar", "activation",
                              untile[:, cb * 512:(cb + 1) * 512],
                              psx[:, 0:512], AF.Silu,
                              waits=[mb] + ([un_prev] if (cb == 0 and un_prev) else []),
                              inc="s_e")
                    p.bank_done("abcd"[cb], ev)
                    evs.append(ev)
                un_prev = p.dma("sync", uN[b, tt * 128:(tt + 1) * 128, :], untile[:, :],
                                waits=evs)
                xring.release(xi, *evs)
            p.fence()

        with contextlib.ExitStack() as sC:
            xt3 = sC.enter_context(sbt(nc, "xt3", [128, 2 * 16 * 128], BF16))
            wvb = sC.enter_context(sbt(nc, "wvb", [128, 16 * 2048], BF16))
            vt2 = sC.enter_context(sbt(nc, "vt", [128, 2 * D], BF16))
            kt2 = sC.enter_context(sbt(nc, "kt", [128, 2 * D], BF16))
            wvt = p.dma("gpsimd", wvb[:].rearrange("p (c m) -> p c m", c=16),
                        C["Wv_bf"].rearrange("(c p) m -> p c m", p=128),
                        waits=C["wtok"]["Wv"])
            xring = Ring(2)
            kvps = (ps_a, ps_b, ps_c, ps_d)
            for tt in range(16):
                vt = vt2[:, (tt % 2) * D:(tt % 2) * D + D]
                kt = kt2[:, (tt % 2) * D:(tt % 2) * D + D]
                xi, xfr = xring.acquire()
                xoff = xi * 2048
                xw = p.dma("gpsimd", xt3[:, xoff:xoff + 2048].rearrange("p (c m) -> p c m", c=16),
                           xT[b, :, tt * 128:(tt + 1) * 128].rearrange("(c p) m -> p c m", p=128),
                           waits=xfr)
                evs = []
                for half in range(2):
                    mm = None
                    for ic in range(16):
                        woff = ic * 2048 + half * 1024
                        for cb in range(2):
                            psx = (ps_o, ps_s)[cb]
                            key = ("o", "s")[cb]
                            mm = p.op("tensor", "matmul", psx[:, 0:512],
                                      xt3[:, xoff + ic * 128: xoff + ic * 128 + 128],
                                      wvb[:, woff + cb * 512: woff + cb * 512 + 512],
                                      start=(ic == 0), stop=False,
                                      waits=([wvt, xw] + p.bank_use(key)
                                             if ic == 0 else []),
                                      inc=("t_e" if cb == 1 else None))
                    for cb in range(2):
                        psx = (ps_o, ps_s)[cb]
                        mb = p.op("tensor", "matmul", psx[:, 0:512], ones_sb[0:1, :],
                                  browb[0:1, half * 1024 + cb * 512: half * 1024 + cb * 512 + 512],
                                  start=False, stop=True, waits=[br_t], inc="t_e")
                        ev = p.op("scalar", "activation",
                                  vt[:, half * 1024 + cb * 512: half * 1024 + cb * 512 + 512],
                                  psx[:, 0:512], AF.Silu, waits=[mb], inc="s_e")
                        p.bank_done(("o", "s")[cb], ev)
                        evs.append(ev)
                cps = []
                for dc in range(16):
                    key = "t" if dc % 2 == 0 else "t2"
                    pst = ps_t if dc % 2 == 0 else ps_t2
                    tp = p.op("tensor", "transpose", pst[:, 0:128],
                              QT[:, dc * S + tt * 128: dc * S + tt * 128 + 128],
                              ident_sb[:, :], waits=p.bank_use(key), inc="t_e")
                    cp = p.op("vector", "tensor_copy", kt[:, dc * 128:(dc + 1) * 128],
                              pst[:, 0:128], waits=[tp], inc="v_e")
                    p.bank_done(key, cp)
                    cps.append(cp)
                for h in range(H):
                    psx = kvps[h // 4]
                    co = (h % 4) * 128
                    bw = p.bank_use("abcd"[h // 4]) if (tt == 0 and h % 4 == 0) else []
                    p.op("tensor", "matmul", psx[:, co:co + 128],
                         kt[:, h * 128:(h + 1) * 128], vt[:, h * 128:(h + 1) * 128],
                         start=(tt == 0 and h % 4 == 0), stop=(tt == 15 and h % 4 == 3),
                         skip_group_check=True,
                         waits=[cps[h], evs[h // 4]] + bw,
                         inc=("t_e" if tt == 15 else None))
                xring.release(xi, *cps)
            for h in range(H):
                psx = kvps[h // 4]
                co = (h % 4) * 128
                cpv = p.op("vector", "tensor_copy", kv_sb[:, h * 128:(h + 1) * 128],
                     psx[:, co:co + 128],
                     waits=[Tok("t_e", p.led["t_e"])], inc="v_e")
                if h % 4 == 3:
                    p.bank_done("abcd"[h // 4], cpv)
            p.fence()

        with contextlib.ExitStack() as sD:
            oTg = sD.enter_context(sbt(nc, "oTg", [128, 16 * 256], BF16))
            oN2 = sD.enter_context(sbt(nc, "oN", [128, 2 * D], BF16))
            o2T2 = sD.enter_context(sbt(nc, "o2T", [128, 2 * 16 * 128], BF16))
            wob = sD.enter_context(sbt(nc, "wob4", [128, 16 * 2048], BF16))
            resf = sD.enter_context(sbt(nc, "resf4", [128, D], F32))
            lnsc = resf
            wot = p.dma("gpsimd", wob[:].rearrange("p (c m) -> p c m", c=16),
                        C["Wo1_bf"].rearrange("(c p) m -> p c m", p=128),
                        waits=C["wtok"]["Wo1"])
            xrs = sD.enter_context(sbt(nc, "xrs4", [128, D], BF16))
            xnew = sD.enter_context(sbt(nc, "xnew4", [128, D], BF16))
            xn_prev = None
            for gt in range(8):
                oTg_free = Tok("t_e", p.led["t_e"])
                for h in range(H):
                    key = "o" if h % 2 == 0 else "s"
                    ps = ps_o if h % 2 == 0 else ps_s
                    mm = p.op("tensor", "matmul", ps[:, 0:256],
                              kv_sb[:, h * 128:(h + 1) * 128],
                              QT[:, h * S + gt * 256: h * S + gt * 256 + 256],
                              start=True, stop=True,
                              waits=p.bank_use(key) + [oTg_free],
                              inc="t_e")
                    cpv = p.op("vector", "tensor_copy", oTg[:, h * 256:(h + 1) * 256],
                         ps[:, 0:256], waits=[mm], inc="v_e")
                    p.bank_done(key, cpv)
                for j in range(2):
                    tt = gt * 2 + j
                    oN = oN2[:, (tt % 2) * D:(tt % 2) * D + D]
                    o2T = o2T2[:, (tt % 2) * D:(tt % 2) * D + D]
                    cps = []
                    oTg_tok = Tok("v_e", p.led["v_e"])
                    for dc in range(16):
                        key = "t" if dc % 2 == 0 else "t2"
                        pst = ps_t if dc % 2 == 0 else ps_t2
                        tp = p.op("tensor", "transpose", pst[:, 0:128],
                                  oTg[:, dc * 256 + j * 128: dc * 256 + j * 128 + 128],
                                  ident_sb[:, :], waits=[oTg_tok] + p.bank_use(key),
                                  inc="t_e")
                        cp = p.op("vector", "tensor_copy", oN[:, dc * 128:(dc + 1) * 128],
                                  pst[:, 0:128], waits=[tp], inc="v_e")
                        p.bank_done(key, cp)
                        cps.append(cp)
                    stat = C["stat"]
                    s1, s2, mu, rst = stat[:, 0:1], stat[:, 1:2], stat[:, 2:3], stat[:, 3:4]
                    t1 = p.op("vector", "tensor_reduce", s1, oN[:, :], AX.X, ALU.add,
                              waits=cps, inc="v_e")
                    t2 = p.op("scalar", "activation", o2T[:, :], oN[:, :], AF.Square,
                              accum_out=s2, waits=cps +
                              [Tok("t_e", p.led["t_e"])], inc="s_e")
                    tm = p.op("vector", "tensor_scalar", mu, s1, 1.0 / D, None, ALU.mult,
                              waits=[t1], inc="v_e")
                    tv = p.op("vector", "tensor_mul", stat[:, 4:5], mu, mu, waits=[tm], inc="v_e")
                    tv = p.op("vector", "tensor_scalar", stat[:, 4:5], stat[:, 4:5], LN_EPS,
                              None, ALU.subtract, waits=[tv], inc="v_e")
                    tv = p.op("vector", "scalar_tensor_tensor", stat[:, 5:6], s2, 1.0 / D,
                              stat[:, 4:5], ALU.mult, ALU.subtract, waits=[tv, t2], inc="v_e")
                    tv = newton_rsqrt(p, C, rst, stat[:, 5:6], 1, waits=[tv])
                    ta = p.op("vector", "scalar_tensor_tensor", lnsc[:, :], oN[:, :], mu,
                              lngb[:, 0:D], ALU.subtract, ALU.mult,
                              waits=[tv] + ([xn_prev] if xn_prev else []), inc="v_e")
                    tg2 = p.op("vector", "scalar_tensor_tensor", lnsc[:, :], lnsc[:, :], rst,
                              lngb[:, D:2 * D], ALU.mult, ALU.add,
                              waits=[ta], inc="v_e")
                    ut = p.dma("gpsimd", xrs[:, :], uN[b, tt * 128:(tt + 1) * 128, :],
                               waits=[Tok("v_e", p.led["v_e"])])
                    tu = p.op("vector", "tensor_mul", oN[:, :], lnsc[:, :], xrs[:, :],
                              waits=[tg2, ut], inc="v_e")
                    cps2 = []
                    for dc in range(16):
                        key = "t" if dc % 2 == 0 else "t2"
                        pst = ps_t if dc % 2 == 0 else ps_t2
                        tp = p.op("tensor", "transpose", pst[:, 0:128],
                                  oN[:, dc * 128:(dc + 1) * 128], ident_sb[:, :],
                                  waits=[tu] + p.bank_use(key), inc="t_e")
                        cp = p.op("vector", "tensor_copy", o2T[:, dc * 128:(dc + 1) * 128],
                                  pst[:, 0:128], waits=[tp], inc="v_e")
                        p.bank_done(key, cp)
                        cps2.append(cp)
                    xr = p.dma("gpsimd", xrs[:, :], xN[b, tt * 128:(tt + 1) * 128, :],
                               waits=[tu, Tok("v_e", p.led["v_e"])])
                    mmf = []
                    for dc in range(16):
                        woff = dc * 2048
                        for cb in range(4):
                            psx = (ps_a, ps_b, ps_c, ps_d)[cb]
                            mm = p.op("tensor", "matmul", psx[:, 0:512],
                                      o2T[:, dc * 128:(dc + 1) * 128],
                                      wob[:, woff + cb * 512: woff + cb * 512 + 512],
                                      start=(dc == 0), stop=False,
                                      waits=([wot, cps2[-1]] + p.bank_use("abcd"[cb])
                                             if dc == 0 else []),
                                      inc=("t_e" if cb == 3 else None))
                    ra = []
                    for cb in range(4):
                        psx = (ps_a, ps_b, ps_c, ps_d)[cb]
                        mb = p.op("tensor", "matmul", psx[:, 0:512], ones_sb[0:1, :],
                                  browb[0:1, 2 * D + cb * 512: 2 * D + cb * 512 + 512],
                                  start=False, stop=True, waits=[br_o], inc="t_e")
                        t = p.op("vector", "tensor_add", resf[:, cb * 512:(cb + 1) * 512],
                                 psx[:, 0:512], xrs[:, cb * 512:(cb + 1) * 512],
                                 waits=[mb, xr], inc="v_e")
                        p.bank_done("abcd"[cb], t)
                        ra.append(t)
                    emit_ln(p, C, resf[:, :], xnew[:, :],
                            waits=ra + ([xn_prev] if xn_prev else []))
                    xn_prev = p.dma("sync", xN[b, tt * 128:(tt + 1) * 128, :], xnew[:, :],
                                    waits=[Tok("v_e", p.led["v_e"])])
            p.fence()
    emit_xN_to_xT(p, C, b)



def emit_P5(p, C, moe_sched):
    nc = C["nc"]
    xT, xN, y_out = C["xT"], C["xN"], C["y_out"]
    gwT, WT = C["gwT"], C["WT"]
    atoms_p, imp_p, midx, eaw_r = C["atoms_p"], C["imp_p"], C["midx_s"], C["eaw_r"]
    o3T = C["o3T"]
    ident_sb, ones_sb = C["ident_sb"], C["ones_sb"]
    iota_rc_c, iota8_c = C["iota_rc_c"], C["iota8_c"]
    ps_a, ps_b, ps_c, ps_d = C["ps_a"], C["ps_b"], C["ps_c"], C["ps_d"]
    ps_o, ps_s, ps_t, ps_t2 = C["ps_o"], C["ps_s"], C["ps_t"], C["ps_t2"]
    sched, nslots = moe_sched["sched"], moe_sched["nslots"]

    with contextlib.ExitStack() as sco:
        wsb = sco.enter_context(sbt(nc, "wsb", [8, 16384], BF16))
        midx_f = sco.enter_context(sbt(nc, "midxf", [128, 128], F32))
        rows_f = sco.enter_context(sbt(nc, "rowsf", [128, 128], F32))
        cols_i = sco.enter_context(sbt(nc, "colsi", [128, 128], I32))
        iota_rc = sco.enter_context(sbt(nc, "iotarc", [128, 128], F32))
        rsh = sco.enter_context(sbt(nc, "rsh", [128, 1], F32))
        iota8 = sco.enter_context(sbt(nc, "iota8", [128, 8], F32))
        gw_sb = sco.enter_context(sbt(nc, "gw_sb", [128, 16 * 8], BF16))

        it1 = p.dma("sync", iota_rc[:, :], iota_rc_c.ap()[:, :])
        it2 = p.dma("sync", iota8[:, :], iota8_c.ap()[:, :])
        gwt = p.dma("gpsimd", gw_sb[:].rearrange("p (c m) -> p c m", c=16),
                    gwT.rearrange("(c p) m -> p c m", p=128))

        with contextlib.ExitStack() as spro:
            smw = spro.enter_context(sbt(nc, "smw", [8, 8], F32))
            e8 = spro.enter_context(sbt(nc, "e8", [8, 64], F32))
            e8b = spro.enter_context(sbt(nc, "e8b", [8, 64], BF16))
            smT = spro.enter_context(sbt(nc, "smT", [64, 8], BF16))
            atomb = spro.enter_context(sbt(nc, "atomb", [64, 2048], BF16))
            impb = spro.enter_context(sbt(nc, "impb", [8, 2048], F32))
            lt = p.dma("gpsimd", e8[:, :], eaw_r[:, :])
            t1 = p.op("vector", "tensor_reduce", smw[0:8, 0:1], e8[:, :], AX.X, ALU.max,
                      negate=True, waits=[lt], inc="v_e")
            t2 = p.op("scalar", "activation", e8b[:, :], e8[:, :], AF.Exp,
                      bias=smw[0:8, 0:1], accum_out=smw[0:8, 1:2], waits=[t1], inc="s_e")
            t3 = p.op("vector", "reciprocal", smw[0:8, 2:3], smw[0:8, 1:2], waits=[t2], inc="v_e")
            t4 = p.op("vector", "tensor_scalar", e8b[:, :], e8b[:, :], smw[0:8, 2:3], None,
                      ALU.mult, waits=[t3], inc="v_e")
            tmm = p.op("tensor", "transpose", ps_t[0:64, 0:8], e8b[:, :], ident_sb[0:8, 0:8],
                       waits=[t4, Tok("v_e", p.led["v_e"])], inc="t_e")
            tcp = p.op("vector", "tensor_copy", smT[:, :], ps_t[0:64, 0:8], waits=[tmm], inc="v_e")
            for blk in range(8):
                at = p.dma("gpsimd", atomb[:, :], atoms_p[:, blk * 2048:(blk + 1) * 2048],
                           waits=[Tok("v_e", p.led["v_e"])])
                imt = p.dma("gpsimd", impb[:, :], imp_p[:, blk * 2048:(blk + 1) * 2048],
                            waits=[Tok("s_e", p.led["s_e"])])
                for q4 in range(4):
                    mm = p.op("tensor", "matmul", ps_o[0:8, 0:512], smT[:, :],
                              atomb[:, q4 * 512:(q4 + 1) * 512], start=True, stop=True,
                              waits=[at, tcp] + p.bank_use("o"), inc="t_e")
                    sg = p.op("scalar", "activation", impb[0:8, q4 * 512:(q4 + 1) * 512],
                              impb[0:8, q4 * 512:(q4 + 1) * 512], AF.Sigmoid,
                              waits=[imt], inc="s_e")
                    wv = p.op("vector", "tensor_mul",
                         wsb[:, blk * 2048 + q4 * 512: blk * 2048 + q4 * 512 + 512],
                         ps_o[0:8, 0:512], impb[0:8, q4 * 512:(q4 + 1) * 512],
                         waits=[mm, sg], inc="v_e")
                    p.bank_done("o", wv)
            mi = spro.enter_context(sbt(nc, "mi", [128, 128], I32))
            mi2 = spro.enter_context(sbt(nc, "mi2", [128, 128], I32))
            mt = p.dma("gpsimd", mi[:, :], midx[:, :])
            c1 = p.op("vector", "tensor_single_scalar", mi2[:, :], mi[:, :], 11,
                      ALU.arith_shift_right, waits=[mt], inc="v_e")
            c2 = p.op("vector", "tensor_copy", rows_f[:, :], mi2[:, :], waits=[c1], inc="v_e")
            p.op("vector", "tensor_single_scalar", cols_i[:, :], mi[:, :], 2047,
                 ALU.bitwise_and, waits=[c2], inc="v_e")
            p.fence()

        xt_all = sco.enter_context(sbt(nc, "xt_all5", [128, 16 * S], BF16))
        coefT = sco.enter_context(sbt(nc, "coefT", [8, S], BF16))
        xgv = sco.enter_context(sbt(nc, "xgv", [128, nslots * S], BF16))
        xg1 = sco.enter_context(sbt(nc, "xg1", [128, 2 * S], BF16))
        max_nk = max((len(s) for s in sched), default=1)
        rh = sco.enter_context(sbt(nc, "rh", [128, max(4, max_nk) * 128], BF16))
        wtb = sco.enter_context(sbt(nc, "wtb", [128, 2 * 16 * 128], BF16))
        cwk = sco.enter_context(sbt(nc, "cwk", [128, 48], F32))
        cwku = sco.enter_context(sbt(nc, "cwku", [128, 8], mybir.dt.uint32))
        coefb = sco.enter_context(sbt(nc, "coefb", [128, 8], BF16))
        otile = sco.enter_context(sbt(nc, "otile", [128, 512], BF16))
        resf = sco.enter_context(sbt(nc, "resf5", [128, 2 * D], F32))
        onb = sco.enter_context(sbt(nc, "onb", [128, D], BF16))
        xrs = sco.enter_context(sbt(nc, "xrs5", [128, 2 * D], BF16))
        yout_f = resf

        for b in range(NB):
            xw = p.dma("gpsimd", xt_all[:].rearrange("p (c m) -> p c m", c=16),
                       xT[b].rearrange("(c p) m -> p c m", p=128),
                       waits=[Tok("v_e", p.led["v_e"]), Tok("t_e", p.led["t_e"])])
            e3_prev = None
            for tt in range(16):
                mm = None
                for ic in range(16):
                    mm = p.op("tensor", "matmul", ps_o[:, 0:8],
                              xt_all[:, ic * S + tt * 128: ic * S + tt * 128 + 128],
                              gw_sb[:, ic * 8:(ic + 1) * 8],
                              start=(ic == 0), stop=(ic == 15),
                              waits=([xw, gwt, it2] + p.bank_use("o") if ic == 0 else []),
                              inc=("t_e" if ic == 15 else None))
                g0 = p.op("vector", "tensor_scalar_min", cwk[:, 0:8], ps_o[:, 0:8],
                          50.0, waits=[mm] + ([e3_prev] if e3_prev else []), inc="v_e")
                p.bank_done("o", g0)
                g1 = p.op("vector", "tensor_scalar_max", cwk[:, 0:8], cwk[:, 0:8],
                          -50.0, waits=[g0], inc="v_e")
                g2 = p.op("vector", "max", cwk[:, 8:16], cwk[:, 0:8], waits=[g1], inc="v_e")
                g3 = p.op("vector", "max_index", cwku[:, 0:8], cwk[:, 8:16], cwk[:, 0:8],
                          waits=[g2], inc="v_e")
                g4 = p.op("vector", "tensor_sub", cwk[:, 16:17], cwk[:, 9:10], cwk[:, 8:9],
                          waits=[g3], inc="v_e")
                g5 = p.op("scalar", "activation", cwk[:, 17:18], cwk[:, 16:17], AF.Exp,
                          waits=[g4], inc="s_e")
                g6 = p.op("vector", "tensor_scalar", cwk[:, 18:19], cwk[:, 17:18], 1.0, None,
                          ALU.add, waits=[g5], inc="v_e")
                g7 = p.op("vector", "reciprocal", cwk[:, 19:20], cwk[:, 18:19],
                          waits=[g6], inc="v_e")
                g8 = p.op("vector", "tensor_scalar", cwk[:, 20:21], cwk[:, 19:20], -1.0, 1.0,
                          ALU.mult, ALU.add, waits=[g7], inc="v_e")
                g9 = p.op("vector", "tensor_copy", cwk[:, 21:22], cwku[:, 0:1],
                          waits=[g3], inc="v_e")
                g10 = p.op("vector", "tensor_copy", cwk[:, 22:23], cwku[:, 1:2],
                           waits=[g3], inc="v_e")
                e1 = p.op("vector", "tensor_scalar", cwk[:, 24:32], iota8[:, :],
                          cwk[:, 21:22], None, ALU.is_equal, waits=[g9], inc="v_e")
                e1b = p.op("vector", "tensor_scalar", cwk[:, 32:40], cwk[:, 24:32],
                           cwk[:, 19:20], None, ALU.mult, waits=[e1, g7], inc="v_e")
                e2 = p.op("vector", "tensor_scalar", cwk[:, 24:32], iota8[:, :],
                          cwk[:, 22:23], None, ALU.is_equal, waits=[g10, e1b], inc="v_e")
                e3 = p.op("vector", "scalar_tensor_tensor", coefb[:, :], cwk[:, 24:32],
                          cwk[:, 20:21], cwk[:, 32:40], ALU.mult, ALU.add,
                          waits=[e2, g8], inc="v_e")
                e3_prev = e3
                tmc = p.op("tensor", "transpose", ps_t[0:8, 0:128], coefb[:, :],
                           ident_sb[:, :], waits=[e3] + p.bank_use("t"), inc="t_e")
                ctc = p.op("vector", "tensor_copy", coefT[:, tt * 128:(tt + 1) * 128],
                     ps_t[0:8, 0:128], waits=[tmc], inc="v_e")
                p.bank_done("t", ctc)
            computed = {}
            slot_readers = {}
            st_toks = []
            xg1ring = Ring(2)
            wring = Ring(2)
            st_prev = None
            for rc in range(16):
                for k in sched[rc]:
                    if k in computed:
                        continue
                    slot = k % nslots
                    xgi, xgfr = xg1ring.acquire()
                    gt = p.dma("gpsimd", xg1[:, xgi * S:(xgi + 1) * S], C["xT_full"],
                               waits=xgfr + slot_readers.pop(slot, []),
                               indirect_in=cols_i[:, k:k + 1], indirect_axis=1,
                               element_offset=b * D * S)
                    vv = []
                    for tbp in range(4):
                        vkey = "o" if tbp % 2 == 0 else "s"
                        vps = ps_o if tbp % 2 == 0 else ps_s
                        mmv = p.op("tensor", "matmul", vps[:, 0:512],
                                   wsb[:, k * 128:(k + 1) * 128],
                                   coefT[:, tbp * 512:(tbp + 1) * 512],
                                   start=True, stop=True,
                                   waits=p.bank_use(vkey), inc="t_e")
                        xv = p.op("vector", "tensor_mul",
                                  xgv[:, slot * S + tbp * 512: slot * S + tbp * 512 + 512],
                                  xg1[:, xgi * S + tbp * 512: xgi * S + tbp * 512 + 512],
                                  vps[:, 0:512],
                                  waits=[mmv, gt], inc="v_e")
                        p.bank_done(vkey, xv)
                        vv.append(xv)
                    computed[k] = vv[-1]
                    xg1ring.release(xgi, *vv)
                rhtoks = []
                rsp = None
                for (j, k) in enumerate(sched[rc]):
                    rsp = p.op("vector", "tensor_single_scalar", rsh[:, 0:1],
                               rows_f[:, k:k + 1], float(rc * 128), ALU.subtract,
                               waits=[Tok("t_e", p.led["t_e"])] +
                                     ([rhtoks[-1]] if rhtoks else []),
                               inc="v_e")
                    rhtoks.append(p.op("vector", "tensor_scalar",
                                       rh[:, j * 128:(j + 1) * 128],
                                       iota_rc[:, :],
                                       rsh[:, 0:1], None, ALU.is_equal,
                                       waits=[it1, rsp], inc="v_e"))
                wi, wfr = wring.acquire()
                woff = wi * 2048
                wt = p.dma("gpsimd", wtb[:, woff:woff + 2048].rearrange("p (c m) -> p c m", c=16),
                           C["WT_bf"][:, rc * 128:(rc + 1) * 128].rearrange("(c p) m -> p c m", p=128),
                           waits=wfr + C["wtok"]["WT"])
                lastmm = []
                for tb in range(4):
                    psx = (ps_a, ps_b, ps_c, ps_d)[tb]
                    mm = None
                    nk = len(sched[rc])
                    for ic in range(16):
                        mm = p.op("tensor", "matmul", psx[:, 0:512],
                                  wtb[:, woff + ic * 128: woff + ic * 128 + 128],
                                  xt_all[:, ic * S + tb * 512: ic * S + tb * 512 + 512],
                                  start=(ic == 0), stop=(ic == 15 and nk == 0),
                                  waits=([wt, xw] + p.bank_use("abcd"[tb]) if ic == 0 else []),
                                  inc=("t_e" if (ic == 15 and nk == 0) else None))
                    for (j, k) in enumerate(sched[rc]):
                        slot = k % nslots
                        mm = p.op("tensor", "matmul", psx[:, 0:512],
                                  rh[:, j * 128:(j + 1) * 128],
                                  xgv[:, slot * S + tb * 512: slot * S + tb * 512 + 512],
                                  start=False, stop=(j == nk - 1),
                                  waits=[rhtoks[j], computed[k]],
                                  inc="t_e")
                        slot_readers.setdefault(slot, []).append(mm)
                    ev = p.op("vector", "tensor_copy", otile[:, 0:512], psx[:, 0:512],
                              waits=[mm] + ([st_prev] if st_prev else []),
                              inc="v_e")
                    p.bank_done("abcd"[tb], ev)
                    st_prev = p.dma("sync", o3T[rc * 128:(rc + 1) * 128, tb * 512:(tb + 1) * 512],
                                    otile[:, 0:512], waits=[ev])
                    st_toks.append(st_prev)
                    lastmm.append(mm)
                wring.release(wi, *lastmm)
            o3_done = list(st_toks)
            stat = C["stat"]
            fin_prev = {}
            for tt in range(16):
                par = tt % 2
                onbS = onb[:, :]
                xrsS = xrs[:, par * D:(par + 1) * D]
                resS = resf[:, par * D:(par + 1) * D]
                youtS = yout_f[:, par * D:(par + 1) * D]
                pv = fin_prev.get(par, {})
                tpd = p.dma("scalar", onbS, o3T[:, tt * 128:(tt + 1) * 128],
                            transpose=True,
                            waits=o3_done + fin_prev.get(1 - par, {}).get("ra", []) +
                                  pv.get("ra", []))
                xr = p.dma("gpsimd", xrsS, xN[b, tt * 128:(tt + 1) * 128, :],
                           waits=pv.get("sq", []))
                s1 = stat[:, 8 + par:9 + par]
                s2 = stat[:, 10 + par:11 + par]
                ra = p.op("vector", "scalar_tensor_tensor", resS, onbS, 1.0, xrsS,
                          ALU.mult, ALU.add, accum_out=s1,
                          waits=[tpd, xr] + pv.get("ap", []) + pv.get("yst", []),
                          inc="v_e")
                sq = p.op("scalar", "activation", xrsS, resS, AF.Square,
                          accum_out=s2, waits=[ra], inc="s_e")
                mu = stat[:, 12 + par:13 + par]
                rst = stat[:, 14 + par:15 + par]
                tm = p.op("vector", "tensor_scalar", mu, s1, 1.0 / D, None, ALU.mult,
                          waits=[ra] + pv.get("nw", []), inc="v_e")
                tmpc = stat[:, 4 + par:5 + par]
                varc = stat[:, 6 + par:7 + par]
                tv = p.op("vector", "tensor_mul", tmpc, mu, mu, waits=[tm], inc="v_e")
                tv = p.op("vector", "tensor_scalar", tmpc, tmpc, LN_EPS,
                          None, ALU.subtract, waits=[tv], inc="v_e")
                tv = p.op("vector", "scalar_tensor_tensor", varc, s2, 1.0 / D,
                          tmpc, ALU.mult, ALU.subtract, waits=[tv, sq], inc="v_e")
                nw = newton_rsqrt(p, C, rst, varc, 1, waits=[tv])
                ap = p.op("vector", "tensor_scalar", youtS, resS, mu, rst,
                          ALU.subtract, ALU.mult,
                          waits=[nw] + pv.get("yst", []), inc="v_e")
                yst = p.dma("sync", y_out[b, tt * 128:(tt + 1) * 128, :], youtS,
                            waits=[ap])
                fin_prev[par] = dict(ra=[ra], sq=[sq], ap=[ap], yst=[yst], nw=[nw])
            p.fence()


def dbg_dump(p, C, name, sbuf_ap, rows, cols):
    nc = C["nc"]
    t = nc.dram_tensor(name, [rows, cols], mybir.dt.float32 if sbuf_ap.dtype == F32 else BF16,
                       kind="ExternalOutput").ap()
    p.dma("sync", t[:, :], sbuf_ap, waits=p.all_toks())
    p.fence()


def build_program(moe_sched=None, debug=()):
    nc = bass.Bass("TRN2", target_bir_lowering=False)
    p = Plan(nc)

    x_in = nc.dram_tensor("x", [NB, S, D], F32, kind="ExternalInput").ap()
    wqT = nc.dram_tensor("wqT", [D, D], F32, kind="ExternalInput").ap()
    wkvT = nc.dram_tensor("wkvT", [D, R], F32, kind="ExternalInput").ap()
    wkrT = nc.dram_tensor("wkrT", [D, RD], F32, kind="ExternalInput").ap()
    wkuT = nc.dram_tensor("wkuT", [R, H * ND], F32, kind="ExternalInput").ap()
    wvuT = nc.dram_tensor("wvuT", [R, D], F32, kind="ExternalInput").ap()
    woT = nc.dram_tensor("woT", [D, D], F32, kind="ExternalInput").ap()
    wgate = nc.dram_tensor("wgate", [D, I8K], F32, kind="ExternalInput").ap()
    wup = nc.dram_tensor("wup", [D, I8K], F32, kind="ExternalInput").ap()
    wdown = nc.dram_tensor("wdown", [I8K, D], F32, kind="ExternalInput").ap()
    Wqk = nc.dram_tensor("Wqk", [D, D], F32, kind="ExternalInput").ap()
    Wv = nc.dram_tensor("Wv", [D, D], F32, kind="ExternalInput").ap()
    Wu = nc.dram_tensor("Wu", [D, D], F32, kind="ExternalInput").ap()
    Wo1 = nc.dram_tensor("Wo1", [D, D], F32, kind="ExternalInput").ap()
    bqk_t = nc.dram_tensor("bqk_t", [128, 16], F32, kind="ExternalInput").ap()
    bv = nc.dram_tensor("bv", [D], F32, kind="ExternalInput").ap()
    bu = nc.dram_tensor("bu", [D], F32, kind="ExternalInput").ap()
    bo1 = nc.dram_tensor("bo1", [D], F32, kind="ExternalInput").ap()
    ln_g = nc.dram_tensor("ln_g", [D], F32, kind="ExternalInput").ap()
    ln_b = nc.dram_tensor("ln_b", [D], F32, kind="ExternalInput").ap()
    gwT = nc.dram_tensor("gwT", [D, 8], F32, kind="ExternalInput").ap()
    WT = nc.dram_tensor("WT", [D, D], F32, kind="ExternalInput").ap()
    atoms_p = nc.dram_tensor("atoms_p", [64, 16384], F32, kind="ExternalInput").ap()
    eaw_r = nc.dram_tensor("eaw_r", [8, 64], F32, kind="ExternalInput").ap()
    imp_p = nc.dram_tensor("imp_p", [8, 16384], F32, kind="ExternalInput").ap()
    midx_s = nc.dram_tensor("midx_s", [128, 128], I32, kind="ExternalInput").ap()

    y_out = nc.dram_tensor("y", [NB, S, D], F32, kind="ExternalOutput").ap()

    ident_c = nc.inline_tensor(np.eye(128, dtype=ml_dtypes.bfloat16), name="c_ident")
    ones_c = nc.inline_tensor(np.ones((128, 128), dtype=ml_dtypes.bfloat16), name="c_ones")
    pos = np.arange(S, dtype=np.float64)
    freqs = 1.0 / (10000.0 ** (np.arange(0, RD, 2, dtype=np.float64) / RD))
    ang = pos[None, :] * freqs[:, None]
    cos_np = np.repeat(np.cos(ang), 2, axis=0).astype(np.float32)
    sin_np = np.repeat(np.sin(ang), 2, axis=0).astype(np.float32)
    cos_c = nc.inline_tensor(bfc(cos_np), name="c_cos")
    sin_c = nc.inline_tensor(bfc(sin_np), name="c_sin")
    Rm = np.zeros((RD, RD), dtype=np.float32)
    for i in range(0, RD, 2):
        Rm[i, i + 1] = -1.0
        Rm[i + 1, i] = 1.0
    rot_c = nc.inline_tensor(bfc(Rm.T.copy()), name="c_rot")
    mvars = np.zeros((4, 128, 512), dtype=np.float32)
    for dlt in range(4):
        for pk in range(128):
            mvars[dlt, pk, :dlt * 128 + pk] = NEG_BIG
    mask_c = nc.inline_tensor(bfc(mvars.reshape(4 * 128, 512)), name="c_mask")
    iota_rc_c = nc.inline_tensor(np.tile(np.arange(128, dtype=np.float32), (128, 1)),
                                 name="c_iotarc")
    iota8_c = nc.inline_tensor(np.tile(np.arange(8, dtype=np.float32), (128, 1)),
                               name="c_iota8")

    xT_h = nc.dram_tensor("xT", [NB, D, S], BF16,
                          kind=("ExternalOutput" if "xT" in debug else "Internal"))
    xT = xT_h.ap()
    xN = nc.dram_tensor("xN", [NB, S, D], BF16,
                        kind=("ExternalOutput" if "xN" in debug else "Internal")).ap()
    hT = nc.dram_tensor("hT", [NB, I8K, S], BF16, kind="Internal").ap()
    uN = nc.dram_tensor("uN", [NB, S, D], BF16, kind="Internal").ap()
    o3T = nc.dram_tensor("o3T", [D, S], BF16, kind="Internal").ap()
    wdown_bf = nc.dram_tensor("wdown_bf", [I8K, D], BF16, kind="Internal").ap()
    wqT_bf = nc.dram_tensor("wqT_bf", [D, D], BF16, kind="Internal").ap()
    woT_bf = nc.dram_tensor("woT_bf", [D, D], BF16, kind="Internal").ap()
    wgate_bf = nc.dram_tensor("wgate_bf", [D, I8K], BF16, kind="Internal").ap()
    wup_bf = nc.dram_tensor("wup_bf", [D, I8K], BF16, kind="Internal").ap()
    Wqk_bf = nc.dram_tensor("Wqk_bf", [D, D], BF16, kind="Internal").ap()
    Wv_bf = nc.dram_tensor("Wv_bf", [D, D], BF16, kind="Internal").ap()
    Wu_bf = nc.dram_tensor("Wu_bf", [D, D], BF16, kind="Internal").ap()
    Wo1_bf = nc.dram_tensor("Wo1_bf", [D, D], BF16, kind="Internal").ap()
    WT_bf = nc.dram_tensor("WT_bf", [D, D], BF16, kind="Internal").ap()

    consts = {}
    ident_sb = nc.alloc_sbuf_tensor("ident_sb", [128, 128], BF16)
    ones_sb = nc.alloc_sbuf_tensor("ones_sb", [128, 128], BF16)
    cos_sb = nc.alloc_sbuf_tensor("cos_sb", [128, S], BF16)
    sin_sb = nc.alloc_sbuf_tensor("sin_sb", [128, S], BF16)
    rot_sb = nc.alloc_sbuf_tensor("rot_sb", [128, 64], BF16)


    magic_c = nc.inline_tensor(np.full((128, 64), MAGIC, dtype=np.int32), name="c_magic")
    pm32 = nc.alloc_sbuf_tensor("pm32", [128, 32], F32)
    pmt32 = nc.alloc_sbuf_tensor("pmt32", [128, 32], F32)
    nr1 = nc.alloc_sbuf_tensor("nr1", [128, 64], F32)
    nr2 = nc.alloc_sbuf_tensor("nr2", [128, 64], F32)
    magic_sb = nc.alloc_sbuf_tensor("magic_sb", [128, 64], I32)
    stat = nc.alloc_sbuf_tensor("stat", [128, 16], F32)

    ps_a = nc.alloc_psum_tensor("ps_a", [128, 512], F32)
    ps_b = nc.alloc_psum_tensor("ps_b", [128, 512], F32)
    ps_c = nc.alloc_psum_tensor("ps_c", [128, 512], F32)
    ps_d = nc.alloc_psum_tensor("ps_d", [128, 512], F32)
    ps_o = nc.alloc_psum_tensor("ps_o", [128, 512], F32)
    ps_s = nc.alloc_psum_tensor("ps_s", [128, 512], F32)
    ps_t = nc.alloc_psum_tensor("ps_t", [128, 512], BF16)
    ps_t2 = nc.alloc_psum_tensor("ps_t2", [128, 512], BF16)

    C = dict(nc=nc, p=p, x_in=x_in, y_out=y_out,
             wqT=wqT, wkvT=wkvT, wkrT=wkrT, wkuT=wkuT, wvuT=wvuT, woT=woT,
             wgate=wgate, wup=wup, wdown=wdown, hT=hT, wdown_bf=wdown_bf,
             wqT_bf=wqT_bf, woT_bf=woT_bf, wgate_bf=wgate_bf, wup_bf=wup_bf,
             Wqk_bf=Wqk_bf, Wv_bf=Wv_bf, Wu_bf=Wu_bf, Wo1_bf=Wo1_bf, WT_bf=WT_bf,
             Wqk=Wqk, Wv=Wv, Wu=Wu, Wo1=Wo1, bqk_t=bqk_t, bv=bv, bu=bu, bo1=bo1,
             ln_g=ln_g, ln_b=ln_b, uN=uN, o3T=o3T,
             gwT=gwT, WT=WT, atoms_p=atoms_p, eaw_r=eaw_r, imp_p=imp_p, midx_s=midx_s,
             iota_rc_c=iota_rc_c, iota8_c=iota8_c,
             xT=xT, xN=xN, xT_full=xT_h.ap(),
             ident_sb=ident_sb, ones_sb=ones_sb, cos_sb=cos_sb, sin_sb=sin_sb,
             rot_sb=rot_sb, mask_c=mask_c, debug=debug,
             pm32=pm32, pmt32=pmt32, nr1=nr1, nr2=nr2, magic_sb=magic_sb,
             stat=stat,
             ps_a=ps_a, ps_b=ps_b, ps_c=ps_c, ps_d=ps_d, ps_o=ps_o, ps_s=ps_s,
             ps_t=ps_t, ps_t2=ps_t2)

    p.dma("sync", ident_sb[:], ident_c.ap()[:, :])
    p.dma("sync", ones_sb[:], ones_c.ap()[:, :])
    p.dma("sync", cos_sb[0:64, :], cos_c.ap()[:, :])
    p.dma("sync", cos_sb[64:128, :], cos_c.ap()[:, :])
    p.dma("sync", sin_sb[0:64, :], sin_c.ap()[:, :])
    p.dma("sync", sin_sb[64:128, :], sin_c.ap()[:, :])
    p.dma("sync", rot_sb[0:64, :], rot_c.ap()[:, :])
    p.dma("sync", rot_sb[64:128, :], rot_c.ap()[:, :])
    p.dma("sync", magic_sb[:], magic_c.ap()[:, :])
    p.fence()

    emit_P0b(p, C, 0)
    PRECAST_PAIRS = [
        ("wq", C["wqT"], C["wqT_bf"], D, D),
        ("wo", C["woT"], C["woT_bf"], D, D),
        ("Wqk", C["Wqk"], C["Wqk_bf"], D, D),
        ("wd", C["wdown"], C["wdown_bf"], I8K, D),
        ("Wv", C["Wv"], C["Wv_bf"], D, D),
        ("Wu", C["Wu"], C["Wu_bf"], D, D),
        ("Wo1", C["Wo1"], C["Wo1_bf"], D, D),
        ("WT", C["WT"], C["WT_bf"], D, D),
    ]
    for b in range(NB):
        with contextlib.ExitStack() as sbatch:
            KAll = sbatch.enter_context(sbt(nc, "KAll", [128, H * S], BF16))
            Vst = sbatch.enter_context(sbt(nc, "Vst", [128, 16 * 2080], BF16))
            C["KAll"], C["Vst"] = KAll, Vst
            emit_P1(p, C, b, None)
            if b == 0:
                emit_P0b(p, C, 1, qload="gpsimd")
                emit_precast(p, C, PRECAST_PAIRS[0:2])
            if "p1" in debug and b == 0:
                dbg_dump(p, C, "dbg_KAll", KAll[:, :], 128, H * S)
                dbg_dump(p, C, "dbg_Vst", Vst[:, :], 128, 16 * 2080)
                break
            emit_P2(p, C, b)
            if b == 0:
                emit_precast(p, C, PRECAST_PAIRS[2:4], first=False)
        if "p2" in debug and b == 0:
            break
        emit_P3(p, C, b)
        if b == 0:
            emit_precast(p, C, PRECAST_PAIRS[4:], first=False)
        if "p3" in debug and b == 0:
            break
        emit_P4(p, C, b)
        if "p4" in debug and b == 0:
            break
    if not any(d in debug for d in ("p1", "p2", "p3", "p4")):
        emit_P5(p, C, moe_sched)

    p.fence()

    with nc.Block() as block:
        @block.tensor
        def _(t):
            for f in p.seq["tensor"]:
                f(t)
        @block.vector
        def _(v):
            for f in p.seq["vector"]:
                f(v)
        @block.scalar
        def _(s):
            for f in p.seq["scalar"]:
                f(s)
        @block.gpsimd
        def _(g):
            for f in p.seq["gpsimd"]:
                f(g)
        @block.sync
        def _(s):
            for f in p.seq["sync"]:
                f(s)
    return nc


def host_prep(inputs):
    T = lambda a: np.ascontiguousarray(np.asarray(a).T)
    m = dict(
        wqT=T(inputs["L0_attn_wq"]),
        wkvT=T(inputs["L0_attn_w_kv_down"]),
        wkrT=T(inputs["L0_attn_w_k_rope"]),
        wkuT=T(inputs["L0_attn_w_k_up"]),
        wvuT=T(inputs["L0_attn_w_v_up"]),
        woT=T(inputs["L0_attn_wo"]),
        wgate=np.ascontiguousarray(inputs["L0_ffn_w_gate"]),
        wup=np.ascontiguousarray(inputs["L0_ffn_w_up"]),
        wdown=np.ascontiguousarray(inputs["L0_ffn_w_down"]),
        Wqk=np.ascontiguousarray(inputs["L1_attn_Wqk"]),
        Wv=np.ascontiguousarray(inputs["L1_attn_Wv"]),
        Wu=np.ascontiguousarray(inputs["L1_attn_Wu"]),
        Wo1=np.ascontiguousarray(inputs["L1_attn_Wo"]),
        bqk_t=np.ascontiguousarray(inputs["L1_attn_bqk"].reshape(16, 128).T),
        bv=np.ascontiguousarray(inputs["L1_attn_bv"]),
        bu=np.ascontiguousarray(inputs["L1_attn_bu"]),
        bo1=np.ascontiguousarray(inputs["L1_attn_bo"]),
        ln_g=np.ascontiguousarray(inputs["L1_attn_ln_g"]),
        ln_b=np.ascontiguousarray(inputs["L1_attn_ln_b"]),
    )
    return m


def moe_schedule(mask_idx):
    perm = np.argsort(mask_idx, kind="stable")
    ms = np.asarray(mask_idx)[perm]
    rows = ms // D
    sched = []
    for rc in range(16):
        ks = np.nonzero((rows // 128 == rc).reshape(128, 128).any(axis=1))[0]
        sched.append(list(int(k) for k in ks))
    nslots = 4
    for rc in range(16):
        if sched[rc]:
            nslots = max(nslots, sched[rc][-1] - sched[rc][0] + 1)
    nslots = min(nslots + 1, 128)
    return perm, sched, nslots


def kernel(**inputs):
    shared = host_prep(inputs)
    mask_idx = np.asarray(inputs["L1_ffn_mask_idx"])
    perm, sched, nslots = moe_schedule(mask_idx)
    ms = mask_idx[perm].astype(np.int32)
    shared["midx_s"] = np.ascontiguousarray(ms.reshape(128, 128).T)
    shared["atoms_p"] = np.ascontiguousarray(inputs["L1_ffn_atoms"][:, perm])
    shared["imp_p"] = np.ascontiguousarray(inputs["L1_ffn_importance"][:, perm])
    shared["eaw_r"] = np.ascontiguousarray(inputs["L1_ffn_expert_atom_weights"])
    shared["gwT"] = np.ascontiguousarray(inputs["L1_ffn_gate_w"].T)
    shared["WT"] = np.ascontiguousarray(inputs["L1_ffn_W"].T)
    x = np.ascontiguousarray(inputs["L0_attn_x"])
    nc = build_program(moe_sched=dict(sched=sched, nslots=nslots))
    in_maps = []
    for c in range(8):
        im = dict(shared)
        im["x"] = np.ascontiguousarray(x[2 * c:2 * c + 2])
        in_maps.append(im)
    res = run_bass_kernel_spmd(nc, in_maps, core_ids=list(range(8)))
    return np.concatenate([r["y"] for r in res.results], axis=0)
